# version 01
# baseline (speedup 1.0000x reference)
import numpy as np
import ml_dtypes
from contextlib import ExitStack

import concourse.bass as bass
import concourse.mybir as mybir
from concourse.bass_utils import run_bass_kernel_spmd

F32 = mybir.dt.float32
BF16 = mybir.dt.bfloat16
AF = mybir.ActivationFunctionType
ALU = mybir.AluOpType
AX = mybir.AxisListType

NCORES = 8
T = 4096
S = 2048
D = 1024
NDC = 8
TT = 512
NTT = T // TT
BF = np.dtype(ml_dtypes.bfloat16)


class Rec:
    ENGS = ("te", "v", "sc", "g", "sy")

    def __init__(self):
        self.q = {e: [] for e in self.ENGS}
        self.cnt = {}
        self.sems = {}

    def plan(self, sem, k=1):
        self.cnt[sem] = self.cnt.get(sem, 0) + k
        return self.cnt[sem]

    def n(self, sem):
        return self.cnt.get(sem, 0)

    def op(self, eng, meth, *args, inc=None, inck=1, **kw):
        tgt = self.plan(inc, inck) if inc else None

        def run(e, meth=meth, args=args, kw=kw, inc=inc, inck=inck):
            inst = getattr(e, meth)(*args, **kw)
            if inc is not None:
                inst.then_inc(self.sems[inc], inck)
        self.q[eng].append(run)
        return tgt

    def wait(self, eng, sem, val=None):
        if val is None:
            val = self.n(sem)
        if val <= 0:
            self.cnt.setdefault(sem, 0)
            return
        self.cnt.setdefault(sem, 0)

        def run(e, sem=sem, val=val):
            e.wait_ge(self.sems[sem], val)
        self.q[eng].append(run)

    def build(self, nc, stack):
        for s in self.cnt:
            self.sems[s] = stack.enter_context(nc.semaphore(s))
        with nc.Block() as b:
            b.tensor(lambda te: [f(te) for f in self.q["te"]])
            b.vector(lambda v: [f(v) for f in self.q["v"]])
            b.scalar(lambda sc: [f(sc) for f in self.q["sc"]])
            b.gpsimd(lambda g: [f(g) for f in self.q["g"]])
            b.sync(lambda sy: [f(sy) for f in self.q["sy"]])


def emit_ln_pass(r, bufs, src, eps_name, pref, xn_dst=None, gamma_col=None,
                 beta_col=None, fp32_dst=None):
    xf = bufs["lnxf"]
    sq = bufs["wk0"]
    tmp = bufs["wk1"]
    tmp2 = bufs["wk2"]
    row = bufs["row"]
    ones = bufs["ones"]
    psA = bufs["psA"]
    psB = bufs["psB"]
    psC = bufs["psC"]
    sld, smm, ssc, sv, sst = (pref + x for x in ("ld", "mm", "sc", "v", "st"))

    for tt in range(NTT):
        t0 = tt * TT
        r.wait("sy", sv)
        for dc in range(NDC):
            r.op("sy", "dma_start", out=xf[:, dc * TT:(dc + 1) * TT],
                 in_=src[dc, :, t0:t0 + TT], inc=sld, inck=16)
        r.wait("te", sld)
        for dc in range(NDC):
            r.op("te", "matmul", psA[0:1, :], ones[:, 0:1],
                 xf[:, dc * TT:(dc + 1) * TT], start=(dc == 0),
                 stop=(dc == NDC - 1), inc=(smm if dc == NDC - 1 else None))
        r.wait("sc", sld)
        s2 = pref + "s2"
        sqb = (sq, bufs["wk3"])
        for dc in range(NDC):
            r.wait("sc", s2, r.n(s2) - 1)
            r.op("sc", "activation", sqb[dc % 2][:], xf[:, dc * TT:(dc + 1) * TT],
                 AF.Square, inc=ssc)
            r.wait("te", ssc)
            r.op("te", "matmul", psC[0:1, :], ones[:, 0:1], sqb[dc % 2][:],
                 start=(dc == 0), stop=(dc == NDC - 1), inc=s2)
        mean, var, a_r, b_r = (row[:, i * TT:(i + 1) * TT] for i in range(4))
        r.wait("v", smm)
        r.wait("v", pref + "s2")
        r.op("v", "tensor_scalar_mul", mean, psA[0:1, :], 1.0 / D)
        r.op("v", "tensor_scalar_mul", var, psC[0:1, :], 1.0 / D)
        r.op("v", "tensor_tensor", sq[0:1, :], mean, mean, ALU.mult)
        r.op("v", "tensor_tensor", var, var, sq[0:1, :], ALU.subtract, inc=sv)
        r.wait("sc", sv)
        ec = CONST_COLS[eps_name]
        r.op("sc", "activation", a_r, var, AF.Ln,
             bias=bufs["cvec"][0:1, ec:ec + 1])
        r.op("sc", "activation", a_r, a_r, AF.Exp, scale=-0.5, inc=ssc)
        r.wait("v", ssc)
        r.op("v", "tensor_tensor", b_r, mean, a_r, ALU.mult, inc=sv)
        r.wait("te", sv)
        r.op("te", "matmul", psA[:], ones[0:1, :], a_r, start=True, stop=True)
        r.op("te", "matmul", psB[:], ones[0:1, :], b_r, start=True, stop=True,
             inc=smm)
        r.wait("v", smm)
        for dc in range(NDC):
            xsrc = xf[:, dc * TT:(dc + 1) * TT]
            r.op("v", "tensor_tensor", tmp[:], xsrc, psA[:], ALU.mult)
            need_f32 = fp32_dst is not None
            if xn_dst is not None and gamma_col is None and not need_f32:
                r.op("v", "tensor_tensor", xn_dst[:, dc * T + t0: dc * T + t0 + TT],
                     tmp[:], psB[:], ALU.subtract, inc=sv)
            else:
                if need_f32:
                    r.wait("v", sst)
                r.op("v", "tensor_tensor", tmp2[:], tmp[:], psB[:], ALU.subtract,
                     inc=sv)
                if xn_dst is not None:
                    dst = xn_dst[:, dc * T + t0: dc * T + t0 + TT]
                    if gamma_col is not None:
                        r.op("v", "tensor_scalar", dst, tmp2[:],
                             gamma_col[:, dc:dc + 1], beta_col[:, dc:dc + 1],
                             ALU.mult, ALU.add, inc=sv)
                    else:
                        r.op("v", "tensor_copy", dst, tmp2[:], inc=sv)
                if need_f32:
                    r.wait("sc", sv)
                    r.op("sc", "dma_start", out=fp32_dst[dc, :, t0:t0 + TT],
                         in_=tmp2[:], inc=sst, inck=16)
        if fp32_dst is not None:
            r.wait("sy", sst)
    for e in ("te", "v", "sc", "sy", "g"):
        r.wait(e, sv)
        if fp32_dst is not None:
            r.wait(e, sst)


def emit_residual(r, bufs, xin, xout, pref, delta_of, nchunk_free=None):
    raise NotImplementedError


def emit_l0_ffn(r, bufs, io, tag):
    xin, xout = io["xin"], io["xout"]
    wfcT, wprojT, hS = io["wfcT"], io["wprojT"], io["hS"]
    xn = bufs["xn"]
    wA = bufs["wA"]
    hA = bufs["hA"]
    yb, mem, spk, acc = bufs["wk1"], bufs["wk2"], bufs["wk3"], bufs["wk4"]
    xr = bufs["wk5"]
    psA, psB = bufs["psA"], bufs["psB"]
    swl, smm, sev, ssc, shs = (tag + x for x in ("wl", "mm", "ev", "sc", "hs"))
    shl, sxl, sxs = (tag + x for x in ("hl", "xl", "xs"))

    emit_ln_pass(r, bufs, xin, "eps5", tag + "ln", xn_dst=xn)
    if "tap_xn" in io:
        for dc in range(NDC):
            for tt2 in range(NTT):
                r.op("g", "dma_start",
                     out=io["tap_xn"][dc, :, tt2 * TT:(tt2 + 1) * TT],
                     in_=xn[:, dc * T + tt2 * TT: dc * T + (tt2 + 1) * TT],
                     inc=tag + "tapxn", inck=16)
        r.wait("g", tag + "tapxn")
        for e in ("te", "v", "sc", "sy"):
            r.wait(e, tag + "tapxn")

    for ib in range(8):
        r.wait("g", smm)
        for kc in range(NDC):
            r.op("g", "dma_start", out=wA[:, kc * 512:(kc + 1) * 512],
                 in_=wfcT[kc * 128:(kc + 1) * 128, ib * 512:(ib + 1) * 512],
                 inc=swl, inck=16)
        r.wait("te", swl)
        for tt in range(NTT):
            t0 = tt * TT
            for mc in range(4):
                pt = psA if mc % 2 == 0 else psB
                r.wait("te", sev)
                for kc in range(NDC):
                    r.op("te", "matmul", pt[:],
                         wA[:, kc * 512 + mc * 128: kc * 512 + (mc + 1) * 128],
                         xn[:, kc * T + t0: kc * T + t0 + TT],
                         start=(kc == 0), stop=(kc == NDC - 1),
                         inc=(smm if kc == NDC - 1 else None))
                r.wait("v", smm)
                r.wait("v", ssc)
                r.op("v", "tensor_scalar_max", yb[:], pt[:], 0.0, inc=sev)
                r.wait("sc", sev)
                r.wait("sc", shs)
                hv = hA[:, mc * 512:(mc + 1) * 512]
                r.op("sc", "activation", hv, yb[:], AF.Square, inc=ssc)
                r.wait("sc", ssc)
                r.op("sc", "dma_start", out=hS[ib * 4 + mc, :, t0:t0 + TT],
                     in_=hv, inc=shs, inck=16)

    for e in ("te", "v", "sc", "g", "sy"):
        r.wait(e, smm)
        r.wait(e, shs)
    wp = bufs["xn"]
    swp = tag + "wp"
    for kc in range(32):
        r.op("g", "dma_start", out=wp[:, kc * 1024:(kc + 1) * 1024],
             in_=wprojT[kc * 128:(kc + 1) * 128, :], inc=swp, inck=16)
    for tt in range(NTT):
        t0 = tt * TT
        r.wait("sy", sev)
        r.wait("sy", smm)
        for kc in range(32):
            r.op("sy", "dma_start", out=hA[:, kc * 512:(kc + 1) * 512],
                 in_=hS[kc, :, t0:t0 + TT], inc=shl, inck=16)
        r.wait("te", swp)
        r.wait("te", shl)
        for dc in range(NDC):
            pt = psA if dc % 2 == 0 else psB
            r.wait("te", sev)
            for kc in range(32):
                r.op("te", "matmul", pt[:],
                     wp[:, kc * 1024 + dc * 128: kc * 1024 + (dc + 1) * 128],
                     hA[:, kc * 512:(kc + 1) * 512],
                     start=(kc == 0), stop=(kc == 31),
                     inc=(smm if kc == 31 else None))
            r.wait("sc", sev)
            r.op("sc", "dma_start", out=xr[:], in_=xin[dc, :, t0:t0 + TT],
                 inc=sxl, inck=16)
            r.wait("v", sxs)
            r.wait("v", smm)
            r.op("v", "tensor_copy", yb[:], pt[:], inc=sev)
            if "tap_y" in io:
                r.wait("sy", sev)
                r.op("sy", "dma_start", out=io["tap_y"][dc, :, t0:t0 + TT],
                     in_=yb[:], inc=tag + "tapy", inck=16)
                r.wait("v", tag + "tapy")
            r.op("v", "tensor_scalar", spk[:], yb[:], 1.0, None, ALU.is_gt)
            r.op("v", "tensor_copy", acc[:], spk[:])
            r.op("v", "tensor_copy", mem[:], yb[:])
            for _ in range(4):
                r.op("v", "scalar_tensor_tensor", mem[:], mem[:], 0.9, yb[:],
                     ALU.mult, ALU.add)
                r.op("v", "tensor_tensor", mem[:], mem[:], spk[:], ALU.subtract)
                r.op("v", "tensor_scalar", spk[:], mem[:], 1.0, None, ALU.is_gt)
                r.op("v", "tensor_tensor", acc[:], acc[:], spk[:], ALU.add)
            r.wait("v", sxl)
            r.op("v", "scalar_tensor_tensor", acc[:], acc[:], 0.2, xr[:],
                 ALU.mult, ALU.add, inc=sev)
            r.wait("sy", sev)
            r.op("sy", "dma_start", out=xout[dc, :, t0:t0 + TT], in_=acc[:],
                 inc=sxs, inck=16)
    for e in ("te", "v", "sc", "g", "sy"):
        r.wait(e, sxs)


def emit_l0_attn_a(r, bufs, io, tag):
    xin, xout = io["xin"], io["xout"]
    qS, kS, vfS, aoS, hS = io["qS"], io["kS"], io["vfS"], io["aoS"], io["hS"]
    inpS, xaS = io["inpS"], io["xaS"]
    xn = bufs["xn"]
    wA2 = bufs["wA2"]
    atn = bufs["atn"]
    idb = bufs["idb"]
    gbc = bufs["gbc"]
    mb1 = bufs["mb1"]
    trb = bufs["trb"]
    onb = bufs["onb"]
    pst = bufs["pst"]
    row = bufs["row"]
    wk1, wk2, xr = bufs["wk1"], bufs["wk2"], bufs["wk5"]
    psA, psB, psC, psE = bufs["psA"], bufs["psB"], bufs["psC"], bufs["psE"]
    psD, psF = bufs["psD"], bufs["psF"]
    swl, smm, ssc, sv, sst = (tag + x for x in ("wl", "mm", "sc", "v", "st"))
    sld, sxl, sxs, str_ = (tag + x for x in ("ld", "xl", "xs", "tr"))

    emit_ln_pass(r, bufs, xin, "eps5", tag + "lnA", fp32_dst=inpS)
    r.op("g", "dma_start", out=gbc[:], in_=io["aagb"], inc=swl, inck=16)
    r.op("g", "dma_start", out=mb1[:], in_=io["aab1"], inc=swl, inck=16)
    r.op("g", "dma_start", out=trb[:], in_=io["triuC"], inc=swl, inck=16)
    r.op("g", "dma_start", out=onb[:], in_=io["onesbC"], inc=swl, inck=16)
    r.op("g", "dma_start", out=idb[:], in_=io["identb"], inc=swl, inck=16)
    for e in Rec.ENGS:
        r.wait(e, swl)
    emit_ln_pass(r, bufs, inpS, "eps6", tag + "lnB", xn_dst=xn,
                 gamma_col=gbc[:, 0:8], beta_col=gbc[:, 8:16])

    for comp, (wname, dstS) in enumerate((("aawq", qS), ("aawk", kS),
                                          ("aawv", vfS))):
        wR = io[wname]
        r.wait("g", smm)
        for kc in range(NDC):
            r.op("g", "dma_start", out=wA2[:, kc * 1024:(kc + 1) * 1024],
                 in_=wR[kc * 128:(kc + 1) * 128, :], inc=swl, inck=16)
        r.wait("te", swl)
        for tt in range(NTT):
            t0 = tt * TT
            for dc in range(NDC):
                r.wait("te", sv)
                for kc in range(NDC):
                    r.op("te", "matmul", psA[:],
                         wA2[:, kc * 1024 + dc * 128: kc * 1024 + (dc + 1) * 128],
                         xn[:, kc * T + t0: kc * T + t0 + TT],
                         start=(kc == 0), stop=(kc == NDC - 1),
                         inc=(smm if kc == NDC - 1 else None))
                r.wait("v", smm)
                r.wait("v", sst)
                if comp < 2:
                    r.op("v", "tensor_scalar", atn[:, 0:TT], psA[:], 0.0,
                         0.001, ALU.max, ALU.add, inc=sv)
                else:
                    r.op("v", "tensor_copy", atn[:, 0:TT], psA[:], inc=sv)
                r.wait("sc", sv)
                r.op("sc", "dma_start", out=dstS[dc, :, t0:t0 + TT],
                     in_=atn[:, 0:TT], inc=sst, inck=16)

    for e in Rec.ENGS:
        r.wait(e, sst)
        r.wait(e, sv)
        r.wait(e, swl)
    sao = tag + "ao"
    for sq in range(2):
        for dc in range(NDC):
            r.wait("sy", smm)
            r.wait("sy", sv)
            r.op("sy", "dma_start", out=atn[:, 0:S],
                 in_=qS[dc, :, sq * S:(sq + 1) * S], inc=sld, inck=16)
            r.op("sy", "dma_start", out=atn[:, S:2 * S],
                 in_=kS[dc, :, sq * S:(sq + 1) * S], inc=sld, inck=16)
            r.op("sy", "dma_start", out=atn[:, 2 * S:3 * S],
                 in_=vfS[dc, :, sq * S:(sq + 1) * S], inc=sld, inck=16)
            r.wait("v", sld)
            r.op("v", "memset", pst[:], 0.0)
            r.op("v", "memset", atn[:, 6848:6849], 1.0)
            r.op("v", "memset", atn[:, 6913:6914], 1.0, inc=sv)
            r.wait("te", sld)
            r.wait("te", sv)
            for j in range(16):
                jc = j * 128
                r.wait("te", sv)
                r.op("te", "transpose", psD[:, 0:128],
                     atn[:, S + jc:S + jc + 128], idb[:])
                r.op("te", "transpose", psD[:, 128:256],
                     atn[:, 2 * S + jc:2 * S + jc + 128], idb[:], inc=str_)
                r.wait("v", str_)
                r.op("v", "tensor_copy", atn[:, 6656:6784], psD[:, 0:128])
                r.op("v", "tensor_copy", atn[:, 6784:6848], psD[:, 128:192])
                r.op("v", "tensor_copy", atn[:, 6849:6913], psD[:, 192:256],
                     inc=sv)
                ktm = atn[:, 6656:6784]
                for h in range(2):
                    hb = h * 64
                    r.wait("te", sv)
                    r.op("te", "matmul", psA[:, 0:128],
                         atn[hb:hb + 64, S + jc:S + jc + 128],
                         atn[hb:hb + 64, jc:jc + 128],
                         start=True, stop=True, inc=smm)
                    r.wait("v", smm)
                    r.op("v", "tensor_tensor", atn[:, 6944:7072], psA[:, 0:128],
                         trb[:], ALU.mult, inc=sv)
                    r.wait("te", sv)
                    vx0 = 6784 + h * 65
                    r.op("te", "matmul", psB[0:64, 0:128],
                         atn[:, vx0:vx0 + 64], atn[:, 6944:7072],
                         start=True, stop=False)
                    r.op("te", "matmul", psB[0:64, 0:128],
                         pst[hb:hb + 64, 0:64],
                         atn[hb:hb + 64, jc:jc + 128],
                         start=False, stop=True)
                    r.op("te", "matmul", psC[0:1, 0:128],
                         onb[:, 0:1], atn[:, 6944:7072],
                         start=True, stop=False)
                    r.op("te", "matmul", psC[0:1, 0:128],
                         pst[hb:hb + 64, 64:65],
                         atn[hb:hb + 64, jc:jc + 128],
                         start=False, stop=True, inc=smm)
                    psX = psE if h == 0 else psF
                    r.op("te", "matmul", psX[0:64, 0:65],
                         ktm[:, hb:hb + 64], atn[:, vx0:vx0 + 65],
                         start=(j == 0), stop=True, inc=smm)
                    r.wait("v", smm)
                    r.op("v", "tensor_copy", pst[hb:hb + 64, 0:65],
                         psX[0:64, 0:65], inc=sv)
                    r.op("v", "reciprocal", row[:, 0:128], psC[0:1, 0:128],
                         inc=sv)
                    r.wait("te", sv)
                    r.op("te", "matmul", psC[0:64, 0:128],
                         bufs["ones"][0:1, 0:64], row[:, 0:128],
                         start=True, stop=True, inc=smm)
                    r.wait("v", smm)
                    r.op("v", "tensor_copy", wk1[0:64, 0:128], psB[0:64, 0:128])
                    r.wait("v", sao)
                    r.op("v", "tensor_tensor", atn[0:64, 6144:6272],
                         wk1[0:64, 0:128], psC[0:64, 0:128], ALU.mult, inc=sv)
                    r.wait("sc", sv)
                    r.op("sc", "dma_start",
                         out=aoS[dc, hb:hb + 64, sq * S + jc: sq * S + jc + 128],
                         in_=atn[0:64, 6144:6272], inc=sao, inck=16)

    for e in Rec.ENGS:
        r.wait(e, sao)
        r.wait(e, sv)
    for kc in range(NDC):
        r.op("g", "dma_start", out=wA2[:, kc * 1024:(kc + 1) * 1024],
             in_=io["aawo"][kc * 128:(kc + 1) * 128, :], inc=swl, inck=16)
    for tt in range(NTT):
        t0 = tt * TT
        r.wait("sy", smm)
        for kc in range(NDC):
            r.op("sy", "dma_start", out=bufs["hA"][:, kc * 512:(kc + 1) * 512],
                 in_=aoS[kc, :, t0:t0 + TT], inc=sld, inck=16)
        r.wait("te", swl)
        r.wait("te", sld)
        for dci in range(NDC):
            pt = psA if dci % 2 == 0 else psB
            r.wait("te", sv)
            for kc in range(NDC):
                r.op("te", "matmul", pt[:],
                     wA2[:, kc * 1024 + dci * 128: kc * 1024 + (dci + 1) * 128],
                     bufs["hA"][:, kc * 512:(kc + 1) * 512],
                     start=(kc == 0), stop=(kc == NDC - 1),
                     inc=(smm if kc == NDC - 1 else None))
            r.wait("sc", sv)
            r.op("sc", "dma_start", out=xr[:], in_=inpS[dci, :, t0:t0 + TT],
                 inc=sxl, inck=16)
            r.wait("v", sst)
            r.wait("v", smm)
            r.wait("v", sxl)
            r.op("v", "tensor_tensor", wk1[:], pt[:], xr[:], ALU.add, inc=sv)
            r.wait("sc", sv)
            r.op("sc", "dma_start", out=xaS[dci, :, t0:t0 + TT], in_=wk1[:],
                 inc=sst, inck=16)
            r.wait("v", sst)
    for e in Rec.ENGS:
        r.wait(e, sst)
        r.wait(e, sv)

    emit_ln_pass(r, bufs, xaS, "eps6", tag + "lnC", xn_dst=xn,
                 gamma_col=gbc[:, 16:24], beta_col=gbc[:, 24:32])
    wfcT = io["aaw1"]
    hA = bufs["hA"]
    for ib in range(8):
        r.wait("g", smm)
        for kc in range(NDC):
            r.op("g", "dma_start", out=bufs["wA"][:, kc * 512:(kc + 1) * 512],
                 in_=wfcT[kc * 128:(kc + 1) * 128, ib * 512:(ib + 1) * 512],
                 inc=swl, inck=16)
        r.wait("te", swl)
        for tt in range(NTT):
            t0 = tt * TT
            for mc in range(4):
                fi = ib * 4 + mc
                b1c = mb1[:, fi: fi + 1]
                pt = psA if mc % 2 == 0 else psB
                r.wait("te", sv)
                r.wait("te", ssc)
                for kc in range(NDC):
                    r.op("te", "matmul", pt[:],
                         bufs["wA"][:, kc * 512 + mc * 128: kc * 512 + (mc + 1) * 128],
                         xn[:, kc * T + t0: kc * T + t0 + TT],
                         start=(kc == 0), stop=(kc == NDC - 1),
                         inc=(smm if kc == NDC - 1 else None))
                r.wait("v", smm)
                r.wait("v", ssc)
                r.op("v", "tensor_scalar_add", wk1[:], pt[:], b1c, inc=sv)
                r.wait("sc", sv)
                r.op("sc", "activation", wk2[:], pt[:], AF.Square, bias=b1c,
                     inc=ssc)
                r.wait("v", ssc)
                r.op("v", "tensor_scalar", wk2[:], wk2[:], 0.0356774081,
                     0.7978845608, ALU.mult, ALU.add)
                r.op("v", "tensor_tensor", wk2[:], wk2[:], wk1[:], ALU.mult,
                     inc=sv)
                r.wait("sc", sv)
                r.op("sc", "activation", wk2[:], wk2[:], AF.Tanh, inc=ssc)
                r.wait("v", ssc)
                r.wait("v", tag + "hs")
                r.op("v", "tensor_scalar", wk2[:], wk2[:], 0.5, 0.5,
                     ALU.mult, ALU.add)
                hv = hA[:, mc * 512:(mc + 1) * 512]
                r.op("v", "tensor_tensor", hv, wk2[:], wk1[:], ALU.mult,
                     inc=sv)
                r.wait("sc", sv)
                r.op("sc", "dma_start", out=hS[fi, :, t0:t0 + TT],
                     in_=hv, inc=tag + "hs", inck=16)

    for e in Rec.ENGS:
        r.wait(e, smm)
        r.wait(e, tag + "hs")
        r.wait(e, sv)
    wp = bufs["xn"]
    swp = tag + "wp"
    for kc in range(32):
        r.op("g", "dma_start", out=wp[:, kc * 1024:(kc + 1) * 1024],
             in_=io["aaw2"][kc * 128:(kc + 1) * 128, :], inc=swp, inck=16)
    for tt in range(NTT):
        t0 = tt * TT
        r.wait("sy", sv)
        r.wait("sy", smm)
        for kc in range(32):
            r.op("sy", "dma_start", out=hA[:, kc * 512:(kc + 1) * 512],
                 in_=hS[kc, :, t0:t0 + TT], inc=sld, inck=16)
        r.wait("te", swp)
        r.wait("te", sld)
        for dci in range(NDC):
            pt = psA if dci % 2 == 0 else psB
            r.wait("te", sv)
            for kc in range(32):
                r.op("te", "matmul", pt[:],
                     wp[:, kc * 1024 + dci * 128: kc * 1024 + (dci + 1) * 128],
                     hA[:, kc * 512:(kc + 1) * 512],
                     start=(kc == 0), stop=(kc == 31),
                     inc=(smm if kc == 31 else None))
            r.wait("sc", sxs)
            r.op("sc", "dma_start", out=xr[:], in_=xin[dci, :, t0:t0 + TT],
                 inc=sxl, inck=16)
            r.op("sc", "dma_start", out=wk2[:], in_=xaS[dci, :, t0:t0 + TT],
                 inc=sxl, inck=16)
            r.wait("v", sxs)
            r.wait("v", smm)
            r.wait("v", sxl)
            r.op("v", "tensor_scalar_add", wk1[:], pt[:],
                 gbc[:, 32 + dci:32 + dci + 1])
            r.op("v", "tensor_tensor", wk1[:], wk1[:], wk2[:], ALU.add)
            r.op("v", "tensor_tensor", wk1[:], wk1[:], xr[:], ALU.add, inc=sv)
            r.wait("sy", sv)
            r.op("sy", "dma_start", out=xout[dci, :, t0:t0 + TT], in_=wk1[:],
                 inc=sxs, inck=16)
    for e in Rec.ENGS:
        r.wait(e, sxs)


ALIBI = [2.0 ** (-8.0 * (i + 1) / 16) for i in range(16)]


def emit_l0_attn_b(r, bufs, io, tag):
    xin, xout = io["xin"], io["xout"]
    wqR, wkR, wvR, woR = io["abWq"], io["abWk"], io["abWv"], io["abWo"]
    qS, kS, vS, aoS = io["qS"], io["kS"], io["vS_tm"], io["aoS"]
    xn = bufs["xn"]
    wA2 = bufs["wA2"]
    atn = bufs["atn"]
    s_sb = bufs["lnxf"][:, 0:S]
    D0 = bufs["lnxf"][:, S:2 * S]
    M4 = bufs["fC"]
    cosT = bufs["fB"][:, 0:TT]
    sinT = bufs["fB"][:, TT:2 * TT]
    idb = bufs["idb"]
    cols = bufs["cols"]
    obc = bufs["obc"]
    wk1, wk2, xr = bufs["wk1"], bufs["wk2"], bufs["wk5"]
    psA, psB, psC = bufs["psA"], bufs["psB"], bufs["psC"]
    psD = bufs["psD"]
    swl, smm, ssc, sv, sst = (tag + x for x in ("wl", "mm", "sc", "v", "st"))
    sld, sxl, sxs, str_ = (tag + x for x in ("ld", "xl", "xs", "tr"))

    emit_ln_pass(r, bufs, xin, "eps5", tag + "ln", xn_dst=xn)
    r.op("g", "dma_start", out=idb[:], in_=io["identb"], inc=swl, inck=16)
    r.op("g", "dma_start", out=obc[:], in_=io["ab_ob"], inc=swl, inck=16)

    for comp, (wR, dstS) in enumerate(((wqR, qS), (wkR, kS), (wvR, vS))):
        r.wait("g", smm)
        for kc in range(NDC):
            r.op("g", "dma_start", out=wA2[:, kc * 1024:(kc + 1) * 1024],
                 in_=wR[kc * 128:(kc + 1) * 128, :], inc=swl, inck=16)
        r.wait("te", swl)
        for tt in range(NTT):
            t0 = tt * TT
            pos0 = t0 % S
            if comp < 2:
                for pc in range(4):
                    r.wait("sy", sv)
                    r.op("sy", "dma_start", out=cosT,
                         in_=io["ropeC"][pc, :, pos0:pos0 + TT], inc=sld, inck=16)
                    r.op("sy", "dma_start", out=sinT,
                         in_=io["ropeS"][pc, :, pos0:pos0 + TT], inc=sld, inck=16)
                    r.wait("te", sv)
                    for kc in range(NDC):
                        r.op("te", "matmul", psA[:],
                             wA2[:, kc * 1024 + pc * 128: kc * 1024 + (pc + 1) * 128],
                             xn[:, kc * T + t0: kc * T + t0 + TT],
                             start=(kc == 0), stop=(kc == NDC - 1))
                    for kc in range(NDC):
                        r.op("te", "matmul", psB[:],
                             wA2[:, kc * 1024 + (pc + 4) * 128: kc * 1024 + (pc + 5) * 128],
                             xn[:, kc * T + t0: kc * T + t0 + TT],
                             start=(kc == 0), stop=(kc == NDC - 1),
                             inc=(smm if kc == NDC - 1 else None))
                    r.wait("v", smm)
                    r.wait("v", sld)
                    r.wait("v", sst)
                    r.op("v", "tensor_tensor", wk1[:], psA[:], cosT, ALU.mult)
                    r.op("v", "tensor_tensor", wk2[:], psB[:], sinT, ALU.mult)
                    r.op("v", "tensor_tensor", atn[:, 0:TT], wk1[:], wk2[:],
                         ALU.subtract)
                    r.op("v", "tensor_tensor", wk1[:], psA[:], sinT, ALU.mult)
                    r.op("v", "tensor_tensor", wk2[:], psB[:], cosT, ALU.mult)
                    r.op("v", "tensor_tensor", atn[:, TT:2 * TT], wk1[:], wk2[:],
                         ALU.add, inc=sv)
                    r.wait("sc", sv)
                    r.op("sc", "dma_start", out=dstS[pc, :, t0:t0 + TT],
                         in_=atn[:, 0:TT], inc=sst, inck=16)
                    r.op("sc", "dma_start", out=dstS[pc + 4, :, t0:t0 + TT],
                         in_=atn[:, TT:2 * TT], inc=sst, inck=16)
            else:
                for dc in range(NDC):
                    r.wait("te", sv)
                    for kc in range(NDC):
                        r.op("te", "matmul", psA[:],
                             wA2[:, kc * 1024 + dc * 128: kc * 1024 + (dc + 1) * 128],
                             xn[:, kc * T + t0: kc * T + t0 + TT],
                             start=(kc == 0), stop=(kc == NDC - 1),
                             inc=(smm if kc == NDC - 1 else None))
                    r.wait("v", smm)
                    r.wait("v", str_)
                    r.wait("v", sst)
                    r.op("v", "tensor_copy", atn[:, 0:TT], psA[:], inc=sv)
                    r.wait("te", sv)
                    for i in range(4):
                        r.op("te", "transpose",
                             psD[:, i * 128:(i + 1) * 128],
                             atn[:, i * 128:(i + 1) * 128], idb[:], inc=str_)
                    r.wait("v", str_)
                    for i in range(4):
                        r.op("v", "tensor_copy",
                             atn[:, TT + i * 128: TT + (i + 1) * 128],
                             psD[:, i * 128:(i + 1) * 128], inc=sv)
                    r.wait("sc", sv)
                    for i in range(4):
                        r.op("sc", "dma_start",
                             out=vS[dc, t0 + i * 128: t0 + (i + 1) * 128, :],
                             in_=atn[:, TT + i * 128: TT + (i + 1) * 128],
                             inc=sst, inck=16)

    for e in ("te", "v", "sc", "g", "sy"):
        r.wait(e, sst)
        r.wait(e, sv)
        r.wait(e, swl)
    r.op("g", "dma_start", out=D0, in_=io["d0c"], inc=swl, inck=16)
    r.op("g", "dma_start", out=M4[:, 0:S], in_=io["m4c"], inc=swl, inck=16)
    for e in ("te", "v", "sc", "sy"):
        r.wait(e, swl)
    sao = tag + "ao"
    for sq in range(2):
        for dc in range(NDC):
            r.wait("sy", smm)
            r.wait("sy", sv)
            r.op("sy", "dma_start", out=atn[:, 0:S],
                 in_=qS[dc, :, sq * S:(sq + 1) * S], inc=sld, inck=16)
            r.op("sy", "dma_start", out=atn[:, S:2 * S],
                 in_=kS[dc, :, sq * S:(sq + 1) * S], inc=sld, inck=16)
            for kt in range(16):
                r.op("sy", "dma_start",
                     out=atn[:, 2 * S + kt * 128: 2 * S + (kt + 1) * 128],
                     in_=vS[dc, sq * S + kt * 128: sq * S + (kt + 1) * 128, :],
                     inc=sld, inck=16)
            r.wait("te", sld)
            for h in range(2):
                hb = h * 64
                slope8 = 8.0 * ALIBI[2 * dc + h]
                for qb in range(4):
                    for qt in range(4):
                        q0 = qb * 512 + qt * 128
                        for kb in range(qb + 1):
                            r.wait("te", sv)
                            r.op("te", "matmul", psA[:],
                                 atn[hb:hb + 64, q0:q0 + 128],
                                 atn[hb:hb + 64, S + kb * 512: S + (kb + 1) * 512],
                                 start=True, stop=True, inc=smm)
                            r.wait("v", smm)
                            r.op("v", "scalar_tensor_tensor",
                                 s_sb[:, kb * 512:(kb + 1) * 512],
                                 D0[:, kb * 512:(kb + 1) * 512], slope8, psA[:],
                                 ALU.mult, ALU.add, inc=sv)
                        r.op("v", "tensor_tensor",
                             s_sb[:, qb * 512:(qb + 1) * 512],
                             s_sb[:, qb * 512:(qb + 1) * 512],
                             M4[:, qt * 512:(qt + 1) * 512], ALU.add)
                        nk = (qb + 1) * 512
                        r.op("v", "reduce_max", cols[:, 0:1], s_sb[:, 0:nk],
                             AX.X, inc=sv)
                        r.wait("v", sv)
                        r.op("v", "tensor_scalar_mul", cols[:, 1:2],
                             cols[:, 0:1], -0.125, inc=sv)
                        Prow = bufs["hA"][:, qt * S: qt * S + nk]
                        r.wait("sc", sv)
                        r.wait("sc", str_)
                        r.op("sc", "activation", Prow, s_sb[:, 0:nk], AF.Exp,
                             bias=cols[:, 1:2], scale=0.125,
                             accum_out=cols[:, 2:3], inc=ssc)
                        r.wait("v", ssc)
                        r.op("v", "reciprocal", cols[:, 3:4], cols[:, 2:3],
                             inc=sv)
                        r.wait("v", sv)
                        r.op("v", "tensor_scalar_mul", Prow, Prow,
                             cols[:, 3:4], inc=tag + "pn")
                    nkt = (qb + 1) * 4
                    r.wait("te", tag + "pn")
                    for grp0 in range(0, nkt * 4, 8):
                        r.wait("te", sv)
                        for gi in range(grp0, min(grp0 + 8, nkt * 4)):
                            kt, qt = divmod(gi, 4)
                            r.op("te", "transpose",
                                 psD[:, (gi % 8) * 128:(gi % 8 + 1) * 128],
                                 bufs["hA"][:, qt * S + kt * 128: qt * S + (kt + 1) * 128],
                                 idb[:], inc=str_)
                        r.wait("v", str_)
                        for gi in range(grp0, min(grp0 + 8, nkt * 4)):
                            kt, qt = divmod(gi, 4)
                            r.op("v", "tensor_copy",
                                 bufs["hA"][:, 8192 + kt * 512 + qt * 128:
                                            8192 + kt * 512 + (qt + 1) * 128],
                                 psD[:, (gi % 8) * 128:(gi % 8 + 1) * 128],
                                 inc=sv)
                    r.wait("te", sv)
                    for kt in range(nkt):
                        r.op("te", "matmul", psB[0:64, :],
                             atn[:, 2 * S + kt * 128 + hb: 2 * S + kt * 128 + hb + 64],
                             bufs["hA"][:, 8192 + kt * 512: 8192 + (kt + 1) * 512],
                             start=(kt == 0), stop=(kt == nkt - 1),
                             inc=(smm if kt == nkt - 1 else None))
                    r.wait("v", smm)
                    r.wait("v", sao)
                    r.op("v", "tensor_copy", atn[0:64, 6144:6656], psB[0:64, :],
                         inc=sv)
                    r.wait("sc", sv)
                    r.op("sc", "dma_start",
                         out=aoS[dc, hb:hb + 64, sq * S + qb * 512: sq * S + (qb + 1) * 512],
                         in_=atn[0:64, 6144:6656], inc=sao, inck=16)

    for e in ("te", "v", "sc", "g", "sy"):
        r.wait(e, sao)
        r.wait(e, sv)
    for kc in range(NDC):
        r.op("g", "dma_start", out=wA2[:, kc * 1024:(kc + 1) * 1024],
             in_=woR[kc * 128:(kc + 1) * 128, :], inc=swl, inck=16)
    for tt in range(NTT):
        t0 = tt * TT
        r.wait("sy", smm)
        for kc in range(NDC):
            r.op("sy", "dma_start", out=bufs["hA"][:, kc * 512:(kc + 1) * 512],
                 in_=aoS[kc, :, t0:t0 + TT], inc=sld, inck=16)
        r.wait("te", swl)
        r.wait("te", sld)
        for dci in range(NDC):
            pt = psA if dci % 2 == 0 else psB
            r.wait("te", sv)
            for kc in range(NDC):
                r.op("te", "matmul", pt[:],
                     wA2[:, kc * 1024 + dci * 128: kc * 1024 + (dci + 1) * 128],
                     bufs["hA"][:, kc * 512:(kc + 1) * 512],
                     start=(kc == 0), stop=(kc == NDC - 1),
                     inc=(smm if kc == NDC - 1 else None))
            r.wait("sc", sv)
            r.op("sc", "dma_start", out=xr[:], in_=xin[dci, :, t0:t0 + TT],
                 inc=sxl, inck=16)
            r.wait("v", sxs)
            r.wait("v", smm)
            r.wait("v", sxl)
            r.op("v", "tensor_tensor", wk1[:], pt[:], xr[:], ALU.add)
            r.op("v", "tensor_scalar_add", wk1[:], wk1[:], obc[:, dci:dci + 1],
                 inc=sv)
            r.wait("sy", sv)
            r.op("sy", "dma_start", out=xout[dci, :, t0:t0 + TT], in_=wk1[:],
                 inc=sxs, inck=16)
    for e in ("te", "v", "sc", "g", "sy"):
        r.wait(e, sxs)


def emit_l1_attn_a(r, bufs, io, tag):
    xin, xout = io["xin"], io["xout"]
    wqkvT = io["laWqkv"]
    qS, kS, vS, vfS, aoS = io["qS"], io["kS"], io["vS_tm"], io["vfS"], io["aoS"]
    xn = bufs["xn"]
    wA2 = bufs["wA2"]
    atn = bufs["atn"]
    s_sb = bufs["lnxf"][:, 0:S]
    cosT = bufs["lnxf"][:, S:S + TT]
    sinT = bufs["lnxf"][:, S + TT:S + 2 * TT]
    idb = bufs["idb"]
    cols = bufs["cols"]
    wk1, wk2, wk3, xr = bufs["wk1"], bufs["wk2"], bufs["wk3"], bufs["wk5"]
    psA, psB = bufs["psA"], bufs["psB"]
    psD = bufs["psD"]
    lep = bufs["lep"]
    vle = bufs["fC"]
    swl, smm, ssc, sv, sst = (tag + x for x in ("wl", "mm", "sc", "v", "st"))
    sld, sxl, sxs, str_ = (tag + x for x in ("ld", "xl", "xs", "tr"))

    emit_ln_pass(r, bufs, xin, "eps5", tag + "ln", xn_dst=xn)
    r.op("g", "dma_start", out=idb[:], in_=io["identb"], inc=swl, inck=16)
    r.op("g", "dma_start", out=lep[:], in_=io["la_lepe"], inc=swl, inck=16)
    r.op("g", "dma_start", out=bufs["bq3"][:], in_=io["la_bias"], inc=swl, inck=16)
    bq3 = bufs["bq3"]

    for comp in range(3):
        r.wait("g", smm)
        for kc in range(NDC):
            r.op("g", "dma_start", out=wA2[:, kc * 1024:(kc + 1) * 1024],
                 in_=wqkvT[kc * 128:(kc + 1) * 128,
                           comp * 1024:(comp + 1) * 1024], inc=swl, inck=16)
        r.wait("te", swl)
        for tt in range(NTT):
            t0 = tt * TT
            pos0 = t0 % S
            if comp < 2:
                r.wait("sy", sv)
                r.op("sy", "dma_start", out=cosT,
                     in_=io["ropeC1"][:, pos0:pos0 + TT], inc=sld, inck=16)
                r.op("sy", "dma_start", out=sinT,
                     in_=io["ropeS1"][:, pos0:pos0 + TT], inc=sld, inck=16)
            dstS = (qS, kS, None)[comp]
            if comp < 2:
                for pc in range(4):
                    bE = bq3[:, comp * 8 + pc: comp * 8 + pc + 1]
                    bO = bq3[:, comp * 8 + pc + 4: comp * 8 + pc + 5]
                    r.wait("te", sv)
                    r.wait("te", ssc)
                    for kc in range(NDC):
                        r.op("te", "matmul", psA[:],
                             wA2[:, kc * 1024 + pc * 128: kc * 1024 + (pc + 1) * 128],
                             xn[:, kc * T + t0: kc * T + t0 + TT],
                             start=(kc == 0), stop=(kc == NDC - 1))
                    for kc in range(NDC):
                        r.op("te", "matmul", psB[:],
                             wA2[:, kc * 1024 + (pc + 4) * 128: kc * 1024 + (pc + 5) * 128],
                             xn[:, kc * T + t0: kc * T + t0 + TT],
                             start=(kc == 0), stop=(kc == NDC - 1),
                             inc=(smm if kc == NDC - 1 else None))
                    r.wait("v", smm)
                    r.wait("v", ssc)
                    r.op("v", "tensor_scalar", wk1[:], psA[:], bE, 0.0,
                         ALU.add, ALU.min)
                    r.op("v", "tensor_scalar", wk3[:], psA[:], bE, 0.0,
                         ALU.add, ALU.max)
                    r.op("v", "tensor_scalar", wk2[:], psB[:], bO, 0.0,
                         ALU.add, ALU.min)
                    r.op("v", "tensor_scalar", bufs["wk4"][:], psB[:], bO, 0.0,
                         ALU.add, ALU.max, inc=sv)
                    r.wait("sc", sv)
                    r.op("sc", "activation", wk1[:], wk1[:], AF.Exp)
                    r.op("sc", "activation", wk2[:], wk2[:], AF.Exp, inc=ssc)
                    r.wait("v", ssc)
                    r.wait("v", sld)
                    r.wait("v", sst)
                    r.op("v", "tensor_tensor", wk3[:], wk1[:], wk3[:], ALU.add)
                    r.op("v", "tensor_tensor", bufs["wk4"][:], wk2[:],
                         bufs["wk4"][:], ALU.add)
                    r.op("v", "tensor_tensor", wk1[:], wk3[:], cosT, ALU.mult)
                    r.op("v", "tensor_tensor", wk2[:], bufs["wk4"][:], sinT,
                         ALU.mult)
                    r.op("v", "tensor_tensor", atn[:, 0:TT], wk1[:], wk2[:],
                         ALU.subtract)
                    r.op("v", "tensor_tensor", wk1[:], wk3[:], sinT, ALU.mult)
                    r.op("v", "tensor_tensor", wk2[:], bufs["wk4"][:], cosT,
                         ALU.mult)
                    r.op("v", "tensor_tensor", atn[:, TT:2 * TT], wk1[:],
                         wk2[:], ALU.add, inc=sv)
                    r.wait("sc", sv)
                    for g4 in range(4):
                        h = pc * 4 + g4
                        r.op("sc", "dma_start",
                             out=dstS[h // 2, (h % 2) * 64:(h % 2) * 64 + 32,
                                      t0:t0 + TT],
                             in_=atn[g4 * 32:(g4 + 1) * 32, 0:TT],
                             inc=sst, inck=16)
                        r.op("sc", "dma_start",
                             out=dstS[h // 2, (h % 2) * 64 + 32:(h % 2) * 64 + 64,
                                      t0:t0 + TT],
                             in_=atn[g4 * 32:(g4 + 1) * 32, TT:2 * TT],
                             inc=sst, inck=16)
                continue
            for dc in range(NDC):
                bcol = bq3[:, comp * 8 + dc: comp * 8 + dc + 1]
                r.wait("te", sv)
                r.wait("te", ssc)
                for kc in range(NDC):
                    r.op("te", "matmul", psA[:],
                         wA2[:, kc * 1024 + dc * 128: kc * 1024 + (dc + 1) * 128],
                         xn[:, kc * T + t0: kc * T + t0 + TT],
                         start=(kc == 0), stop=(kc == NDC - 1),
                         inc=(smm if kc == NDC - 1 else None))
                if True:
                    r.wait("v", smm)
                    r.wait("v", str_)
                    r.wait("v", sst)
                    r.op("v", "tensor_scalar_add", atn[:, 0:TT], psA[:], bcol,
                         inc=sv)
                    r.wait("sc", sv)
                    r.op("sc", "dma_start", out=vfS[dc, :, t0:t0 + TT],
                         in_=atn[:, 0:TT], inc=sst, inck=16)
                    r.wait("te", sv)
                    for i in range(4):
                        r.op("te", "transpose", psD[:, i * 128:(i + 1) * 128],
                             atn[:, i * 128:(i + 1) * 128], idb[:], inc=str_)
                    r.wait("v", str_)
                    for i in range(4):
                        r.op("v", "tensor_copy",
                             atn[:, TT + i * 128: TT + (i + 1) * 128],
                             psD[:, i * 128:(i + 1) * 128], inc=sv)
                    r.wait("sc", sv)
                    for i in range(4):
                        r.op("sc", "dma_start",
                             out=vS[dc, t0 + i * 128: t0 + (i + 1) * 128, :],
                             in_=atn[:, TT + i * 128: TT + (i + 1) * 128],
                             inc=sst, inck=16)

    for e in ("te", "v", "sc", "g", "sy"):
        r.wait(e, sst)
        r.wait(e, sv)
        r.wait(e, swl)
    sao = tag + "ao"
    for sq in range(2):
        for dc in range(NDC):
            r.wait("sy", smm)
            r.wait("sy", sv)
            r.op("sy", "dma_start", out=atn[:, 0:S],
                 in_=qS[dc, :, sq * S:(sq + 1) * S], inc=sld, inck=16)
            r.op("sy", "dma_start", out=atn[:, S:2 * S],
                 in_=kS[dc, :, sq * S:(sq + 1) * S], inc=sld, inck=16)
            for kt in range(16):
                r.op("sy", "dma_start",
                     out=atn[:, 2 * S + kt * 128: 2 * S + (kt + 1) * 128],
                     in_=vS[dc, sq * S + kt * 128: sq * S + (kt + 1) * 128, :],
                     inc=sld, inck=16)
            r.wait("te", sld)
            for h in range(2):
                hb = h * 64
                for qb in range(4):
                    for qt in range(4):
                        q0 = qb * 512 + qt * 128
                        for kb in range(4):
                            r.wait("te", sv)
                            r.op("te", "matmul", psA[:],
                                 atn[hb:hb + 64, q0:q0 + 128],
                                 atn[hb:hb + 64, S + kb * 512: S + (kb + 1) * 512],
                                 start=True, stop=True, inc=smm)
                            r.wait("v", smm)
                            r.op("v", "tensor_copy",
                                 s_sb[:, kb * 512:(kb + 1) * 512], psA[:],
                                 inc=sv)
                        r.op("v", "reduce_max", cols[:, 0:1], s_sb[:, 0:S],
                             AX.X, inc=sv)
                        r.wait("v", sv)
                        r.op("v", "tensor_scalar_mul", cols[:, 1:2],
                             cols[:, 0:1], -0.125, inc=sv)
                        Prow = bufs["hA"][:, qt * S: (qt + 1) * S]
                        r.wait("sc", sv)
                        r.wait("sc", str_)
                        r.op("sc", "activation", Prow, s_sb[:, 0:S], AF.Exp,
                             bias=cols[:, 1:2], scale=0.125,
                             accum_out=cols[:, 2:3], inc=ssc)
                        r.wait("v", ssc)
                        r.op("v", "reciprocal", cols[:, 3:4], cols[:, 2:3],
                             inc=sv)
                        r.wait("v", sv)
                        r.op("v", "tensor_scalar_mul", Prow, Prow,
                             cols[:, 3:4], inc=tag + "pn")
                    r.wait("te", tag + "pn")
                    for grp0 in range(0, 64, 8):
                        r.wait("te", sv)
                        for gi in range(grp0, grp0 + 8):
                            kt, qt = divmod(gi, 4)
                            r.op("te", "transpose",
                                 psD[:, (gi % 8) * 128:(gi % 8 + 1) * 128],
                                 bufs["hA"][:, qt * S + kt * 128: qt * S + (kt + 1) * 128],
                                 idb[:], inc=str_)
                        r.wait("v", str_)
                        for gi in range(grp0, grp0 + 8):
                            kt, qt = divmod(gi, 4)
                            r.op("v", "tensor_copy",
                                 bufs["hA"][:, 8192 + kt * 512 + qt * 128:
                                            8192 + kt * 512 + (qt + 1) * 128],
                                 psD[:, (gi % 8) * 128:(gi % 8 + 1) * 128],
                                 inc=sv)
                    r.wait("te", sv)
                    for kt in range(16):
                        r.op("te", "matmul", psB[0:64, :],
                             atn[:, 2 * S + kt * 128 + hb: 2 * S + kt * 128 + hb + 64],
                             bufs["hA"][:, 8192 + kt * 512: 8192 + (kt + 1) * 512],
                             start=(kt == 0), stop=(kt == 15),
                             inc=(smm if kt == 15 else None))
                    r.wait("v", smm)
                    r.wait("v", sao)
                    r.op("v", "tensor_copy", atn[0:64, 6144:6656], psB[0:64, :],
                         inc=sv)
                    r.wait("sc", sv)
                    r.op("sc", "dma_start",
                         out=aoS[dc, hb:hb + 64, sq * S + qb * 512: sq * S + (qb + 1) * 512],
                         in_=atn[0:64, 6144:6656], inc=sao, inck=16)

    for e in ("te", "v", "sc", "g", "sy"):
        r.wait(e, sao)
        r.wait(e, sv)
    for tt in range(NTT):
        t0 = tt * TT
        sq0 = (t0 % S == 0)
        sq1 = ((t0 + TT) % S == 0)
        for dc in range(NDC):
            r.wait("sy", sv)
            r.wait("g", sv)
            lo = t0 if sq0 else t0 - 1
            hi = t0 + TT if sq1 else t0 + TT + 1
            off = 1 if sq0 else 0
            r.op("g", "dma_start", out=vle[:, off:off + (hi - lo)],
                 in_=vfS[dc, :, lo:hi], inc=sld, inck=16)
            r.op("sy", "dma_start", out=wk2[:], in_=xin[dc, :, t0:t0 + TT],
                 inc=sld, inck=16)
            r.op("sy", "dma_start", out=atn[:, 0:TT],
                 in_=aoS[dc, :, t0:t0 + TT], inc=sld, inck=16)
            r.wait("v", sld)
            if sq0 or sq1:
                if sq0:
                    r.op("v", "memset", vle[:, 0:1], 0.0)
                if sq1:
                    r.op("v", "memset", vle[:, TT + 1:TT + 2], 0.0, inc=sv)
                else:
                    r.op("v", "memset", vle[:, 0:1], 0.0, inc=sv)
                r.wait("v", sv)
            r.op("v", "tensor_scalar", wk1[:], vle[:, 1:TT + 1],
                 lep[:, 8 + dc:8 + dc + 1], lep[:, 24 + dc:24 + dc + 1],
                 ALU.mult, ALU.add)
            r.op("v", "scalar_tensor_tensor", wk1[:], vle[:, 0:TT],
                 lep[:, dc:dc + 1], wk1[:], ALU.mult, ALU.add)
            r.op("v", "scalar_tensor_tensor", wk1[:], vle[:, 2:TT + 2],
                 lep[:, 16 + dc:16 + dc + 1], wk1[:], ALU.mult, ALU.add)
            r.op("v", "tensor_tensor", wk1[:], wk1[:], atn[:, 0:TT], ALU.add)
            r.op("v", "tensor_tensor", wk1[:], wk1[:], wk2[:], ALU.add, inc=sv)
            r.wait("sy", sv)
            r.op("sy", "dma_start", out=xout[dc, :, t0:t0 + TT], in_=wk1[:],
                 inc=sxs, inck=16)
            r.wait("v", sxs)
    for e in ("te", "v", "sc", "g", "sy"):
        r.wait(e, sxs)


def emit_l1_attn_b(r, bufs, io, tag):
    xin, xout = io["xin"], io["xout"]
    WiT, WfT, WgT, WoT = io["hgWiT"], io["hgWfT"], io["hgWgT"], io["hgWoT"]
    hSf, sgS = io["hSf"], io["sgS"]
    nwcol = bufs["hg_nw"]
    xn = bufs["xn"]
    wB = bufs["wB"]
    wA = bufs["wA2"]
    fA, fB, fC = bufs["fA"], bufs["fB"], bufs["fC"]
    sil = bufs["wk0"]
    tmp = bufs["wk1"]
    sq = bufs["wk2"]
    sq2 = bufs["wk3"]
    row = bufs["row"]
    ones = bufs["ones"]
    xr = bufs["wk5"]
    psA, psB, psC = bufs["psA"], bufs["psB"], bufs["psC"]
    swl, smm, ssc, sv, sst = (tag + x for x in ("wl", "mm", "sc", "v", "st"))
    shl, sxl, sxs, s2 = (tag + x for x in ("hl", "xl", "xs", "s2"))

    emit_ln_pass(r, bufs, xin, "eps5", tag + "ln", xn_dst=xn)
    r.op("g", "dma_start", out=nwcol[:], in_=io["hg_nw_d"], inc=swl, inck=16)
    for kc in range(NDC):
        r.op("g", "dma_start", out=wA[:, kc * 1024:(kc + 1) * 1024],
             in_=WoT[kc * 128:(kc + 1) * 128, :], inc=swl, inck=16)

    for dc in range(NDC):
        r.wait("g", smm)
        for wi, WT in enumerate((WiT, WfT, WgT)):
            for kc in range(NDC):
                r.op("g", "dma_start",
                     out=wB[:, wi * 1024 + kc * 128: wi * 1024 + (kc + 1) * 128],
                     in_=WT[kc * 128:(kc + 1) * 128, dc * 128:(dc + 1) * 128],
                     inc=swl, inck=16)
        r.wait("te", swl)
        for sqi in range(2):
            for ttl in range(4):
                t0 = sqi * S + ttl * TT
                ts_loc = ttl * TT
                r.wait("te", sv)
                r.wait("te", ssc)
                for wi, pt in ((0, psA), (1, psB), (2, psC)):
                    for kc in range(NDC):
                        r.op("te", "matmul", pt[:],
                             wB[:, wi * 1024 + kc * 128: wi * 1024 + (kc + 1) * 128],
                             xn[:, kc * T + t0: kc * T + t0 + TT],
                             start=(kc == 0), stop=(kc == NDC - 1),
                             inc=(smm if kc == NDC - 1 else None))
                r.wait("sc", smm)
                r.wait("sc", sv)
                r.op("sc", "activation", fA[:, ts_loc:ts_loc + TT], psB[:],
                     AF.Sigmoid, inc=ssc)
                r.op("sc", "activation", sil[:], psA[:], AF.Silu, inc=ssc)
                r.wait("sc", tag + "sg")
                r.op("sc", "activation", tmp[:], psC[:], AF.Silu, inc=ssc)
                r.wait("sc", ssc)
                r.op("sc", "dma_start", out=sgS[dc, :, t0:t0 + TT], in_=tmp[:],
                     inc=tag + "sg", inck=16)
                r.wait("v", ssc)
                r.op("v", "tensor_scalar", sq[:], fA[:, ts_loc:ts_loc + TT],
                     -1.0, 1.0, ALU.mult, ALU.add)
                r.op("v", "tensor_tensor", fB[:, ts_loc:ts_loc + TT], sil[:],
                     sq[:], ALU.mult, inc=sv)
            r.wait("v", smm)
            r.op("v", "tensor_tensor_scan", fC[:], fA[:], fB[:], 0.0,
                 ALU.mult, ALU.add, inc=sv)
            r.wait("sy", sv)
            r.op("sy", "dma_start", out=hSf[dc, :, sqi * S:(sqi + 1) * S],
                 in_=fC[:], inc=sst, inck=16)
            r.wait("v", sst)

    for e in ("te", "v", "sc", "g", "sy"):
        r.wait(e, sst)
        r.wait(e, tag + "sg")
        r.wait(e, swl)
        r.wait(e, sv)
    for tt in range(NTT):
        t0 = tt * TT
        r.wait("sy", sv)
        for dc in range(NDC):
            r.op("sy", "dma_start", out=bufs["lnxf"][:, dc * TT:(dc + 1) * TT],
                 in_=hSf[dc, :, t0:t0 + TT], inc=shl, inck=16)
        r.wait("sc", shl)
        sqb = (sq, sq2)
        for dc in range(NDC):
            r.wait("sc", s2, r.n(s2) - 1)
            r.op("sc", "activation", sqb[dc % 2][:],
                 bufs["lnxf"][:, dc * TT:(dc + 1) * TT], AF.Square, inc=ssc)
            r.wait("te", ssc)
            r.op("te", "matmul", psC[0:1, :], ones[:, 0:1], sqb[dc % 2][:],
                 start=(dc == 0), stop=(dc == NDC - 1), inc=s2)
        mrow = row[:, 0:TT]
        r.wait("v", s2)
        r.op("v", "tensor_scalar_mul", mrow, psC[0:1, :], 1.0 / D, inc=sv)
        r.wait("sc", sv)
        r.op("sc", "activation", mrow, mrow, AF.Ln,
             bias=bufs["cvec"][0:1, CONST_COLS["eps_hgrn"]:CONST_COLS["eps_hgrn"] + 1])
        r.op("sc", "activation", mrow, mrow, AF.Exp, scale=-0.5, inc=ssc)
        r.wait("te", ssc)
        r.op("te", "matmul", psA[:], ones[0:1, :], mrow, start=True, stop=True,
             inc=smm)
        r.wait("v", smm)
        for dc in range(NDC):
            r.wait("sc", sv)
            r.op("sc", "dma_start", out=sil[:], in_=sgS[dc, :, t0:t0 + TT],
                 inc=sxl, inck=16)
            r.op("v", "tensor_tensor", tmp[:],
                 bufs["lnxf"][:, dc * TT:(dc + 1) * TT], psA[:], ALU.mult)
            r.op("v", "tensor_scalar_mul", tmp[:], tmp[:], nwcol[:, dc:dc + 1])
            r.wait("v", sxl)
            r.op("v", "tensor_tensor", xn[:, dc * T + t0: dc * T + t0 + TT],
                 tmp[:], sil[:], ALU.mult, inc=sv)

    for e in ("te", "v", "sc", "g", "sy"):
        r.wait(e, sv)
    for tt in range(NTT):
        t0 = tt * TT
        for dc in range(NDC):
            pt = psA if dc % 2 == 0 else psB
            r.wait("te", sv)
            for kc in range(NDC):
                r.op("te", "matmul", pt[:],
                     wA[:, kc * 1024 + dc * 128: kc * 1024 + (dc + 1) * 128],
                     xn[:, kc * T + t0: kc * T + t0 + TT],
                     start=(kc == 0), stop=(kc == NDC - 1),
                     inc=(smm if kc == NDC - 1 else None))
            r.wait("sc", sv)
            r.op("sc", "dma_start", out=xr[:], in_=xin[dc, :, t0:t0 + TT],
                 inc=sxl, inck=16)
            r.wait("v", sxs)
            r.wait("v", smm)
            r.wait("v", sxl)
            r.op("v", "tensor_tensor", tmp[:], pt[:], xr[:], ALU.add, inc=sv)
            r.wait("sy", sv)
            r.op("sy", "dma_start", out=xout[dc, :, t0:t0 + TT], in_=tmp[:],
                 inc=sxs, inck=16)
    for e in ("te", "v", "sc", "g", "sy"):
        r.wait(e, sxs)


def emit_expert_cast(r, bufs, io):
    evS = io["evS"]
    hA = bufs["hA"]
    sld, sst = "ecld", "ecst"
    for e in range(64):
        groups = (
            [(io["moeEG"][e, kc * 128:(kc + 1) * 128, :],
              evS[e, :, kc * 512:(kc + 1) * 512], kc * 512, 512)
             for kc in range(NDC)] +
            [(io["moeEU"][e, kc * 128:(kc + 1) * 128, :],
              evS[e, :, 4096 + kc * 512: 4096 + (kc + 1) * 512],
              4096 + kc * 512, 512) for kc in range(NDC)] +
            [(io["moeED"][e, kc * 128:(kc + 1) * 128, :],
              evS[e, :, 8192 + kc * 1024: 8192 + (kc + 1) * 1024],
              8192 + kc * 1024, 1024) for kc in range(4)])
        r.wait("g", sst)
        for (srcap, dstap, off, wdt) in groups:
            r.op("g", "dma_start", out=hA[:, off:off + wdt], in_=srcap,
                 inc=sld, inck=16)
        r.wait("sc", sld)
        for (srcap, dstap, off, wdt) in groups:
            r.op("sc", "dma_start", out=dstap, in_=hA[:, off:off + wdt],
                 inc=sst, inck=16)
    for e in ("g", "sc", "sy", "v", "te"):
        r.wait(e, sst)


def emit_l1_ffn(r, bufs, io, tag):
    xin, xout = io["xin"], io["xout"]
    evS, combS, routS = io["evS"], io["combS"], io["routS"]
    shS, hS = io["xaS"], io["hS"]
    xn = bufs["xn"]
    wA2, wA, wB = bufs["wA2"], bufs["wA"], bufs["wB"]
    hA = bufs["hA"]
    atn = bufs["atn"]
    idf = bufs["idf"]
    cols = bufs["cols"]
    wk0, wk1, wk2, wk3, wk4, xr = (bufs[f"wk{i}"] for i in range(6))
    row = bufs["row"]
    zb16 = bufs["zb16"]
    gmx = bufs["gmx"]
    psA, psB, psC = bufs["psA"], bufs["psB"], bufs["psC"]
    psD = bufs["psD"]
    dnb = (bufs["psE"], bufs["psF"], bufs["psG"], bufs["psH"])
    swl, smm, ssc, sv, sst = (tag + x for x in ("wl", "mm", "sc", "v", "st"))
    sld, sxl, sxs, str_ = (tag + x for x in ("ld", "xl", "xs", "tr"))
    scb, sfl = tag + "cb", tag + "fl"

    emit_ln_pass(r, bufs, xin, "eps5", tag + "ln", xn_dst=xn)
    r.op("v", "memset", zb16[:], 0.0, inc=sv)
    r.op("g", "dma_start", out=idf[:], in_=io["identf"], inc=swl, inck=16)
    r.op("g", "dma_start", out=bufs["gbb"][:], in_=io["moeGB"], inc=swl, inck=16)
    for kc in range(NDC):
        r.op("g", "dma_start", out=wB[:, kc * 64:(kc + 1) * 64],
             in_=io["moeGT"][kc * 128:(kc + 1) * 128, :], inc=swl, inck=16)
    for e in Rec.ENGS:
        r.wait(e, swl)
        r.wait(e, sv)

    r.op("v", "memset", wk1[64:128, :], 0.0)
    r.op("v", "memset", wk4[:, 64:128], 0.0, inc=sv)
    for tt in range(NTT):
        t0 = tt * TT
        r.wait("te", sv)
        r.wait("te", ssc)
        for kc in range(NDC):
            r.op("te", "matmul", psA[0:64, :], wB[:, kc * 64:(kc + 1) * 64],
                 xn[:, kc * T + t0: kc * T + t0 + TT],
                 start=(kc == 0), stop=(kc == NDC - 1),
                 inc=(smm if kc == NDC - 1 else None))
        r.wait("sc", smm)
        r.wait("sc", sv)
        r.wait("sc", str_)
        r.op("sc", "activation", wk1[0:64, :], psA[0:64, :], AF.Sigmoid,
             inc=ssc)
        r.wait("v", ssc)
        r.op("v", "tensor_scalar_add", wk1[0:64, :], wk1[0:64, :],
             bufs["gbb"][0:64, 0:1], inc=sv)
        for i4 in range(4):
            tloc = t0 + i4 * 128
            r.wait("te", sv)
            r.op("te", "transpose", psB[:, 0:128],
                 wk1[:, i4 * 128:(i4 + 1) * 128], idf[:], inc=str_)
            r.wait("v", str_)
            r.wait("v", scb)
            stok, ms, comb = wk2[:, 0:64], wk3[:, 0:64], wk4[:, 0:64]
            eq64 = wk0[:, 0:64]
            gmax, gw, gsel, eq8 = (gmx[:, i * 8:(i + 1) * 8] for i in range(4))

            def vs(meth, *a, **kw):
                r.op("v", meth, *a, inc=sv, **kw)
                r.wait("v", sv)

            vs("tensor_copy", stok, psB[:, 0:64])
            s3 = bass.AP(stok.tensor, stok.offset,
                         [stok.ap[0], [8, 8], [1, 8]])
            vs("reduce_max", gmax, s3, AX.X)
            vs("tensor_copy", gw, gmax)
            vs("memset", gsel, 0.0)
            for _ in range(4):
                vs("reduce_max", cols[:, 0:1], gw, AX.X)
                vs("tensor_scalar", eq8, gw, cols[:, 0:1], None, ALU.is_ge)
                vs("tensor_tensor", gsel, gsel, eq8, ALU.add)
                vs("scalar_tensor_tensor", gw, eq8, -1e30, gw,
                   ALU.mult, ALU.add)
            for g8 in range(8):
                vs("tensor_scalar_mul", ms[:, g8 * 8:(g8 + 1) * 8],
                   stok[:, g8 * 8:(g8 + 1) * 8], gsel[:, g8:g8 + 1])
            vs("memset", comb, 0.0)
            vs("memset", cols[:, 4:5], 0.0)
            for _ in range(6):
                vs("reduce_max", cols[:, 0:1], ms, AX.X)
                vs("tensor_scalar_max", cols[:, 1:2], cols[:, 0:1], 0.0)
                vs("tensor_tensor", cols[:, 4:5], cols[:, 4:5],
                   cols[:, 1:2], ALU.add)
                vs("tensor_scalar", eq64, ms, cols[:, 0:1], None, ALU.is_ge)
                vs("scalar_tensor_tensor", comb, eq64, cols[:, 1:2],
                   comb, ALU.mult, ALU.add)
                vs("scalar_tensor_tensor", ms, eq64, -1e30, ms,
                   ALU.mult, ALU.add)
            vs("tensor_scalar_add", cols[:, 5:6], cols[:, 4:5], 1e-8)
            vs("reciprocal", cols[:, 6:7], cols[:, 5:6])
            vs("tensor_scalar_mul", comb, comb, cols[:, 6:7])
            r.wait("te", sv)
            r.op("te", "transpose", psB[:, 128:256], wk4[:, 0:128], idf[:],
                 inc=str_)
            r.wait("v", str_)
            r.op("v", "tensor_copy", wk0[0:64, 0:128], psB[0:64, 128:256],
                 inc=sv)
            r.wait("sc", sv)
            r.op("sc", "dma_start", out=combS[0:64, tloc:tloc + 128],
                 in_=wk0[0:64, 0:128], inc=scb, inck=16)

    for e in Rec.ENGS:
        r.wait(e, sv)
        r.wait(e, scb)
    for kc in range(NDC):
        r.op("g", "dma_start", out=wA2[:, kc * 1024:(kc + 1) * 1024],
             in_=io["moeSG"][kc * 128:(kc + 1) * 128, :], inc=swl, inck=16)
        r.op("g", "dma_start", out=hA[:, 8192 + kc * 1024: 8192 + (kc + 1) * 1024],
             in_=io["moeSU"][kc * 128:(kc + 1) * 128, :], inc=swl, inck=16)
    r.wait("te", swl)
    for tt in range(NTT):
        t0 = tt * TT
        for ic in range(NDC):
            r.wait("te", sv)
            r.wait("te", ssc)
            for kc in range(NDC):
                r.op("te", "matmul", psA[:],
                     wA2[:, kc * 1024 + ic * 128: kc * 1024 + (ic + 1) * 128],
                     xn[:, kc * T + t0: kc * T + t0 + TT],
                     start=(kc == 0), stop=(kc == NDC - 1))
            for kc in range(NDC):
                r.op("te", "matmul", psB[:],
                     hA[:, 8192 + kc * 1024 + ic * 128: 8192 + kc * 1024 + (ic + 1) * 128],
                     xn[:, kc * T + t0: kc * T + t0 + TT],
                     start=(kc == 0), stop=(kc == NDC - 1),
                     inc=(smm if kc == NDC - 1 else None))
            r.wait("sc", smm)
            r.wait("sc", sv)
            r.op("sc", "activation", wk1[:], psA[:], AF.Silu, inc=ssc)
            r.wait("v", ssc)
            r.wait("v", sst)
            r.op("v", "tensor_tensor", atn[:, 0:TT], wk1[:], psB[:], ALU.mult,
                 inc=sv)
            r.wait("sc", sv)
            r.op("sc", "dma_start", out=hS[ic, :, t0:t0 + TT],
                 in_=atn[:, 0:TT], inc=sst, inck=16)
    for e in Rec.ENGS:
        r.wait(e, sst)
        r.wait(e, sv)
    for kc in range(NDC):
        r.op("g", "dma_start", out=wA2[:, kc * 1024:(kc + 1) * 1024],
             in_=io["moeSD"][kc * 128:(kc + 1) * 128, :], inc=swl, inck=16)
    for tt in range(NTT):
        t0 = tt * TT
        r.wait("sy", smm)
        r.wait("sy", sv)
        for kc in range(NDC):
            r.op("sy", "dma_start", out=hA[:, kc * 512:(kc + 1) * 512],
                 in_=hS[kc, :, t0:t0 + TT], inc=sld, inck=16)
        r.wait("te", swl)
        r.wait("te", sld)
        for dci in range(NDC):
            pt = psA if dci % 2 == 0 else psB
            r.wait("te", sv)
            for kc in range(NDC):
                r.op("te", "matmul", pt[:],
                     wA2[:, kc * 1024 + dci * 128: kc * 1024 + (dci + 1) * 128],
                     hA[:, kc * 512:(kc + 1) * 512],
                     start=(kc == 0), stop=(kc == NDC - 1),
                     inc=(smm if kc == NDC - 1 else None))
            r.wait("v", smm)
            r.wait("v", sst)
            r.op("v", "tensor_copy", wk1[:], pt[:], inc=sv)
            r.wait("sc", sv)
            r.op("sc", "dma_start", out=shS[dci, :, t0:t0 + TT], in_=wk1[:],
                 inc=sst, inck=16)
            r.wait("v", sst)

    for e in Rec.ENGS:
        r.wait(e, sst)
        r.wait(e, sv)
        r.wait(e, smm)
    for tt in range(NTT):
        t0 = tt * TT
        for g8 in range(8):
            for el in range(8):
                e = g8 * 8 + el
                r.wait("g", smm)
                r.op("g", "dma_start", out=wA2[:], in_=evS[e, :, 0:8192],
                     inc=swl, inck=16)
                r.wait("sy", smm)
                r.op("sy", "dma_start", out=row[:, 0:TT],
                     in_=combS[e, t0:t0 + TT], inc=sld, inck=16)
                r.wait("te", sv)
                r.wait("te", sld)
                r.op("te", "matmul", psC[:], bufs["ones"][0:1, :],
                     row[:, 0:TT], start=True, stop=True, inc=smm)
                r.wait("te", swl)
                for ic in range(4):
                    r.wait("te", sv)
                    for kc in range(NDC):
                        r.op("te", "matmul", psA[:],
                             wA2[:, kc * 512 + ic * 128: kc * 512 + (ic + 1) * 128],
                             xn[:, kc * T + t0: kc * T + t0 + TT],
                             start=(kc == 0), stop=(kc == NDC - 1))
                    for kc in range(NDC):
                        r.op("te", "matmul", psB[:],
                             wA2[:, 4096 + kc * 512 + ic * 128: 4096 + kc * 512 + (ic + 1) * 128],
                             xn[:, kc * T + t0: kc * T + t0 + TT],
                             start=(kc == 0), stop=(kc == NDC - 1),
                             inc=(smm if kc == NDC - 1 else None))
                    r.wait("sc", smm)
                    r.wait("sc", sv)
                    r.op("sc", "activation", wk1[:], psA[:], AF.Silu, inc=ssc)
                    r.wait("v", ssc)
                    r.op("v", "tensor_tensor", wk2[:], wk1[:], psB[:], ALU.mult)
                    r.op("v", "tensor_tensor",
                         hA[:, el * 2048 + ic * 512: el * 2048 + (ic + 1) * 512],
                         wk2[:], psC[:], ALU.mult, inc=sv)
            for th in range(2):
                r.wait("te", sv)
                for bi in range(4):
                    r.op("te", "matmul", dnb[bi][:], zb16[:], xn[:, 0:512],
                         start=True, stop=True)
                for el in range(8):
                    e = g8 * 8 + el
                    r.wait("g", smm)
                    r.op("g", "dma_start", out=wA[:], in_=evS[e, :, 8192:12288],
                         inc=swl, inck=16)
                    r.wait("te", swl)
                    for hc in range(NDC):
                        pb = dnb[hc % 4][:, (hc // 4) * 256:(hc // 4) * 256 + 256]
                        for ic in range(4):
                            r.op("te", "matmul", pb,
                                 wA[:, ic * 1024 + hc * 128: ic * 1024 + (hc + 1) * 128],
                                 hA[:, el * 2048 + ic * 512 + th * 256:
                                    el * 2048 + ic * 512 + th * 256 + 256],
                                 start=False, stop=(ic == 3),
                                 inc=(smm if (ic == 3 and hc == NDC - 1) else None))
                r.wait("v", smm)
                for hc in range(NDC):
                    r.wait("v", sfl)
                    r.op("v", "tensor_copy", wk3[:, 0:256],
                         dnb[hc % 4][:, (hc // 4) * 256:(hc // 4) * 256 + 256],
                         inc=sv)
                    r.wait("g", sv)
                    kw = {} if g8 == 0 else {"accum_op": ALU.add}
                    r.op("g", "dma_start",
                         out=routS[hc, :, t0 + th * 256: t0 + th * 256 + 256],
                         in_=wk3[:, 0:256], inc=sfl, inck=16, **kw)

    for e in Rec.ENGS:
        r.wait(e, sfl)
        r.wait(e, sv)
    for tt in range(NTT):
        t0 = tt * TT
        for dci in range(NDC):
            r.wait("sy", sv)
            r.op("sy", "dma_start", out=wk1[:], in_=xin[dci, :, t0:t0 + TT],
                 inc=sld, inck=16)
            r.op("sy", "dma_start", out=wk2[:], in_=shS[dci, :, t0:t0 + TT],
                 inc=sld, inck=16)
            r.op("sy", "dma_start", out=wk3[:], in_=routS[dci, :, t0:t0 + TT],
                 inc=sld, inck=16)
            r.wait("v", sld)
            r.wait("v", sxs)
            r.op("v", "tensor_tensor", wk4[:], wk1[:], wk2[:], ALU.add)
            r.op("v", "tensor_tensor", wk4[:], wk4[:], wk3[:], ALU.add, inc=sv)
            r.wait("sy", sv)
            r.op("sy", "dma_start", out=xout[dci, :, t0:t0 + TT], in_=wk4[:],
                 inc=sxs, inck=16)
    for e in Rec.ENGS:
        r.wait(e, sxs)


def alloc_bufs(nc, stack):
    bufs = {}

    def sb(name, shape, dt):
        bufs[name] = stack.enter_context(nc.sbuf_tensor("sb_" + name, shape, dt))[:]

    sb("xn", [128, 8 * T], BF16)
    sb("wA", [128, 8 * 512], BF16)
    sb("wA2", [128, 8 * 1024], BF16)
    sb("wB", [128, 3 * 1024], BF16)
    sb("hA", [128, 32 * 512], BF16)
    sb("fC", [128, S], F32)
    sb("hg_nw", [128, 8], F32)
    sb("atn", [128, 8192], BF16)
    sb("idb", [128, 128], BF16)
    sb("cols", [128, 8], F32)
    sb("obc", [128, 8], F32)
    sb("bq3", [128, 24], F32)
    sb("lep", [128, 32], F32)
    sb("gbc", [128, 48], F32)
    sb("mb1", [128, 32], F32)
    sb("trb", [128, 128], F32)
    sb("onb", [128, 128], BF16)
    sb("pst", [128, 128], BF16)
    sb("zb16", [128, 128], BF16)
    sb("idf", [128, 128], F32)
    sb("gmx", [128, 32], F32)
    sb("gbb", [128, 2], F32)
    sb("lnxf", [128, 8 * TT], F32)
    sb("row", [1, 4 * TT], F32)
    sb("ones", [128, 128], F32)
    sb("cvec", [128, 8], F32)
    for i in range(6):
        sb(f"wk{i}", [128, TT], F32)
    bufs["fA"] = bufs["lnxf"][:, 0:S]
    bufs["fB"] = bufs["lnxf"][:, S:2 * S]

    def ps(name, shape, dt):
        bufs[name] = stack.enter_context(nc.psum_tensor("ps_" + name, shape, dt))[:]

    ps("psA", [128, 512], F32)
    ps("psB", [128, 512], F32)
    ps("psC", [128, 512], F32)
    ps("psD", [128, 1024], BF16)
    ps("psE", [128, 512], F32)
    ps("psF", [128, 512], F32)
    ps("psG", [128, 512], F32)
    ps("psH", [128, 512], F32)
    return bufs


def emit_const_init(r, bufs, consts_dram):
    r.op("sy", "dma_start", out=bufs["ones"], in_=consts_dram["ones"],
         inc="c0ld", inck=16)
    r.op("sy", "dma_start", out=bufs["cvec"], in_=consts_dram["cvec"],
         inc="c0ld", inck=16)
    for e in Rec.ENGS:
        r.wait(e, "c0ld")


SUB_BUILDERS = {}


DEBUG_TAPS = []
DEBUG_SCRATCH = []


def build_program(subnames, shapes_inputs):
    nc = bass.Bass("TRN2", target_bir_lowering=False)
    dram = {}
    for name, (shape, dt) in shapes_inputs.items():
        mdt = F32 if dt == np.float32 else BF16
        dram[name] = nc.dram_tensor(name, list(shape), mdt,
                                    kind="ExternalInput").ap()
    xo = nc.dram_tensor("xo", [NDC, 128, T], F32, kind="ExternalOutput").ap()
    def _kind(nm):
        return "ExternalOutput" if nm in DEBUG_SCRATCH else "Internal"
    scratch = {
        "hS": nc.dram_tensor("hS", [32, 128, T], BF16, kind=_kind("hS")).ap(),
        "hSf": nc.dram_tensor("hSf", [NDC, 128, T], F32, kind=_kind("hSf")).ap(),
        "sgS": nc.dram_tensor("sgS", [NDC, 128, T], F32, kind=_kind("sgS")).ap(),
        "qS": nc.dram_tensor("qS", [NDC, 128, T], BF16, kind=_kind("qS")).ap(),
        "kS": nc.dram_tensor("kS", [NDC, 128, T], BF16, kind=_kind("kS")).ap(),
        "vS_tm": nc.dram_tensor("vS_tm", [NDC, T, 128], BF16, kind=_kind("vS_tm")).ap(),
        "aoS": nc.dram_tensor("aoS", [NDC, 128, T], BF16, kind=_kind("aoS")).ap(),
        "vfS": nc.dram_tensor("vfS", [NDC, 128, T], BF16, kind=_kind("vfS")).ap(),
        "inpS": nc.dram_tensor("inpS", [NDC, 128, T], F32, kind=_kind("inpS")).ap(),
        "xaS": nc.dram_tensor("xaS", [NDC, 128, T], F32, kind=_kind("xaS")).ap(),
        "evS": nc.dram_tensor("evS", [64, 128, 12288], BF16, kind="Internal").ap(),
        "combS": nc.dram_tensor("combS", [64, T], F32, kind=_kind("combS")).ap(),
        "routS": nc.dram_tensor("routS", [NDC, 128, T], F32, kind=_kind("routS")).ap(),
    }
    for tapname in DEBUG_TAPS:
        scratch["tap_" + tapname] = nc.dram_tensor(
            "tap_" + tapname, [NDC, 128, T], F32, kind="ExternalOutput").ap()
    stack = ExitStack()
    with stack:
        bufs = alloc_bufs(nc, stack)
        r = Rec()
        emit_const_init(r, bufs, dram)
        if any(s.startswith("l1_ffn") for s in subnames):
            emit_expert_cast(r, bufs, {**dram, **scratch})
        x_cur = dram["xt"]
        for i, sub in enumerate(subnames):
            x_next = xo if i == len(subnames) - 1 else \
                nc.dram_tensor(f"xmid{i}", [NDC, 128, T], F32, kind="Internal").ap()
            io = {"xin": x_cur, "xout": x_next, **dram, **scratch}
            SUB_BUILDERS[sub](r, bufs, io, "u_")
            x_cur = x_next
        r.build(nc, stack)
    return nc


SUB_BUILDERS["l0_ffn"] = emit_l0_ffn
SUB_BUILDERS["l0_ffn2"] = emit_l0_ffn
SUB_BUILDERS["l1_attn_b"] = emit_l1_attn_b
SUB_BUILDERS["l0_attn_b"] = emit_l0_attn_b
SUB_BUILDERS["l1_attn_a"] = emit_l1_attn_a
SUB_BUILDERS["l0_attn_a"] = emit_l0_attn_a
SUB_BUILDERS["l1_ffn"] = emit_l1_ffn
SUB_BUILDERS["l1_ffn2"] = emit_l1_ffn


def to_fmajor(x2):
    n = x2.shape[0]
    return np.ascontiguousarray(
        x2.reshape(n * S, NDC, 128).transpose(1, 2, 0))


def from_fmajor(xt, n):
    return np.ascontiguousarray(xt.transpose(2, 0, 1).reshape(n, S, D))


CONST_COLS = {"eps5": 0, "eps6": 1, "eps_hgrn": 2}


def rope_full_tables():
    half = 512
    pos = np.arange(S, dtype=np.float32)[:, None]
    freqs = (1.0 / 10000.0 ** (np.arange(half, dtype=np.float32) / half))
    ang = pos * freqs[None, :]
    cos = np.cos(ang).T.astype(np.float32)
    sin = np.sin(ang).T.astype(np.float32)
    return (np.ascontiguousarray(cos.reshape(4, 128, S)),
            np.ascontiguousarray(sin.reshape(4, 128, S)))


def host_consts():
    cv = np.zeros((128, 8), np.float32)
    cv[:, 0] = 1e-5
    cv[:, 1] = 1e-6
    cv[:, 2] = 1e-5
    qp = np.arange(128, dtype=np.float32)[:, None]
    kf = np.arange(S, dtype=np.float32)[None, :]
    d0 = kf - qp
    j = np.arange(512)[None, :]
    m4 = np.concatenate(
        [np.where(j <= qt * 128 + qp, 0.0, -1e9).astype(np.float32)
         for qt in range(4)], axis=1)
    rc, rs = rope_full_tables()
    div = np.exp(np.arange(0, 64, 2, dtype=np.float32) * (-np.log(10000.0) / 64))
    ang1 = np.arange(S, dtype=np.float32)[:, None] * div[None, :]
    c1 = np.cos(ang1).T
    s1 = np.sin(ang1).T
    rc1 = np.ascontiguousarray(np.tile(c1, (4, 1)).astype(np.float32))
    rs1 = np.ascontiguousarray(np.tile(s1, (4, 1)).astype(np.float32))
    return {"ones": np.ones((128, 128), np.float32), "cvec": cv,
            "d0c": d0.astype(np.float32), "m4c": m4,
            "ropeC": rc, "ropeS": rs,
            "ropeC1": rc1, "ropeS1": rs1,
            "identb": np.eye(128).astype(BF),
            "triuC": np.triu(np.ones((128, 128), np.float32)).T.copy() * 0 +
                     (np.arange(128)[:, None] <= np.arange(128)[None, :]
                      ).astype(np.float32),
            "onesbC": np.ones((128, 128)).astype(BF),
            "identf": np.eye(128, dtype=np.float32)}


def prep_weights(inputs):
    g = {k: np.asarray(v) for k, v in inputs.items()}
    w = {}
    w["wfcT"] = np.ascontiguousarray(g["L0_ffn_W_fc"].T)
    w["wprojT"] = np.ascontiguousarray(g["L0_ffn_W_proj"].T)
    w["hgWiT"] = np.ascontiguousarray(g["L1_attn_b_Wi"].T)
    w["hgWfT"] = np.ascontiguousarray(g["L1_attn_b_Wf"].T)
    w["hgWgT"] = np.ascontiguousarray(g["L1_attn_b_Wg"].T)
    w["hgWoT"] = np.ascontiguousarray(g["L1_attn_b_Wo"].T)
    w["hg_nw_d"] = np.ascontiguousarray(g["L1_attn_b_norm_w"].reshape(8, 128).T)
    w["abWq"] = np.ascontiguousarray(g["L0_attn_b_wq"].T)
    w["abWk"] = np.ascontiguousarray(g["L0_attn_b_wk"].T)
    w["abWv"] = np.ascontiguousarray(g["L0_attn_b_wv"].T)
    w["abWo"] = np.ascontiguousarray(g["L0_attn_b_out_w"].T)
    w["ab_ob"] = np.ascontiguousarray(g["L0_attn_b_out_b"].reshape(8, 128).T)
    E = np.concatenate([h * 64 + np.arange(0, 64, 2) for h in range(16)])
    O = np.concatenate([h * 64 + np.arange(1, 64, 2) for h in range(16)])
    perm1024 = np.concatenate([E, O])
    wqkv = g["L1_attn_a_w_qkv"]
    bqkv = g["L1_attn_a_b_qkv"]
    wq1 = wqkv[0:1024][perm1024]
    wk1_ = wqkv[1024:2048][perm1024]
    wv1 = wqkv[2048:3072]
    w["laWqkv"] = np.ascontiguousarray(
        np.concatenate([wq1, wk1_, wv1], 0).T)
    bq = bqkv[0:1024][perm1024].reshape(8, 128).T
    bk = bqkv[1024:2048][perm1024].reshape(8, 128).T
    bv = bqkv[2048:3072].reshape(8, 128).T
    w["la_bias"] = np.ascontiguousarray(np.concatenate([bq, bk, bv], 1))
    wl = g["L1_attn_a_w_lepe"][:, 0, :]
    lcols = [np.ascontiguousarray(wl[:, i].reshape(8, 128).T) for i in range(3)]
    lcols.append(np.ascontiguousarray(g["L1_attn_a_b_lepe"].reshape(8, 128).T))
    w["la_lepe"] = np.ascontiguousarray(np.concatenate(lcols, 1))
    w["aawq"] = np.ascontiguousarray(g["L0_attn_a_wq"].reshape(1024, 1024))
    w["aawk"] = np.ascontiguousarray(g["L0_attn_a_wk"].reshape(1024, 1024))
    w["aawv"] = np.ascontiguousarray(g["L0_attn_a_wv"].reshape(1024, 1024))
    w["aawo"] = np.ascontiguousarray(g["L0_attn_a_wo"].reshape(1024, 1024))
    w["aaw1"] = np.ascontiguousarray(g["L0_attn_a_w1"])
    w["aaw2"] = np.ascontiguousarray(g["L0_attn_a_w2"])
    cols48 = np.zeros((128, 48), np.float32)
    cols48[:, 0:8] = g["L0_attn_a_ln1_scale"].reshape(8, 128).T
    cols48[:, 8:16] = g["L0_attn_a_ln1_bias"].reshape(8, 128).T
    cols48[:, 16:24] = g["L0_attn_a_ln2_scale"].reshape(8, 128).T
    cols48[:, 24:32] = g["L0_attn_a_ln2_bias"].reshape(8, 128).T
    cols48[:, 32:40] = g["L0_attn_a_b2"].reshape(8, 128).T
    w["aagb"] = np.ascontiguousarray(cols48)
    w["aab1"] = np.ascontiguousarray(g["L0_attn_a_b1"].reshape(32, 128).T)
    w["moeEG"] = np.ascontiguousarray(g["L1_ffn_eg_w"].transpose(0, 2, 1))
    w["moeEU"] = np.ascontiguousarray(g["L1_ffn_eu_w"].transpose(0, 2, 1))
    w["moeED"] = np.ascontiguousarray(g["L1_ffn_ed_w"].transpose(0, 2, 1))
    w["moeGT"] = np.ascontiguousarray(g["L1_ffn_gate_w"].T)
    gb = np.zeros((128, 2), np.float32)
    gb[0:64, 0] = g["L1_ffn_gate_bias"]
    w["moeGB"] = gb
    w["moeSG"] = np.ascontiguousarray(g["L1_ffn_sg_w"].T)
    w["moeSU"] = np.ascontiguousarray(g["L1_ffn_su_w"].T)
    w["moeSD"] = np.ascontiguousarray(g["L1_ffn_sd_w"].T)
    return w


def kernel_chain(x, inputs, subs):
    x = np.asarray(x)
    consts = host_consts()
    w = prep_weights(inputs)
    per_core_const = {**consts, **w}
    shapes = {"xt": ((NDC, 128, T), np.float32)}
    for k, v in per_core_const.items():
        shapes[k] = (v.shape, v.dtype)
    nc = build_program(subs, shapes)
    in_maps = []
    for c in range(NCORES):
        xt = to_fmajor(x[2 * c:2 * c + 2])
        in_maps.append({"xt": xt, **per_core_const})
    res = run_bass_kernel_spmd(nc, in_maps, core_ids=list(range(NCORES)))
    out = np.empty_like(x)
    for c in range(NCORES):
        out[2 * c:2 * c + 2] = from_fmajor(res.results[c]["xo"], 2)
    return out


ALL_SUBS = ["l0_attn_a", "l0_ffn", "l0_attn_b", "l0_ffn2",
            "l1_attn_a", "l1_ffn", "l1_attn_b", "l1_ffn2"]


def kernel(**inputs):
    x = np.asarray(inputs["L0_attn_a_inputs"])
    return kernel_chain(x, inputs, ALL_SUBS)


# version 02
# speedup vs baseline: 1.2499x; 1.2499x over previous
import numpy as np
import ml_dtypes
from contextlib import ExitStack

import concourse.bass as bass
import concourse.mybir as mybir
from concourse.bass_utils import run_bass_kernel_spmd

F32 = mybir.dt.float32
BF16 = mybir.dt.bfloat16
AF = mybir.ActivationFunctionType
ALU = mybir.AluOpType
AX = mybir.AxisListType

NCORES = 8
T = 4096
S = 2048
D = 1024
NDC = 8
TT = 512
NTT = T // TT
BF = np.dtype(ml_dtypes.bfloat16)


class Rec:
    ENGS = ("te", "v", "sc", "g", "sy")

    def __init__(self):
        self.q = {e: [] for e in self.ENGS}
        self.cnt = {}
        self.sems = {}

    def plan(self, sem, k=1):
        self.cnt[sem] = self.cnt.get(sem, 0) + k
        return self.cnt[sem]

    def n(self, sem):
        return self.cnt.get(sem, 0)

    def op(self, eng, meth, *args, inc=None, inck=1, **kw):
        tgt = self.plan(inc, inck) if inc else None

        def run(e, meth=meth, args=args, kw=kw, inc=inc, inck=inck):
            inst = getattr(e, meth)(*args, **kw)
            if inc is not None:
                inst.then_inc(self.sems[inc], inck)
        self.q[eng].append(run)
        return tgt

    def wait(self, eng, sem, val=None):
        if val is None:
            val = self.n(sem)
        if val <= 0:
            self.cnt.setdefault(sem, 0)
            return
        self.cnt.setdefault(sem, 0)

        def run(e, sem=sem, val=val):
            e.wait_ge(self.sems[sem], val)
        self.q[eng].append(run)

    def build(self, nc, stack):
        for s in self.cnt:
            self.sems[s] = stack.enter_context(nc.semaphore(s))
        with nc.Block() as b:
            b.tensor(lambda te: [f(te) for f in self.q["te"]])
            b.vector(lambda v: [f(v) for f in self.q["v"]])
            b.scalar(lambda sc: [f(sc) for f in self.q["sc"]])
            b.gpsimd(lambda g: [f(g) for f in self.q["g"]])
            b.sync(lambda sy: [f(sy) for f in self.q["sy"]])


def emit_ln_pass(r, bufs, src, eps_name, pref, xn_dst=None, gamma_col=None,
                 beta_col=None, fp32_dst=None):
    xf = bufs["lnxf"]
    sq = bufs["wk0"]
    tmp = bufs["wk1"]
    tmp2 = bufs["wk2"]
    row = bufs["row"]
    ones = bufs["ones"]
    psA = bufs["psA"]
    psB = bufs["psB"]
    psC = bufs["psC"]
    sld, smm, ssc, sv, sst = (pref + x for x in ("ld", "mm", "sc", "v", "st"))

    for tt in range(NTT):
        t0 = tt * TT
        r.wait("sy", sv)
        for dc in range(NDC):
            r.op("sy", "dma_start", out=xf[:, dc * TT:(dc + 1) * TT],
                 in_=src[dc, :, t0:t0 + TT], inc=sld, inck=16)
        r.wait("te", sld)
        for dc in range(NDC):
            r.op("te", "matmul", psA[0:1, :], ones[:, 0:1],
                 xf[:, dc * TT:(dc + 1) * TT], start=(dc == 0),
                 stop=(dc == NDC - 1), inc=(smm if dc == NDC - 1 else None))
        r.wait("sc", sld)
        s2 = pref + "s2"
        sqb = (sq, bufs["wk3"])
        for dc in range(NDC):
            r.wait("sc", s2, r.n(s2) - 1)
            r.op("sc", "activation", sqb[dc % 2][:], xf[:, dc * TT:(dc + 1) * TT],
                 AF.Square, inc=ssc)
            r.wait("te", ssc)
            r.op("te", "matmul", psC[0:1, :], ones[:, 0:1], sqb[dc % 2][:],
                 start=(dc == 0), stop=(dc == NDC - 1), inc=s2)
        mean, var, a_r, b_r = (row[:, i * TT:(i + 1) * TT] for i in range(4))
        r.wait("v", smm)
        r.wait("v", pref + "s2")
        r.op("v", "tensor_scalar_mul", mean, psA[0:1, :], 1.0 / D)
        r.op("v", "tensor_scalar_mul", var, psC[0:1, :], 1.0 / D)
        r.op("v", "tensor_tensor", sq[0:1, :], mean, mean, ALU.mult)
        r.op("v", "tensor_tensor", var, var, sq[0:1, :], ALU.subtract, inc=sv)
        r.wait("sc", sv)
        ec = CONST_COLS[eps_name]
        r.op("sc", "activation", a_r, var, AF.Ln,
             bias=bufs["cvec"][0:1, ec:ec + 1])
        r.op("sc", "activation", a_r, a_r, AF.Exp, scale=-0.5, inc=ssc)
        r.wait("v", ssc)
        r.op("v", "tensor_tensor", b_r, mean, a_r, ALU.mult, inc=sv)
        r.wait("te", sv)
        r.op("te", "matmul", psA[:], ones[0:1, :], a_r, start=True, stop=True)
        r.op("te", "matmul", psB[:], ones[0:1, :], b_r, start=True, stop=True,
             inc=smm)
        r.wait("v", smm)
        for dc in range(NDC):
            xsrc = xf[:, dc * TT:(dc + 1) * TT]
            r.op("v", "tensor_tensor", tmp[:], xsrc, psA[:], ALU.mult)
            need_f32 = fp32_dst is not None
            if xn_dst is not None and gamma_col is None and not need_f32:
                r.op("v", "tensor_tensor", xn_dst[:, dc * T + t0: dc * T + t0 + TT],
                     tmp[:], psB[:], ALU.subtract, inc=sv)
            else:
                if need_f32:
                    r.wait("v", sst)
                r.op("v", "tensor_tensor", tmp2[:], tmp[:], psB[:], ALU.subtract,
                     inc=sv)
                if xn_dst is not None:
                    dst = xn_dst[:, dc * T + t0: dc * T + t0 + TT]
                    if gamma_col is not None:
                        r.op("v", "tensor_scalar", dst, tmp2[:],
                             gamma_col[:, dc:dc + 1], beta_col[:, dc:dc + 1],
                             ALU.mult, ALU.add, inc=sv)
                    else:
                        r.op("v", "tensor_copy", dst, tmp2[:], inc=sv)
                if need_f32:
                    r.wait("sc", sv)
                    r.op("sc", "dma_start", out=fp32_dst[dc, :, t0:t0 + TT],
                         in_=tmp2[:], inc=sst, inck=16)
        if fp32_dst is not None:
            r.wait("sy", sst)
    for e in ("te", "v", "sc", "sy", "g"):
        r.wait(e, sv)
        if fp32_dst is not None:
            r.wait(e, sst)


def emit_residual(r, bufs, xin, xout, pref, delta_of, nchunk_free=None):
    raise NotImplementedError


def emit_l0_ffn(r, bufs, io, tag):
    xin, xout = io["xin"], io["xout"]
    wfcT, wprojT, hS = io["wfcT"], io["wprojT"], io["hS"]
    xn = bufs["xn"]
    wA = bufs["wA"]
    hA = bufs["hA"]
    yb, mem, spk, acc = bufs["wk1"], bufs["wk2"], bufs["wk3"], bufs["wk4"]
    xr = bufs["wk5"]
    psA, psB = bufs["psA"], bufs["psB"]
    swl, smm, sev, ssc, shs = (tag + x for x in ("wl", "mm", "ev", "sc", "hs"))
    shl, sxl, sxs = (tag + x for x in ("hl", "xl", "xs"))

    emit_ln_pass(r, bufs, xin, "eps5", tag + "ln", xn_dst=xn)
    if "tap_xn" in io:
        for dc in range(NDC):
            for tt2 in range(NTT):
                r.op("g", "dma_start",
                     out=io["tap_xn"][dc, :, tt2 * TT:(tt2 + 1) * TT],
                     in_=xn[:, dc * T + tt2 * TT: dc * T + (tt2 + 1) * TT],
                     inc=tag + "tapxn", inck=16)
        r.wait("g", tag + "tapxn")
        for e in ("te", "v", "sc", "sy"):
            r.wait(e, tag + "tapxn")

    for ib in range(8):
        r.wait("g", smm)
        for kc in range(NDC):
            r.op("g", "dma_start", out=wA[:, kc * 512:(kc + 1) * 512],
                 in_=wfcT[kc * 128:(kc + 1) * 128, ib * 512:(ib + 1) * 512],
                 inc=swl, inck=16)
        r.wait("te", swl)
        for tt in range(NTT):
            t0 = tt * TT
            for mc in range(4):
                pt = psA if mc % 2 == 0 else psB
                r.wait("te", sev)
                for kc in range(NDC):
                    r.op("te", "matmul", pt[:],
                         wA[:, kc * 512 + mc * 128: kc * 512 + (mc + 1) * 128],
                         xn[:, kc * T + t0: kc * T + t0 + TT],
                         start=(kc == 0), stop=(kc == NDC - 1),
                         inc=(smm if kc == NDC - 1 else None))
                r.wait("v", smm)
                r.wait("v", ssc)
                r.op("v", "tensor_scalar_max", yb[:], pt[:], 0.0, inc=sev)
                r.wait("sc", sev)
                r.wait("sc", shs)
                hv = hA[:, mc * 512:(mc + 1) * 512]
                r.op("sc", "activation", hv, yb[:], AF.Square, inc=ssc)
                r.wait("sc", ssc)
                r.op("sc", "dma_start", out=hS[ib * 4 + mc, :, t0:t0 + TT],
                     in_=hv, inc=shs, inck=16)

    for e in ("te", "v", "sc", "g", "sy"):
        r.wait(e, smm)
        r.wait(e, shs)
    wp = bufs["xn"]
    swp = tag + "wp"
    for kc in range(32):
        r.op("g", "dma_start", out=wp[:, kc * 1024:(kc + 1) * 1024],
             in_=wprojT[kc * 128:(kc + 1) * 128, :], inc=swp, inck=16)
    for tt in range(NTT):
        t0 = tt * TT
        r.wait("sy", sev)
        r.wait("sy", smm)
        for kc in range(32):
            r.op("sy", "dma_start", out=hA[:, kc * 512:(kc + 1) * 512],
                 in_=hS[kc, :, t0:t0 + TT], inc=shl, inck=16)
        r.wait("te", swp)
        r.wait("te", shl)
        for dc in range(NDC):
            pt = psA if dc % 2 == 0 else psB
            r.wait("te", sev)
            for kc in range(32):
                r.op("te", "matmul", pt[:],
                     wp[:, kc * 1024 + dc * 128: kc * 1024 + (dc + 1) * 128],
                     hA[:, kc * 512:(kc + 1) * 512],
                     start=(kc == 0), stop=(kc == 31),
                     inc=(smm if kc == 31 else None))
            r.wait("sc", sev)
            r.op("sc", "dma_start", out=xr[:], in_=xin[dc, :, t0:t0 + TT],
                 inc=sxl, inck=16)
            r.wait("v", sxs)
            r.wait("v", smm)
            r.op("v", "tensor_copy", yb[:], pt[:], inc=sev)
            if "tap_y" in io:
                r.wait("sy", sev)
                r.op("sy", "dma_start", out=io["tap_y"][dc, :, t0:t0 + TT],
                     in_=yb[:], inc=tag + "tapy", inck=16)
                r.wait("v", tag + "tapy")
            r.op("v", "tensor_scalar", spk[:], yb[:], 1.0, None, ALU.is_gt)
            r.op("v", "tensor_copy", acc[:], spk[:])
            r.op("v", "tensor_copy", mem[:], yb[:])
            for _ in range(4):
                r.op("v", "scalar_tensor_tensor", mem[:], mem[:], 0.9, yb[:],
                     ALU.mult, ALU.add)
                r.op("v", "tensor_tensor", mem[:], mem[:], spk[:], ALU.subtract)
                r.op("v", "tensor_scalar", spk[:], mem[:], 1.0, None, ALU.is_gt)
                r.op("v", "tensor_tensor", acc[:], acc[:], spk[:], ALU.add)
            r.wait("v", sxl)
            r.op("v", "scalar_tensor_tensor", acc[:], acc[:], 0.2, xr[:],
                 ALU.mult, ALU.add, inc=sev)
            r.wait("sy", sev)
            r.op("sy", "dma_start", out=xout[dc, :, t0:t0 + TT], in_=acc[:],
                 inc=sxs, inck=16)
    for e in ("te", "v", "sc", "g", "sy"):
        r.wait(e, sxs)


def emit_l0_attn_a(r, bufs, io, tag):
    xin, xout = io["xin"], io["xout"]
    qS, kS, vfS, aoS, hS = io["qS"], io["kS"], io["vfS"], io["aoS"], io["hS"]
    inpS, xaS = io["inpS"], io["xaS"]
    xn = bufs["xn"]
    wA2 = bufs["wA2"]
    atn = bufs["atn"]
    idb = bufs["idb"]
    gbc = bufs["gbc"]
    mb1 = bufs["mb1"]
    trb = bufs["trb"]
    onb = bufs["onb"]
    pst = bufs["pst"]
    row = bufs["row"]
    wk1, wk2, xr = bufs["wk1"], bufs["wk2"], bufs["wk5"]
    psA, psB, psC, psE = bufs["psA"], bufs["psB"], bufs["psC"], bufs["psE"]
    psD, psF = bufs["psDb"], bufs["psF"]
    swl, smm, ssc, sv, sst = (tag + x for x in ("wl", "mm", "sc", "v", "st"))
    sld, sxl, sxs, str_ = (tag + x for x in ("ld", "xl", "xs", "tr"))

    emit_ln_pass(r, bufs, xin, "eps5", tag + "lnA", fp32_dst=inpS)
    r.op("g", "dma_start", out=gbc[:], in_=io["aagb"], inc=swl, inck=16)
    r.op("g", "dma_start", out=mb1[:], in_=io["aab1"], inc=swl, inck=16)
    r.op("g", "dma_start", out=trb[:], in_=io["triuC"], inc=swl, inck=16)
    r.op("g", "dma_start", out=onb[:], in_=io["onesbC"], inc=swl, inck=16)
    r.op("g", "dma_start", out=idb[:], in_=io["identb"], inc=swl, inck=16)
    for e in Rec.ENGS:
        r.wait(e, swl)
    emit_ln_pass(r, bufs, inpS, "eps6", tag + "lnB", xn_dst=xn,
                 gamma_col=gbc[:, 0:8], beta_col=gbc[:, 8:16])

    for comp, (wname, dstS) in enumerate((("aawq", qS), ("aawk", kS),
                                          ("aawv", vfS))):
        wR = io[wname]
        r.wait("g", smm)
        for kc in range(NDC):
            r.op("g", "dma_start", out=wA2[:, kc * 1024:(kc + 1) * 1024],
                 in_=wR[kc * 128:(kc + 1) * 128, :], inc=swl, inck=16)
        r.wait("te", swl)
        for tt in range(NTT):
            t0 = tt * TT
            for dc in range(NDC):
                r.wait("te", sv)
                for kc in range(NDC):
                    r.op("te", "matmul", psA[:],
                         wA2[:, kc * 1024 + dc * 128: kc * 1024 + (dc + 1) * 128],
                         xn[:, kc * T + t0: kc * T + t0 + TT],
                         start=(kc == 0), stop=(kc == NDC - 1),
                         inc=(smm if kc == NDC - 1 else None))
                r.wait("v", smm)
                r.wait("v", sst)
                if comp < 2:
                    r.op("v", "tensor_scalar", atn[:, 0:TT], psA[:], 0.0,
                         0.001, ALU.max, ALU.add, inc=sv)
                else:
                    r.op("v", "tensor_copy", atn[:, 0:TT], psA[:], inc=sv)
                r.wait("sc", sv)
                r.op("sc", "dma_start", out=dstS[dc, :, t0:t0 + TT],
                     in_=atn[:, 0:TT], inc=sst, inck=16)

    for e in Rec.ENGS:
        r.wait(e, sst)
        r.wait(e, sv)
        r.wait(e, swl)
    sao = tag + "ao"
    for sq in range(2):
        for dc in range(NDC):
            r.wait("sy", smm)
            r.wait("sy", sv)
            r.op("sy", "dma_start", out=atn[:, 0:S],
                 in_=qS[dc, :, sq * S:(sq + 1) * S], inc=sld, inck=16)
            r.op("sy", "dma_start", out=atn[:, S:2 * S],
                 in_=kS[dc, :, sq * S:(sq + 1) * S], inc=sld, inck=16)
            r.op("sy", "dma_start", out=atn[:, 2 * S:3 * S],
                 in_=vfS[dc, :, sq * S:(sq + 1) * S], inc=sld, inck=16)
            r.wait("v", sld)
            r.op("v", "memset", pst[:], 0.0)
            r.op("v", "memset", atn[:, 6848:6849], 1.0)
            r.op("v", "memset", atn[:, 6913:6914], 1.0, inc=sv)
            r.wait("te", sld)
            r.wait("te", sv)
            for j in range(16):
                jc = j * 128
                r.wait("te", sv)
                r.op("te", "transpose", psD[:, 0:128],
                     atn[:, S + jc:S + jc + 128], idb[:])
                r.op("te", "transpose", psD[:, 128:256],
                     atn[:, 2 * S + jc:2 * S + jc + 128], idb[:], inc=str_)
                r.wait("v", str_)
                r.op("v", "tensor_copy", atn[:, 6656:6784], psD[:, 0:128])
                r.op("v", "tensor_copy", atn[:, 6784:6848], psD[:, 128:192])
                r.op("v", "tensor_copy", atn[:, 6849:6913], psD[:, 192:256],
                     inc=sv)
                ktm = atn[:, 6656:6784]
                for h in range(2):
                    hb = h * 64
                    r.wait("te", sv)
                    r.op("te", "matmul", psA[:, 0:128],
                         atn[hb:hb + 64, S + jc:S + jc + 128],
                         atn[hb:hb + 64, jc:jc + 128],
                         start=True, stop=True, inc=smm)
                    r.wait("v", smm)
                    r.op("v", "tensor_tensor", atn[:, 6944:7072], psA[:, 0:128],
                         trb[:], ALU.mult, inc=sv)
                    r.wait("te", sv)
                    vx0 = 6784 + h * 65
                    r.op("te", "matmul", psB[0:64, 0:128],
                         atn[:, vx0:vx0 + 64], atn[:, 6944:7072],
                         start=True, stop=False)
                    r.op("te", "matmul", psB[0:64, 0:128],
                         pst[hb:hb + 64, 0:64],
                         atn[hb:hb + 64, jc:jc + 128],
                         start=False, stop=True)
                    r.op("te", "matmul", psC[0:1, 0:128],
                         onb[:, 0:1], atn[:, 6944:7072],
                         start=True, stop=False)
                    r.op("te", "matmul", psC[0:1, 0:128],
                         pst[hb:hb + 64, 64:65],
                         atn[hb:hb + 64, jc:jc + 128],
                         start=False, stop=True, inc=smm)
                    psX = psE if h == 0 else psF
                    r.op("te", "matmul", psX[0:64, 0:65],
                         ktm[:, hb:hb + 64], atn[:, vx0:vx0 + 65],
                         start=(j == 0), stop=True, inc=smm)
                    r.wait("v", smm)
                    r.op("v", "tensor_copy", pst[hb:hb + 64, 0:65],
                         psX[0:64, 0:65], inc=sv)
                    r.op("v", "reciprocal", row[:, 0:128], psC[0:1, 0:128],
                         inc=sv)
                    r.wait("te", sv)
                    r.op("te", "matmul", psC[0:64, 0:128],
                         bufs["ones"][0:1, 0:64], row[:, 0:128],
                         start=True, stop=True, inc=smm)
                    r.wait("v", smm)
                    r.op("v", "tensor_copy", wk1[0:64, 0:128], psB[0:64, 0:128])
                    r.wait("v", sao)
                    r.op("v", "tensor_tensor", atn[0:64, 6144:6272],
                         wk1[0:64, 0:128], psC[0:64, 0:128], ALU.mult, inc=sv)
                    r.wait("sc", sv)
                    r.op("sc", "dma_start",
                         out=aoS[dc, hb:hb + 64, sq * S + jc: sq * S + jc + 128],
                         in_=atn[0:64, 6144:6272], inc=sao, inck=16)

    for e in Rec.ENGS:
        r.wait(e, sao)
        r.wait(e, sv)
    for kc in range(NDC):
        r.op("g", "dma_start", out=wA2[:, kc * 1024:(kc + 1) * 1024],
             in_=io["aawo"][kc * 128:(kc + 1) * 128, :], inc=swl, inck=16)
    for tt in range(NTT):
        t0 = tt * TT
        r.wait("sy", smm)
        for kc in range(NDC):
            r.op("sy", "dma_start", out=bufs["hA"][:, kc * 512:(kc + 1) * 512],
                 in_=aoS[kc, :, t0:t0 + TT], inc=sld, inck=16)
        r.wait("te", swl)
        r.wait("te", sld)
        for dci in range(NDC):
            pt = psA if dci % 2 == 0 else psB
            r.wait("te", sv)
            for kc in range(NDC):
                r.op("te", "matmul", pt[:],
                     wA2[:, kc * 1024 + dci * 128: kc * 1024 + (dci + 1) * 128],
                     bufs["hA"][:, kc * 512:(kc + 1) * 512],
                     start=(kc == 0), stop=(kc == NDC - 1),
                     inc=(smm if kc == NDC - 1 else None))
            r.wait("sc", sv)
            r.op("sc", "dma_start", out=xr[:], in_=inpS[dci, :, t0:t0 + TT],
                 inc=sxl, inck=16)
            r.wait("v", sst)
            r.wait("v", smm)
            r.wait("v", sxl)
            r.op("v", "tensor_tensor", wk1[:], pt[:], xr[:], ALU.add, inc=sv)
            r.wait("sc", sv)
            r.op("sc", "dma_start", out=xaS[dci, :, t0:t0 + TT], in_=wk1[:],
                 inc=sst, inck=16)
            r.wait("v", sst)
    for e in Rec.ENGS:
        r.wait(e, sst)
        r.wait(e, sv)

    emit_ln_pass(r, bufs, xaS, "eps6", tag + "lnC", xn_dst=xn,
                 gamma_col=gbc[:, 16:24], beta_col=gbc[:, 24:32])
    wfcT = io["aaw1"]
    hA = bufs["hA"]
    for ib in range(8):
        r.wait("g", smm)
        for kc in range(NDC):
            r.op("g", "dma_start", out=bufs["wA"][:, kc * 512:(kc + 1) * 512],
                 in_=wfcT[kc * 128:(kc + 1) * 128, ib * 512:(ib + 1) * 512],
                 inc=swl, inck=16)
        r.wait("te", swl)
        for tt in range(NTT):
            t0 = tt * TT
            for mc in range(4):
                fi = ib * 4 + mc
                b1c = mb1[:, fi: fi + 1]
                pt = psA if mc % 2 == 0 else psB
                r.wait("te", sv)
                r.wait("te", ssc)
                for kc in range(NDC):
                    r.op("te", "matmul", pt[:],
                         bufs["wA"][:, kc * 512 + mc * 128: kc * 512 + (mc + 1) * 128],
                         xn[:, kc * T + t0: kc * T + t0 + TT],
                         start=(kc == 0), stop=(kc == NDC - 1),
                         inc=(smm if kc == NDC - 1 else None))
                r.wait("v", smm)
                r.wait("v", ssc)
                r.op("v", "tensor_scalar_add", wk1[:], pt[:], b1c, inc=sv)
                r.wait("sc", sv)
                r.op("sc", "activation", wk2[:], pt[:], AF.Square, bias=b1c,
                     inc=ssc)
                r.wait("v", ssc)
                r.op("v", "tensor_scalar", wk2[:], wk2[:], 0.0356774081,
                     0.7978845608, ALU.mult, ALU.add)
                r.op("v", "tensor_tensor", wk2[:], wk2[:], wk1[:], ALU.mult,
                     inc=sv)
                r.wait("sc", sv)
                r.op("sc", "activation", wk2[:], wk2[:], AF.Tanh, inc=ssc)
                r.wait("v", ssc)
                r.wait("v", tag + "hs")
                r.op("v", "tensor_scalar", wk2[:], wk2[:], 0.5, 0.5,
                     ALU.mult, ALU.add)
                hv = hA[:, mc * 512:(mc + 1) * 512]
                r.op("v", "tensor_tensor", hv, wk2[:], wk1[:], ALU.mult,
                     inc=sv)
                r.wait("sc", sv)
                r.op("sc", "dma_start", out=hS[fi, :, t0:t0 + TT],
                     in_=hv, inc=tag + "hs", inck=16)

    for e in Rec.ENGS:
        r.wait(e, smm)
        r.wait(e, tag + "hs")
        r.wait(e, sv)
    wp = bufs["xn"]
    swp = tag + "wp"
    for kc in range(32):
        r.op("g", "dma_start", out=wp[:, kc * 1024:(kc + 1) * 1024],
             in_=io["aaw2"][kc * 128:(kc + 1) * 128, :], inc=swp, inck=16)
    for tt in range(NTT):
        t0 = tt * TT
        r.wait("sy", sv)
        r.wait("sy", smm)
        for kc in range(32):
            r.op("sy", "dma_start", out=hA[:, kc * 512:(kc + 1) * 512],
                 in_=hS[kc, :, t0:t0 + TT], inc=sld, inck=16)
        r.wait("te", swp)
        r.wait("te", sld)
        for dci in range(NDC):
            pt = psA if dci % 2 == 0 else psB
            r.wait("te", sv)
            for kc in range(32):
                r.op("te", "matmul", pt[:],
                     wp[:, kc * 1024 + dci * 128: kc * 1024 + (dci + 1) * 128],
                     hA[:, kc * 512:(kc + 1) * 512],
                     start=(kc == 0), stop=(kc == 31),
                     inc=(smm if kc == 31 else None))
            r.wait("sc", sxs)
            r.op("sc", "dma_start", out=xr[:], in_=xin[dci, :, t0:t0 + TT],
                 inc=sxl, inck=16)
            r.op("sc", "dma_start", out=wk2[:], in_=xaS[dci, :, t0:t0 + TT],
                 inc=sxl, inck=16)
            r.wait("v", sxs)
            r.wait("v", smm)
            r.wait("v", sxl)
            r.op("v", "tensor_scalar_add", wk1[:], pt[:],
                 gbc[:, 32 + dci:32 + dci + 1])
            r.op("v", "tensor_tensor", wk1[:], wk1[:], wk2[:], ALU.add)
            r.op("v", "tensor_tensor", wk1[:], wk1[:], xr[:], ALU.add, inc=sv)
            r.wait("sy", sv)
            r.op("sy", "dma_start", out=xout[dci, :, t0:t0 + TT], in_=wk1[:],
                 inc=sxs, inck=16)
    for e in Rec.ENGS:
        r.wait(e, sxs)


ALIBI = [2.0 ** (-8.0 * (i + 1) / 16) for i in range(16)]


def emit_l0_attn_b(r, bufs, io, tag):
    xin, xout = io["xin"], io["xout"]
    wqR, wkR, wvR, woR = io["abWq"], io["abWk"], io["abWv"], io["abWo"]
    qS, kS, vS, aoS = io["qS"], io["kS"], io["vS_tm"], io["aoS"]
    xn = bufs["xn"]
    wA2 = bufs["wA2"]
    atn = bufs["atn"]
    s_sb = bufs["lnxf"][:, 0:S]
    D0 = bufs["lnxf"][:, S:2 * S]
    M4 = bufs["fC"]
    cosT = bufs["fB"][:, 0:TT]
    sinT = bufs["fB"][:, TT:2 * TT]
    idb = bufs["idb"]
    cols = bufs["cols"]
    obc = bufs["obc"]
    wk1, wk2, xr = bufs["wk1"], bufs["wk2"], bufs["wk5"]
    psA, psB, psC = bufs["psA"], bufs["psB"], bufs["psC"]
    psD = bufs["psDb"]
    swl, smm, ssc, sv, sst = (tag + x for x in ("wl", "mm", "sc", "v", "st"))
    sld, sxl, sxs, str_ = (tag + x for x in ("ld", "xl", "xs", "tr"))

    emit_ln_pass(r, bufs, xin, "eps5", tag + "ln", xn_dst=xn)
    r.op("g", "dma_start", out=idb[:], in_=io["identb"], inc=swl, inck=16)
    r.op("g", "dma_start", out=obc[:], in_=io["ab_ob"], inc=swl, inck=16)

    for comp, (wR, dstS) in enumerate(((wqR, qS), (wkR, kS), (wvR, vS))):
        r.wait("g", smm)
        for kc in range(NDC):
            r.op("g", "dma_start", out=wA2[:, kc * 1024:(kc + 1) * 1024],
                 in_=wR[kc * 128:(kc + 1) * 128, :], inc=swl, inck=16)
        r.wait("te", swl)
        for tt in range(NTT):
            t0 = tt * TT
            pos0 = t0 % S
            if comp < 2:
                for pc in range(4):
                    r.wait("sy", sv)
                    r.op("sy", "dma_start", out=cosT,
                         in_=io["ropeC"][pc, :, pos0:pos0 + TT], inc=sld, inck=16)
                    r.op("sy", "dma_start", out=sinT,
                         in_=io["ropeS"][pc, :, pos0:pos0 + TT], inc=sld, inck=16)
                    r.wait("te", sv)
                    for kc in range(NDC):
                        r.op("te", "matmul", psA[:],
                             wA2[:, kc * 1024 + pc * 128: kc * 1024 + (pc + 1) * 128],
                             xn[:, kc * T + t0: kc * T + t0 + TT],
                             start=(kc == 0), stop=(kc == NDC - 1))
                    for kc in range(NDC):
                        r.op("te", "matmul", psB[:],
                             wA2[:, kc * 1024 + (pc + 4) * 128: kc * 1024 + (pc + 5) * 128],
                             xn[:, kc * T + t0: kc * T + t0 + TT],
                             start=(kc == 0), stop=(kc == NDC - 1),
                             inc=(smm if kc == NDC - 1 else None))
                    r.wait("v", smm)
                    r.wait("v", sld)
                    r.wait("v", sst)
                    r.op("v", "tensor_tensor", wk1[:], psA[:], cosT, ALU.mult)
                    r.op("v", "tensor_tensor", wk2[:], psB[:], sinT, ALU.mult)
                    r.op("v", "tensor_tensor", atn[:, 0:TT], wk1[:], wk2[:],
                         ALU.subtract)
                    r.op("v", "tensor_tensor", wk1[:], psA[:], sinT, ALU.mult)
                    r.op("v", "tensor_tensor", wk2[:], psB[:], cosT, ALU.mult)
                    r.op("v", "tensor_tensor", atn[:, TT:2 * TT], wk1[:], wk2[:],
                         ALU.add, inc=sv)
                    r.wait("sc", sv)
                    r.op("sc", "dma_start", out=dstS[pc, :, t0:t0 + TT],
                         in_=atn[:, 0:TT], inc=sst, inck=16)
                    r.op("sc", "dma_start", out=dstS[pc + 4, :, t0:t0 + TT],
                         in_=atn[:, TT:2 * TT], inc=sst, inck=16)
            else:
                for dc in range(NDC):
                    r.wait("te", sv)
                    for kc in range(NDC):
                        r.op("te", "matmul", psA[:],
                             wA2[:, kc * 1024 + dc * 128: kc * 1024 + (dc + 1) * 128],
                             xn[:, kc * T + t0: kc * T + t0 + TT],
                             start=(kc == 0), stop=(kc == NDC - 1),
                             inc=(smm if kc == NDC - 1 else None))
                    r.wait("v", smm)
                    r.wait("v", str_)
                    r.wait("v", sst)
                    r.op("v", "tensor_copy", atn[:, 0:TT], psA[:], inc=sv)
                    r.wait("te", sv)
                    for i in range(4):
                        r.op("te", "transpose",
                             psD[:, i * 128:(i + 1) * 128],
                             atn[:, i * 128:(i + 1) * 128], idb[:], inc=str_)
                    r.wait("v", str_)
                    for i in range(4):
                        r.op("v", "tensor_copy",
                             atn[:, TT + i * 128: TT + (i + 1) * 128],
                             psD[:, i * 128:(i + 1) * 128], inc=sv)
                    r.wait("sc", sv)
                    for i in range(4):
                        r.op("sc", "dma_start",
                             out=vS[dc, t0 + i * 128: t0 + (i + 1) * 128, :],
                             in_=atn[:, TT + i * 128: TT + (i + 1) * 128],
                             inc=sst, inck=16)

    for e in ("te", "v", "sc", "g", "sy"):
        r.wait(e, sst)
        r.wait(e, sv)
        r.wait(e, swl)
    r.op("g", "dma_start", out=D0, in_=io["d0c"], inc=swl, inck=16)
    r.op("g", "dma_start", out=M4[:, 0:S], in_=io["m4c"], inc=swl, inck=16)
    for e in ("te", "v", "sc", "sy"):
        r.wait(e, swl)
    sao = tag + "ao"
    for sq in range(2):
        for dc in range(NDC):
            r.wait("sy", smm)
            r.wait("sy", sv)
            r.op("sy", "dma_start", out=atn[:, 0:S],
                 in_=qS[dc, :, sq * S:(sq + 1) * S], inc=sld, inck=16)
            r.op("sy", "dma_start", out=atn[:, S:2 * S],
                 in_=kS[dc, :, sq * S:(sq + 1) * S], inc=sld, inck=16)
            for kt in range(16):
                r.op("sy", "dma_start",
                     out=atn[:, 2 * S + kt * 128: 2 * S + (kt + 1) * 128],
                     in_=vS[dc, sq * S + kt * 128: sq * S + (kt + 1) * 128, :],
                     inc=sld, inck=16)
            r.wait("te", sld)
            for h in range(2):
                hb = h * 64
                slope8 = 8.0 * ALIBI[2 * dc + h]
                for qb in range(4):
                    for qt in range(4):
                        q0 = qb * 512 + qt * 128
                        for kb in range(qb + 1):
                            r.wait("te", sv)
                            r.op("te", "matmul", psA[:],
                                 atn[hb:hb + 64, q0:q0 + 128],
                                 atn[hb:hb + 64, S + kb * 512: S + (kb + 1) * 512],
                                 start=True, stop=True, inc=smm)
                            r.wait("v", smm)
                            r.op("v", "scalar_tensor_tensor",
                                 s_sb[:, kb * 512:(kb + 1) * 512],
                                 D0[:, kb * 512:(kb + 1) * 512], slope8, psA[:],
                                 ALU.mult, ALU.add, inc=sv)
                        r.op("v", "tensor_tensor",
                             s_sb[:, qb * 512:(qb + 1) * 512],
                             s_sb[:, qb * 512:(qb + 1) * 512],
                             M4[:, qt * 512:(qt + 1) * 512], ALU.add)
                        nk = (qb + 1) * 512
                        r.op("v", "reduce_max", cols[:, 0:1], s_sb[:, 0:nk],
                             AX.X, inc=sv)
                        r.wait("v", sv)
                        r.op("v", "tensor_scalar_mul", cols[:, 1:2],
                             cols[:, 0:1], -0.125, inc=sv)
                        Prow = bufs["hA"][:, qt * S: qt * S + nk]
                        r.wait("sc", sv)
                        r.wait("sc", str_)
                        r.op("sc", "activation", Prow, s_sb[:, 0:nk], AF.Exp,
                             bias=cols[:, 1:2], scale=0.125,
                             accum_out=cols[:, 2:3], inc=ssc)
                        r.wait("v", ssc)
                        r.op("v", "reciprocal", cols[:, 3:4], cols[:, 2:3],
                             inc=sv)
                        r.wait("v", sv)
                        r.op("v", "tensor_scalar_mul", Prow, Prow,
                             cols[:, 3:4], inc=tag + "pn")
                    nkt = (qb + 1) * 4
                    r.wait("te", tag + "pn")
                    for grp0 in range(0, nkt * 4, 8):
                        r.wait("te", sv)
                        for gi in range(grp0, min(grp0 + 8, nkt * 4)):
                            kt, qt = divmod(gi, 4)
                            r.op("te", "transpose",
                                 psD[:, (gi % 8) * 128:(gi % 8 + 1) * 128],
                                 bufs["hA"][:, qt * S + kt * 128: qt * S + (kt + 1) * 128],
                                 idb[:], inc=str_)
                        r.wait("v", str_)
                        for gi in range(grp0, min(grp0 + 8, nkt * 4)):
                            kt, qt = divmod(gi, 4)
                            r.op("v", "tensor_copy",
                                 bufs["hA"][:, 8192 + kt * 512 + qt * 128:
                                            8192 + kt * 512 + (qt + 1) * 128],
                                 psD[:, (gi % 8) * 128:(gi % 8 + 1) * 128],
                                 inc=sv)
                    r.wait("te", sv)
                    for kt in range(nkt):
                        r.op("te", "matmul", psB[0:64, :],
                             atn[:, 2 * S + kt * 128 + hb: 2 * S + kt * 128 + hb + 64],
                             bufs["hA"][:, 8192 + kt * 512: 8192 + (kt + 1) * 512],
                             start=(kt == 0), stop=(kt == nkt - 1),
                             inc=(smm if kt == nkt - 1 else None))
                    r.wait("v", smm)
                    r.wait("v", sao)
                    r.op("v", "tensor_copy", atn[0:64, 6144:6656], psB[0:64, :],
                         inc=sv)
                    r.wait("sc", sv)
                    r.op("sc", "dma_start",
                         out=aoS[dc, hb:hb + 64, sq * S + qb * 512: sq * S + (qb + 1) * 512],
                         in_=atn[0:64, 6144:6656], inc=sao, inck=16)

    for e in ("te", "v", "sc", "g", "sy"):
        r.wait(e, sao)
        r.wait(e, sv)
    for kc in range(NDC):
        r.op("g", "dma_start", out=wA2[:, kc * 1024:(kc + 1) * 1024],
             in_=woR[kc * 128:(kc + 1) * 128, :], inc=swl, inck=16)
    for tt in range(NTT):
        t0 = tt * TT
        r.wait("sy", smm)
        for kc in range(NDC):
            r.op("sy", "dma_start", out=bufs["hA"][:, kc * 512:(kc + 1) * 512],
                 in_=aoS[kc, :, t0:t0 + TT], inc=sld, inck=16)
        r.wait("te", swl)
        r.wait("te", sld)
        for dci in range(NDC):
            pt = psA if dci % 2 == 0 else psB
            r.wait("te", sv)
            for kc in range(NDC):
                r.op("te", "matmul", pt[:],
                     wA2[:, kc * 1024 + dci * 128: kc * 1024 + (dci + 1) * 128],
                     bufs["hA"][:, kc * 512:(kc + 1) * 512],
                     start=(kc == 0), stop=(kc == NDC - 1),
                     inc=(smm if kc == NDC - 1 else None))
            r.wait("sc", sv)
            r.op("sc", "dma_start", out=xr[:], in_=xin[dci, :, t0:t0 + TT],
                 inc=sxl, inck=16)
            r.wait("v", sxs)
            r.wait("v", smm)
            r.wait("v", sxl)
            r.op("v", "tensor_tensor", wk1[:], pt[:], xr[:], ALU.add)
            r.op("v", "tensor_scalar_add", wk1[:], wk1[:], obc[:, dci:dci + 1],
                 inc=sv)
            r.wait("sy", sv)
            r.op("sy", "dma_start", out=xout[dci, :, t0:t0 + TT], in_=wk1[:],
                 inc=sxs, inck=16)
    for e in ("te", "v", "sc", "g", "sy"):
        r.wait(e, sxs)


def emit_l1_attn_a(r, bufs, io, tag):
    xin, xout = io["xin"], io["xout"]
    wqkvT = io["laWqkv"]
    qS, kS, vS, vfS, aoS = io["qS"], io["kS"], io["vS_tm"], io["vfS"], io["aoS"]
    xn = bufs["xn"]
    wA2 = bufs["wA2"]
    atn = bufs["atn"]
    s_sb = bufs["lnxf"][:, 0:S]
    cosT = bufs["lnxf"][:, S:S + TT]
    sinT = bufs["lnxf"][:, S + TT:S + 2 * TT]
    idb = bufs["idb"]
    cols = bufs["cols"]
    wk1, wk2, wk3, xr = bufs["wk1"], bufs["wk2"], bufs["wk3"], bufs["wk5"]
    psA, psB = bufs["psA"], bufs["psB"]
    psD = bufs["psDb"]
    lep = bufs["lep"]
    vle = bufs["fC"]
    swl, smm, ssc, sv, sst = (tag + x for x in ("wl", "mm", "sc", "v", "st"))
    sld, sxl, sxs, str_ = (tag + x for x in ("ld", "xl", "xs", "tr"))

    emit_ln_pass(r, bufs, xin, "eps5", tag + "ln", xn_dst=xn)
    r.op("g", "dma_start", out=idb[:], in_=io["identb"], inc=swl, inck=16)
    r.op("g", "dma_start", out=lep[:], in_=io["la_lepe"], inc=swl, inck=16)
    r.op("g", "dma_start", out=bufs["bq3"][:], in_=io["la_bias"], inc=swl, inck=16)
    bq3 = bufs["bq3"]

    for comp in range(3):
        r.wait("g", smm)
        for kc in range(NDC):
            r.op("g", "dma_start", out=wA2[:, kc * 1024:(kc + 1) * 1024],
                 in_=wqkvT[kc * 128:(kc + 1) * 128,
                           comp * 1024:(comp + 1) * 1024], inc=swl, inck=16)
        r.wait("te", swl)
        for tt in range(NTT):
            t0 = tt * TT
            pos0 = t0 % S
            if comp < 2:
                r.wait("sy", sv)
                r.op("sy", "dma_start", out=cosT,
                     in_=io["ropeC1"][:, pos0:pos0 + TT], inc=sld, inck=16)
                r.op("sy", "dma_start", out=sinT,
                     in_=io["ropeS1"][:, pos0:pos0 + TT], inc=sld, inck=16)
            dstS = (qS, kS, None)[comp]
            if comp < 2:
                for pc in range(4):
                    bE = bq3[:, comp * 8 + pc: comp * 8 + pc + 1]
                    bO = bq3[:, comp * 8 + pc + 4: comp * 8 + pc + 5]
                    r.wait("te", sv)
                    r.wait("te", ssc)
                    for kc in range(NDC):
                        r.op("te", "matmul", psA[:],
                             wA2[:, kc * 1024 + pc * 128: kc * 1024 + (pc + 1) * 128],
                             xn[:, kc * T + t0: kc * T + t0 + TT],
                             start=(kc == 0), stop=(kc == NDC - 1))
                    for kc in range(NDC):
                        r.op("te", "matmul", psB[:],
                             wA2[:, kc * 1024 + (pc + 4) * 128: kc * 1024 + (pc + 5) * 128],
                             xn[:, kc * T + t0: kc * T + t0 + TT],
                             start=(kc == 0), stop=(kc == NDC - 1),
                             inc=(smm if kc == NDC - 1 else None))
                    r.wait("v", smm)
                    r.wait("v", ssc)
                    r.op("v", "tensor_scalar", wk1[:], psA[:], bE, 0.0,
                         ALU.add, ALU.min)
                    r.op("v", "tensor_scalar", wk3[:], psA[:], bE, 0.0,
                         ALU.add, ALU.max)
                    r.op("v", "tensor_scalar", wk2[:], psB[:], bO, 0.0,
                         ALU.add, ALU.min)
                    r.op("v", "tensor_scalar", bufs["wk4"][:], psB[:], bO, 0.0,
                         ALU.add, ALU.max, inc=sv)
                    r.wait("sc", sv)
                    r.op("sc", "activation", wk1[:], wk1[:], AF.Exp)
                    r.op("sc", "activation", wk2[:], wk2[:], AF.Exp, inc=ssc)
                    r.wait("v", ssc)
                    r.wait("v", sld)
                    r.wait("v", sst)
                    r.op("v", "tensor_tensor", wk3[:], wk1[:], wk3[:], ALU.add)
                    r.op("v", "tensor_tensor", bufs["wk4"][:], wk2[:],
                         bufs["wk4"][:], ALU.add)
                    r.op("v", "tensor_tensor", wk1[:], wk3[:], cosT, ALU.mult)
                    r.op("v", "tensor_tensor", wk2[:], bufs["wk4"][:], sinT,
                         ALU.mult)
                    r.op("v", "tensor_tensor", atn[:, 0:TT], wk1[:], wk2[:],
                         ALU.subtract)
                    r.op("v", "tensor_tensor", wk1[:], wk3[:], sinT, ALU.mult)
                    r.op("v", "tensor_tensor", wk2[:], bufs["wk4"][:], cosT,
                         ALU.mult)
                    r.op("v", "tensor_tensor", atn[:, TT:2 * TT], wk1[:],
                         wk2[:], ALU.add, inc=sv)
                    r.wait("sc", sv)
                    for g4 in range(4):
                        h = pc * 4 + g4
                        r.op("sc", "dma_start",
                             out=dstS[h // 2, (h % 2) * 64:(h % 2) * 64 + 32,
                                      t0:t0 + TT],
                             in_=atn[g4 * 32:(g4 + 1) * 32, 0:TT],
                             inc=sst, inck=16)
                        r.op("sc", "dma_start",
                             out=dstS[h // 2, (h % 2) * 64 + 32:(h % 2) * 64 + 64,
                                      t0:t0 + TT],
                             in_=atn[g4 * 32:(g4 + 1) * 32, TT:2 * TT],
                             inc=sst, inck=16)
                continue
            for dc in range(NDC):
                bcol = bq3[:, comp * 8 + dc: comp * 8 + dc + 1]
                r.wait("te", sv)
                r.wait("te", ssc)
                for kc in range(NDC):
                    r.op("te", "matmul", psA[:],
                         wA2[:, kc * 1024 + dc * 128: kc * 1024 + (dc + 1) * 128],
                         xn[:, kc * T + t0: kc * T + t0 + TT],
                         start=(kc == 0), stop=(kc == NDC - 1),
                         inc=(smm if kc == NDC - 1 else None))
                if True:
                    r.wait("v", smm)
                    r.wait("v", str_)
                    r.wait("v", sst)
                    r.op("v", "tensor_scalar_add", atn[:, 0:TT], psA[:], bcol,
                         inc=sv)
                    r.wait("sc", sv)
                    r.op("sc", "dma_start", out=vfS[dc, :, t0:t0 + TT],
                         in_=atn[:, 0:TT], inc=sst, inck=16)
                    r.wait("te", sv)
                    for i in range(4):
                        r.op("te", "transpose", psD[:, i * 128:(i + 1) * 128],
                             atn[:, i * 128:(i + 1) * 128], idb[:], inc=str_)
                    r.wait("v", str_)
                    for i in range(4):
                        r.op("v", "tensor_copy",
                             atn[:, TT + i * 128: TT + (i + 1) * 128],
                             psD[:, i * 128:(i + 1) * 128], inc=sv)
                    r.wait("sc", sv)
                    for i in range(4):
                        r.op("sc", "dma_start",
                             out=vS[dc, t0 + i * 128: t0 + (i + 1) * 128, :],
                             in_=atn[:, TT + i * 128: TT + (i + 1) * 128],
                             inc=sst, inck=16)

    for e in ("te", "v", "sc", "g", "sy"):
        r.wait(e, sst)
        r.wait(e, sv)
        r.wait(e, swl)
    sao = tag + "ao"
    for sq in range(2):
        for dc in range(NDC):
            r.wait("sy", smm)
            r.wait("sy", sv)
            r.op("sy", "dma_start", out=atn[:, 0:S],
                 in_=qS[dc, :, sq * S:(sq + 1) * S], inc=sld, inck=16)
            r.op("sy", "dma_start", out=atn[:, S:2 * S],
                 in_=kS[dc, :, sq * S:(sq + 1) * S], inc=sld, inck=16)
            for kt in range(16):
                r.op("sy", "dma_start",
                     out=atn[:, 2 * S + kt * 128: 2 * S + (kt + 1) * 128],
                     in_=vS[dc, sq * S + kt * 128: sq * S + (kt + 1) * 128, :],
                     inc=sld, inck=16)
            r.wait("te", sld)
            for h in range(2):
                hb = h * 64
                for qb in range(4):
                    for qt in range(4):
                        q0 = qb * 512 + qt * 128
                        for kb in range(4):
                            r.wait("te", sv)
                            r.op("te", "matmul", psA[:],
                                 atn[hb:hb + 64, q0:q0 + 128],
                                 atn[hb:hb + 64, S + kb * 512: S + (kb + 1) * 512],
                                 start=True, stop=True, inc=smm)
                            r.wait("v", smm)
                            r.op("v", "tensor_copy",
                                 s_sb[:, kb * 512:(kb + 1) * 512], psA[:],
                                 inc=sv)
                        r.op("v", "reduce_max", cols[:, 0:1], s_sb[:, 0:S],
                             AX.X, inc=sv)
                        r.wait("v", sv)
                        r.op("v", "tensor_scalar_mul", cols[:, 1:2],
                             cols[:, 0:1], -0.125, inc=sv)
                        Prow = bufs["hA"][:, qt * S: (qt + 1) * S]
                        r.wait("sc", sv)
                        r.wait("sc", str_)
                        r.op("sc", "activation", Prow, s_sb[:, 0:S], AF.Exp,
                             bias=cols[:, 1:2], scale=0.125,
                             accum_out=cols[:, 2:3], inc=ssc)
                        r.wait("v", ssc)
                        r.op("v", "reciprocal", cols[:, 3:4], cols[:, 2:3],
                             inc=sv)
                        r.wait("v", sv)
                        r.op("v", "tensor_scalar_mul", Prow, Prow,
                             cols[:, 3:4], inc=tag + "pn")
                    r.wait("te", tag + "pn")
                    for grp0 in range(0, 64, 8):
                        r.wait("te", sv)
                        for gi in range(grp0, grp0 + 8):
                            kt, qt = divmod(gi, 4)
                            r.op("te", "transpose",
                                 psD[:, (gi % 8) * 128:(gi % 8 + 1) * 128],
                                 bufs["hA"][:, qt * S + kt * 128: qt * S + (kt + 1) * 128],
                                 idb[:], inc=str_)
                        r.wait("v", str_)
                        for gi in range(grp0, grp0 + 8):
                            kt, qt = divmod(gi, 4)
                            r.op("v", "tensor_copy",
                                 bufs["hA"][:, 8192 + kt * 512 + qt * 128:
                                            8192 + kt * 512 + (qt + 1) * 128],
                                 psD[:, (gi % 8) * 128:(gi % 8 + 1) * 128],
                                 inc=sv)
                    r.wait("te", sv)
                    for kt in range(16):
                        r.op("te", "matmul", psB[0:64, :],
                             atn[:, 2 * S + kt * 128 + hb: 2 * S + kt * 128 + hb + 64],
                             bufs["hA"][:, 8192 + kt * 512: 8192 + (kt + 1) * 512],
                             start=(kt == 0), stop=(kt == 15),
                             inc=(smm if kt == 15 else None))
                    r.wait("v", smm)
                    r.wait("v", sao)
                    r.op("v", "tensor_copy", atn[0:64, 6144:6656], psB[0:64, :],
                         inc=sv)
                    r.wait("sc", sv)
                    r.op("sc", "dma_start",
                         out=aoS[dc, hb:hb + 64, sq * S + qb * 512: sq * S + (qb + 1) * 512],
                         in_=atn[0:64, 6144:6656], inc=sao, inck=16)

    for e in ("te", "v", "sc", "g", "sy"):
        r.wait(e, sao)
        r.wait(e, sv)
    for tt in range(NTT):
        t0 = tt * TT
        sq0 = (t0 % S == 0)
        sq1 = ((t0 + TT) % S == 0)
        for dc in range(NDC):
            r.wait("sy", sv)
            r.wait("g", sv)
            lo = t0 if sq0 else t0 - 1
            hi = t0 + TT if sq1 else t0 + TT + 1
            off = 1 if sq0 else 0
            r.op("g", "dma_start", out=vle[:, off:off + (hi - lo)],
                 in_=vfS[dc, :, lo:hi], inc=sld, inck=16)
            r.op("sy", "dma_start", out=wk2[:], in_=xin[dc, :, t0:t0 + TT],
                 inc=sld, inck=16)
            r.op("sy", "dma_start", out=atn[:, 0:TT],
                 in_=aoS[dc, :, t0:t0 + TT], inc=sld, inck=16)
            r.wait("v", sld)
            if sq0 or sq1:
                if sq0:
                    r.op("v", "memset", vle[:, 0:1], 0.0)
                if sq1:
                    r.op("v", "memset", vle[:, TT + 1:TT + 2], 0.0, inc=sv)
                else:
                    r.op("v", "memset", vle[:, 0:1], 0.0, inc=sv)
                r.wait("v", sv)
            r.op("v", "tensor_scalar", wk1[:], vle[:, 1:TT + 1],
                 lep[:, 8 + dc:8 + dc + 1], lep[:, 24 + dc:24 + dc + 1],
                 ALU.mult, ALU.add)
            r.op("v", "scalar_tensor_tensor", wk1[:], vle[:, 0:TT],
                 lep[:, dc:dc + 1], wk1[:], ALU.mult, ALU.add)
            r.op("v", "scalar_tensor_tensor", wk1[:], vle[:, 2:TT + 2],
                 lep[:, 16 + dc:16 + dc + 1], wk1[:], ALU.mult, ALU.add)
            r.op("v", "tensor_tensor", wk1[:], wk1[:], atn[:, 0:TT], ALU.add)
            r.op("v", "tensor_tensor", wk1[:], wk1[:], wk2[:], ALU.add, inc=sv)
            r.wait("sy", sv)
            r.op("sy", "dma_start", out=xout[dc, :, t0:t0 + TT], in_=wk1[:],
                 inc=sxs, inck=16)
            r.wait("v", sxs)
    for e in ("te", "v", "sc", "g", "sy"):
        r.wait(e, sxs)


def emit_l1_attn_b(r, bufs, io, tag):
    xin, xout = io["xin"], io["xout"]
    WiT, WfT, WgT, WoT = io["hgWiT"], io["hgWfT"], io["hgWgT"], io["hgWoT"]
    hSf, sgS = io["hSf"], io["sgS"]
    nwcol = bufs["hg_nw"]
    xn = bufs["xn"]
    wB = bufs["wB"]
    wA = bufs["wA2"]
    fA, fB, fC = bufs["fA"], bufs["fB"], bufs["fC"]
    sil = bufs["wk0"]
    tmp = bufs["wk1"]
    sq = bufs["wk2"]
    sq2 = bufs["wk3"]
    row = bufs["row"]
    ones = bufs["ones"]
    xr = bufs["wk5"]
    psA, psB, psC = bufs["psA"], bufs["psB"], bufs["psC"]
    swl, smm, ssc, sv, sst = (tag + x for x in ("wl", "mm", "sc", "v", "st"))
    shl, sxl, sxs, s2 = (tag + x for x in ("hl", "xl", "xs", "s2"))

    emit_ln_pass(r, bufs, xin, "eps5", tag + "ln", xn_dst=xn)
    r.op("g", "dma_start", out=nwcol[:], in_=io["hg_nw_d"], inc=swl, inck=16)
    for kc in range(NDC):
        r.op("g", "dma_start", out=wA[:, kc * 1024:(kc + 1) * 1024],
             in_=WoT[kc * 128:(kc + 1) * 128, :], inc=swl, inck=16)

    for dc in range(NDC):
        r.wait("g", smm)
        for wi, WT in enumerate((WiT, WfT, WgT)):
            for kc in range(NDC):
                r.op("g", "dma_start",
                     out=wB[:, wi * 1024 + kc * 128: wi * 1024 + (kc + 1) * 128],
                     in_=WT[kc * 128:(kc + 1) * 128, dc * 128:(dc + 1) * 128],
                     inc=swl, inck=16)
        r.wait("te", swl)
        for sqi in range(2):
            for ttl in range(4):
                t0 = sqi * S + ttl * TT
                ts_loc = ttl * TT
                r.wait("te", sv)
                r.wait("te", ssc)
                for wi, pt in ((0, psA), (1, psB), (2, psC)):
                    for kc in range(NDC):
                        r.op("te", "matmul", pt[:],
                             wB[:, wi * 1024 + kc * 128: wi * 1024 + (kc + 1) * 128],
                             xn[:, kc * T + t0: kc * T + t0 + TT],
                             start=(kc == 0), stop=(kc == NDC - 1),
                             inc=(smm if kc == NDC - 1 else None))
                r.wait("sc", smm)
                r.wait("sc", sv)
                r.op("sc", "activation", fA[:, ts_loc:ts_loc + TT], psB[:],
                     AF.Sigmoid, inc=ssc)
                r.op("sc", "activation", sil[:], psA[:], AF.Silu, inc=ssc)
                r.wait("sc", tag + "sg")
                r.op("sc", "activation", tmp[:], psC[:], AF.Silu, inc=ssc)
                r.wait("sc", ssc)
                r.op("sc", "dma_start", out=sgS[dc, :, t0:t0 + TT], in_=tmp[:],
                     inc=tag + "sg", inck=16)
                r.wait("v", ssc)
                r.op("v", "tensor_scalar", sq[:], fA[:, ts_loc:ts_loc + TT],
                     -1.0, 1.0, ALU.mult, ALU.add)
                r.op("v", "tensor_tensor", fB[:, ts_loc:ts_loc + TT], sil[:],
                     sq[:], ALU.mult, inc=sv)
            r.wait("v", smm)
            r.op("v", "tensor_tensor_scan", fC[:], fA[:], fB[:], 0.0,
                 ALU.mult, ALU.add, inc=sv)
            r.wait("sy", sv)
            r.op("sy", "dma_start", out=hSf[dc, :, sqi * S:(sqi + 1) * S],
                 in_=fC[:], inc=sst, inck=16)
            r.wait("v", sst)

    for e in ("te", "v", "sc", "g", "sy"):
        r.wait(e, sst)
        r.wait(e, tag + "sg")
        r.wait(e, swl)
        r.wait(e, sv)
    for tt in range(NTT):
        t0 = tt * TT
        r.wait("sy", sv)
        for dc in range(NDC):
            r.op("sy", "dma_start", out=bufs["lnxf"][:, dc * TT:(dc + 1) * TT],
                 in_=hSf[dc, :, t0:t0 + TT], inc=shl, inck=16)
        r.wait("sc", shl)
        sqb = (sq, sq2)
        for dc in range(NDC):
            r.wait("sc", s2, r.n(s2) - 1)
            r.op("sc", "activation", sqb[dc % 2][:],
                 bufs["lnxf"][:, dc * TT:(dc + 1) * TT], AF.Square, inc=ssc)
            r.wait("te", ssc)
            r.op("te", "matmul", psC[0:1, :], ones[:, 0:1], sqb[dc % 2][:],
                 start=(dc == 0), stop=(dc == NDC - 1), inc=s2)
        mrow = row[:, 0:TT]
        r.wait("v", s2)
        r.op("v", "tensor_scalar_mul", mrow, psC[0:1, :], 1.0 / D, inc=sv)
        r.wait("sc", sv)
        r.op("sc", "activation", mrow, mrow, AF.Ln,
             bias=bufs["cvec"][0:1, CONST_COLS["eps_hgrn"]:CONST_COLS["eps_hgrn"] + 1])
        r.op("sc", "activation", mrow, mrow, AF.Exp, scale=-0.5, inc=ssc)
        r.wait("te", ssc)
        r.op("te", "matmul", psA[:], ones[0:1, :], mrow, start=True, stop=True,
             inc=smm)
        r.wait("v", smm)
        for dc in range(NDC):
            r.wait("sc", sv)
            r.op("sc", "dma_start", out=sil[:], in_=sgS[dc, :, t0:t0 + TT],
                 inc=sxl, inck=16)
            r.op("v", "tensor_tensor", tmp[:],
                 bufs["lnxf"][:, dc * TT:(dc + 1) * TT], psA[:], ALU.mult)
            r.op("v", "tensor_scalar_mul", tmp[:], tmp[:], nwcol[:, dc:dc + 1])
            r.wait("v", sxl)
            r.op("v", "tensor_tensor", xn[:, dc * T + t0: dc * T + t0 + TT],
                 tmp[:], sil[:], ALU.mult, inc=sv)

    for e in ("te", "v", "sc", "g", "sy"):
        r.wait(e, sv)
    for tt in range(NTT):
        t0 = tt * TT
        for dc in range(NDC):
            pt = psA if dc % 2 == 0 else psB
            r.wait("te", sv)
            for kc in range(NDC):
                r.op("te", "matmul", pt[:],
                     wA[:, kc * 1024 + dc * 128: kc * 1024 + (dc + 1) * 128],
                     xn[:, kc * T + t0: kc * T + t0 + TT],
                     start=(kc == 0), stop=(kc == NDC - 1),
                     inc=(smm if kc == NDC - 1 else None))
            r.wait("sc", sv)
            r.op("sc", "dma_start", out=xr[:], in_=xin[dc, :, t0:t0 + TT],
                 inc=sxl, inck=16)
            r.wait("v", sxs)
            r.wait("v", smm)
            r.wait("v", sxl)
            r.op("v", "tensor_tensor", tmp[:], pt[:], xr[:], ALU.add, inc=sv)
            r.wait("sy", sv)
            r.op("sy", "dma_start", out=xout[dc, :, t0:t0 + TT], in_=tmp[:],
                 inc=sxs, inck=16)
    for e in ("te", "v", "sc", "g", "sy"):
        r.wait(e, sxs)


def emit_expert_cast(r, bufs, io):
    evS = io["evS"]
    hA = bufs["hA"]
    sld, sst = "ecld", "ecst"
    for e in range(64):
        groups = (
            [(io["moeEG"][e, kc * 128:(kc + 1) * 128, :],
              evS[e, :, kc * 512:(kc + 1) * 512], kc * 512, 512)
             for kc in range(NDC)] +
            [(io["moeEU"][e, kc * 128:(kc + 1) * 128, :],
              evS[e, :, 4096 + kc * 512: 4096 + (kc + 1) * 512],
              4096 + kc * 512, 512) for kc in range(NDC)] +
            [(io["moeED"][e, kc * 128:(kc + 1) * 128, :],
              evS[e, :, 8192 + kc * 1024: 8192 + (kc + 1) * 1024],
              8192 + kc * 1024, 1024) for kc in range(4)])
        r.wait("g", sst)
        for (srcap, dstap, off, wdt) in groups:
            r.op("g", "dma_start", out=hA[:, off:off + wdt], in_=srcap,
                 inc=sld, inck=16)
        r.wait("sc", sld)
        for (srcap, dstap, off, wdt) in groups:
            r.op("sc", "dma_start", out=dstap, in_=hA[:, off:off + wdt],
                 inc=sst, inck=16)
    for e in ("g", "sc", "sy", "v", "te"):
        r.wait(e, sst)


def emit_l1_ffn(r, bufs, io, tag):
    xin, xout = io["xin"], io["xout"]
    evS, combS, routS = io["evS"], io["combS"], io["routS"]
    shS, hS = io["xaS"], io["hS"]
    xn = bufs["xn"]
    wA2, wA, wB = bufs["wA2"], bufs["wA"], bufs["wB"]
    hA = bufs["hA"]
    atn = bufs["atn"]
    idf = bufs["idf"]
    cols = bufs["cols"]
    wk0, wk1, wk2, wk3, wk4, xr = (bufs[f"wk{i}"] for i in range(6))
    row = bufs["row"]
    zb16 = bufs["zb16"]
    gmx = bufs["gmx"]
    psA, psB, psC = bufs["psA"], bufs["psB"], bufs["psC"]
    psD = bufs["psD"]
    dnb = (bufs["psE"], bufs["psF"], bufs["psG"], bufs["psH"])
    swl, smm, ssc, sv, sst = (tag + x for x in ("wl", "mm", "sc", "v", "st"))
    sld, sxl, sxs, str_ = (tag + x for x in ("ld", "xl", "xs", "tr"))
    scb, sfl = tag + "cb", tag + "fl"

    emit_ln_pass(r, bufs, xin, "eps5", tag + "ln", xn_dst=xn)
    r.op("v", "memset", zb16[:], 0.0, inc=sv)
    r.op("g", "dma_start", out=idf[:], in_=io["identf"], inc=swl, inck=16)
    r.op("g", "dma_start", out=bufs["gbb"][:], in_=io["moeGB"], inc=swl, inck=16)
    for kc in range(NDC):
        r.op("g", "dma_start", out=wB[:, kc * 64:(kc + 1) * 64],
             in_=io["moeGT"][kc * 128:(kc + 1) * 128, :], inc=swl, inck=16)
    for e in Rec.ENGS:
        r.wait(e, swl)
        r.wait(e, sv)

    r.op("v", "memset", wk1[64:128, :], 0.0)
    r.op("v", "memset", wk4[:, 64:128], 0.0, inc=sv)
    for tt in range(NTT):
        t0 = tt * TT
        r.wait("te", sv)
        r.wait("te", ssc)
        for kc in range(NDC):
            r.op("te", "matmul", psA[0:64, :], wB[:, kc * 64:(kc + 1) * 64],
                 xn[:, kc * T + t0: kc * T + t0 + TT],
                 start=(kc == 0), stop=(kc == NDC - 1),
                 inc=(smm if kc == NDC - 1 else None))
        r.wait("sc", smm)
        r.wait("sc", sv)
        r.wait("sc", str_)
        r.op("sc", "activation", wk1[0:64, :], psA[0:64, :], AF.Sigmoid,
             inc=ssc)
        r.wait("v", ssc)
        r.op("v", "tensor_scalar_add", wk1[0:64, :], wk1[0:64, :],
             bufs["gbb"][0:64, 0:1], inc=sv)
        for i4 in range(4):
            tloc = t0 + i4 * 128
            r.wait("te", sv)
            r.op("te", "transpose", psB[:, 0:128],
                 wk1[:, i4 * 128:(i4 + 1) * 128], idf[:], inc=str_)
            r.wait("v", str_)
            r.wait("v", scb)
            stok, ms, comb = wk2[:, 0:64], wk3[:, 0:64], wk4[:, 0:64]
            eq64 = wk0[:, 0:64]
            gmax, gw, gsel, eq8 = (gmx[:, i * 8:(i + 1) * 8] for i in range(4))

            def vs(meth, *a, **kw):
                r.op("v", meth, *a, inc=sv, **kw)
                r.wait("v", sv)

            vs("tensor_copy", stok, psB[:, 0:64])
            s3 = bass.AP(stok.tensor, stok.offset,
                         [stok.ap[0], [8, 8], [1, 8]])
            vs("reduce_max", gmax, s3, AX.X)
            vs("tensor_copy", gw, gmax)
            vs("memset", gsel, 0.0)
            for _ in range(4):
                vs("reduce_max", cols[:, 0:1], gw, AX.X)
                vs("tensor_scalar", eq8, gw, cols[:, 0:1], None, ALU.is_ge)
                vs("tensor_tensor", gsel, gsel, eq8, ALU.add)
                vs("scalar_tensor_tensor", gw, eq8, -1e30, gw,
                   ALU.mult, ALU.add)
            for g8 in range(8):
                vs("tensor_scalar_mul", ms[:, g8 * 8:(g8 + 1) * 8],
                   stok[:, g8 * 8:(g8 + 1) * 8], gsel[:, g8:g8 + 1])
            vs("memset", comb, 0.0)
            vs("memset", cols[:, 4:5], 0.0)
            for _ in range(6):
                vs("reduce_max", cols[:, 0:1], ms, AX.X)
                vs("tensor_scalar_max", cols[:, 1:2], cols[:, 0:1], 0.0)
                vs("tensor_tensor", cols[:, 4:5], cols[:, 4:5],
                   cols[:, 1:2], ALU.add)
                vs("tensor_scalar", eq64, ms, cols[:, 0:1], None, ALU.is_ge)
                vs("scalar_tensor_tensor", comb, eq64, cols[:, 1:2],
                   comb, ALU.mult, ALU.add)
                vs("scalar_tensor_tensor", ms, eq64, -1e30, ms,
                   ALU.mult, ALU.add)
            vs("tensor_scalar_add", cols[:, 5:6], cols[:, 4:5], 1e-8)
            vs("reciprocal", cols[:, 6:7], cols[:, 5:6])
            vs("tensor_scalar_mul", comb, comb, cols[:, 6:7])
            r.wait("te", sv)
            r.op("te", "transpose", psB[:, 128:256], wk4[:, 0:128], idf[:],
                 inc=str_)
            r.wait("v", str_)
            r.op("v", "tensor_copy", wk0[0:64, 0:128], psB[0:64, 128:256],
                 inc=sv)
            r.wait("sc", sv)
            r.op("sc", "dma_start", out=combS[0:64, tloc:tloc + 128],
                 in_=wk0[0:64, 0:128], inc=scb, inck=16)

    for e in Rec.ENGS:
        r.wait(e, sv)
        r.wait(e, scb)
    for kc in range(NDC):
        r.op("g", "dma_start", out=wA2[:, kc * 1024:(kc + 1) * 1024],
             in_=io["moeSG"][kc * 128:(kc + 1) * 128, :], inc=swl, inck=16)
        r.op("g", "dma_start", out=hA[:, 8192 + kc * 1024: 8192 + (kc + 1) * 1024],
             in_=io["moeSU"][kc * 128:(kc + 1) * 128, :], inc=swl, inck=16)
    r.wait("te", swl)
    for tt in range(NTT):
        t0 = tt * TT
        for ic in range(NDC):
            r.wait("te", sv)
            r.wait("te", ssc)
            for kc in range(NDC):
                r.op("te", "matmul", psA[:],
                     wA2[:, kc * 1024 + ic * 128: kc * 1024 + (ic + 1) * 128],
                     xn[:, kc * T + t0: kc * T + t0 + TT],
                     start=(kc == 0), stop=(kc == NDC - 1))
            for kc in range(NDC):
                r.op("te", "matmul", psB[:],
                     hA[:, 8192 + kc * 1024 + ic * 128: 8192 + kc * 1024 + (ic + 1) * 128],
                     xn[:, kc * T + t0: kc * T + t0 + TT],
                     start=(kc == 0), stop=(kc == NDC - 1),
                     inc=(smm if kc == NDC - 1 else None))
            r.wait("sc", smm)
            r.wait("sc", sv)
            r.op("sc", "activation", wk1[:], psA[:], AF.Silu, inc=ssc)
            r.wait("v", ssc)
            r.wait("v", sst)
            r.op("v", "tensor_tensor", atn[:, 0:TT], wk1[:], psB[:], ALU.mult,
                 inc=sv)
            r.wait("sc", sv)
            r.op("sc", "dma_start", out=hS[ic, :, t0:t0 + TT],
                 in_=atn[:, 0:TT], inc=sst, inck=16)
    for e in Rec.ENGS:
        r.wait(e, sst)
        r.wait(e, sv)
    for kc in range(NDC):
        r.op("g", "dma_start", out=wA2[:, kc * 1024:(kc + 1) * 1024],
             in_=io["moeSD"][kc * 128:(kc + 1) * 128, :], inc=swl, inck=16)
    for tt in range(NTT):
        t0 = tt * TT
        r.wait("sy", smm)
        r.wait("sy", sv)
        for kc in range(NDC):
            r.op("sy", "dma_start", out=hA[:, kc * 512:(kc + 1) * 512],
                 in_=hS[kc, :, t0:t0 + TT], inc=sld, inck=16)
        r.wait("te", swl)
        r.wait("te", sld)
        for dci in range(NDC):
            pt = psA if dci % 2 == 0 else psB
            r.wait("te", sv)
            for kc in range(NDC):
                r.op("te", "matmul", pt[:],
                     wA2[:, kc * 1024 + dci * 128: kc * 1024 + (dci + 1) * 128],
                     hA[:, kc * 512:(kc + 1) * 512],
                     start=(kc == 0), stop=(kc == NDC - 1),
                     inc=(smm if kc == NDC - 1 else None))
            r.wait("v", smm)
            r.wait("v", sst)
            r.op("v", "tensor_copy", wk1[:], pt[:], inc=sv)
            r.wait("sc", sv)
            r.op("sc", "dma_start", out=shS[dci, :, t0:t0 + TT], in_=wk1[:],
                 inc=sst, inck=16)
            r.wait("v", sst)

    for e in Rec.ENGS:
        r.wait(e, sst)
        r.wait(e, sv)
        r.wait(e, smm)
    dn = (bufs["psA"], bufs["psB"], bufs["psC"], bufs["psD"],
          bufs["psE"], bufs["psF"], bufs["psG"], bufs["psH"])
    psG, psH = bufs["psG"], bufs["psH"]
    cbuf = bufs["cbuf"]
    fstg = bufs["lnxf"]
    wa2_free = r.n(smm)
    cb_prev = r.n(smm)
    for tp in range(4):
        tA = 2 * tp * TT
        tB = tA + TT
        for g4 in range(16):
            for el in range(4):
                e = g4 * 4 + el
                r.wait("g", smm, wa2_free)
                r.op("g", "dma_start", out=wA2[:], in_=evS[e, :, 0:8192],
                     inc=swl, inck=16)
                r.wait("sy", smm, cb_prev)
                r.op("sy", "dma_start", out=row[:, 0:TT],
                     in_=combS[e, tA:tA + TT], inc=sld, inck=16)
                r.op("sy", "dma_start", out=row[:, TT:2 * TT],
                     in_=combS[e, tB:tB + TT], inc=sld, inck=16)
                r.wait("te", sld)
                r.wait("te", sv)
                r.op("te", "matmul", psG[:], bufs["ones"][0:1, :],
                     row[:, 0:TT], start=True, stop=True)
                r.op("te", "matmul", psH[:], bufs["ones"][0:1, :],
                     row[:, TT:2 * TT], start=True, stop=True, inc=smm)
                cb_mms = cb_prev = r.n(smm)
                r.wait("v", smm, cb_mms)
                r.op("v", "tensor_copy", cbuf[:, 0:TT], psG[:])
                r.op("v", "tensor_copy", cbuf[:, TT:2 * TT], psH[:], inc=sv)
                cbuf_ready = r.n(sv)
                r.wait("te", swl)
                silu_done = r.n(ssc)
                u_cons = r.n(sv)
                for ic in range(4):
                    r.wait("te", ssc, silu_done)
                    r.wait("te", sv, u_cons)
                    for kc in range(NDC):
                        L = wA2[:, kc * 512 + ic * 128: kc * 512 + (ic + 1) * 128]
                        r.op("te", "matmul", dn[0][:], L,
                             xn[:, kc * T + tA: kc * T + tA + TT],
                             start=(kc == 0), stop=(kc == NDC - 1))
                        r.op("te", "matmul", dn[1][:], L,
                             xn[:, kc * T + tB: kc * T + tB + TT],
                             start=(kc == 0), stop=(kc == NDC - 1),
                             inc=(smm if kc == NDC - 1 else None))
                    g_done = r.n(smm)
                    for kc in range(NDC):
                        L = wA2[:, 4096 + kc * 512 + ic * 128: 4096 + kc * 512 + (ic + 1) * 128]
                        r.op("te", "matmul", dn[2][:], L,
                             xn[:, kc * T + tA: kc * T + tA + TT],
                             start=(kc == 0), stop=(kc == NDC - 1))
                        r.op("te", "matmul", dn[4][:], L,
                             xn[:, kc * T + tB: kc * T + tB + TT],
                             start=(kc == 0), stop=(kc == NDC - 1),
                             inc=(smm if kc == NDC - 1 else None))
                    u_done = r.n(smm)
                    r.wait("sc", smm, g_done)
                    r.wait("sc", sv)
                    r.op("sc", "activation", wk1[:], dn[0][:], AF.Silu)
                    r.op("sc", "activation", wk2[:], dn[1][:], AF.Silu,
                         inc=ssc)
                    silu_done = r.n(ssc)
                    r.wait("v", ssc, silu_done)
                    r.wait("v", smm, u_done)
                    r.wait("v", sv, cbuf_ready)
                    hb0 = el * 4096 + ic * 512
                    r.op("v", "tensor_tensor", wk3[:], wk1[:], dn[2][:],
                         ALU.mult)
                    r.op("v", "tensor_tensor", hA[:, hb0:hb0 + TT], wk3[:],
                         cbuf[:, 0:TT], ALU.mult)
                    r.op("v", "tensor_tensor", wk4[:], wk2[:], dn[4][:],
                         ALU.mult)
                    r.op("v", "tensor_tensor", hA[:, hb0 + 2048:hb0 + 2048 + TT],
                         wk4[:], cbuf[:, TT:2 * TT], ALU.mult, inc=sv)
                    u_cons = r.n(sv)
                wa2_free = r.n(smm)
            wa_free = r.n(smm)
            for tx, tbase in ((0, tA), (2048, tB)):
                r.wait("te", sv)
                for hc in range(NDC):
                    r.op("te", "matmul", dn[hc][:], zb16[:], xn[:, 0:512],
                         start=True, stop=True,
                         inc=(smm if hc == NDC - 1 else None))
                for el in range(4):
                    e = g4 * 4 + el
                    r.wait("g", smm, wa_free)
                    r.op("g", "dma_start", out=wA[:], in_=evS[e, :, 8192:12288],
                         inc=swl, inck=16)
                    r.wait("te", swl)
                    for hc in range(NDC):
                        for ic in range(4):
                            r.op("te", "matmul", dn[hc][:],
                                 wA[:, ic * 1024 + hc * 128: ic * 1024 + (hc + 1) * 128],
                                 hA[:, el * 4096 + tx + ic * 512:
                                    el * 4096 + tx + (ic + 1) * 512],
                                 start=False, stop=(ic == 3),
                                 inc=(smm if (hc == NDC - 1 and ic == 3) else None))
                    wa_free = r.n(smm)
                r.wait("v", smm)
                r.wait("v", sfl)
                for hc in range(NDC):
                    r.op("v", "tensor_copy", fstg[:, hc * 512:(hc + 1) * 512],
                         dn[hc][:], inc=sv)
                r.wait("g", sv)
                for hc in range(NDC):
                    kw = {} if g4 == 0 else {"accum_op": ALU.add}
                    r.op("g", "dma_start",
                         out=routS[hc, :, tbase:tbase + TT],
                         in_=fstg[:, hc * 512:(hc + 1) * 512],
                         inc=sfl, inck=16, **kw)

    for e in Rec.ENGS:
        r.wait(e, sfl)
        r.wait(e, sv)
    for tt in range(NTT):
        t0 = tt * TT
        for dci in range(NDC):
            r.wait("sy", sv)
            r.op("sy", "dma_start", out=wk1[:], in_=xin[dci, :, t0:t0 + TT],
                 inc=sld, inck=16)
            r.op("sy", "dma_start", out=wk2[:], in_=shS[dci, :, t0:t0 + TT],
                 inc=sld, inck=16)
            r.op("sy", "dma_start", out=wk3[:], in_=routS[dci, :, t0:t0 + TT],
                 inc=sld, inck=16)
            r.wait("v", sld)
            r.wait("v", sxs)
            r.op("v", "tensor_tensor", wk4[:], wk1[:], wk2[:], ALU.add)
            r.op("v", "tensor_tensor", wk4[:], wk4[:], wk3[:], ALU.add, inc=sv)
            r.wait("sy", sv)
            r.op("sy", "dma_start", out=xout[dci, :, t0:t0 + TT], in_=wk4[:],
                 inc=sxs, inck=16)
    for e in Rec.ENGS:
        r.wait(e, sxs)


def alloc_bufs(nc, stack):
    bufs = {}

    def sb(name, shape, dt):
        bufs[name] = stack.enter_context(nc.sbuf_tensor("sb_" + name, shape, dt))[:]

    sb("xn", [128, 8 * T], BF16)
    sb("wA", [128, 8 * 512], BF16)
    sb("wA2", [128, 8 * 1024], BF16)
    sb("wB", [128, 3 * 1024], BF16)
    sb("hA", [128, 32 * 512], BF16)
    sb("fC", [128, S], F32)
    sb("hg_nw", [128, 8], F32)
    sb("atn", [128, 8192], BF16)
    sb("idb", [128, 128], BF16)
    sb("cols", [128, 8], F32)
    sb("obc", [128, 8], F32)
    sb("bq3", [128, 24], F32)
    sb("lep", [128, 32], F32)
    sb("gbc", [128, 48], F32)
    sb("mb1", [128, 32], F32)
    sb("trb", [128, 128], F32)
    sb("onb", [128, 128], BF16)
    sb("pst", [128, 128], BF16)
    sb("zb16", [128, 128], BF16)
    sb("idf", [128, 128], F32)
    sb("gmx", [128, 32], F32)
    sb("gbb", [128, 2], F32)
    sb("cbuf", [128, 1024], F32)
    sb("lnxf", [128, 8 * TT], F32)
    sb("row", [1, 4 * TT], F32)
    sb("ones", [128, 128], F32)
    sb("cvec", [128, 8], F32)
    for i in range(6):
        sb(f"wk{i}", [128, TT], F32)
    bufs["fA"] = bufs["lnxf"][:, 0:S]
    bufs["fB"] = bufs["lnxf"][:, S:2 * S]

    def ps(name, shape, dt):
        bufs[name] = stack.enter_context(nc.psum_tensor("ps_" + name, shape, dt))[:]

    ps("psA", [128, 512], F32)
    ps("psB", [128, 512], F32)
    ps("psC", [128, 512], F32)
    ps("psD", [128, 512], F32)
    ps("psE", [128, 512], F32)
    ps("psF", [128, 512], F32)
    ps("psG", [128, 512], F32)
    ps("psH", [128, 512], F32)
    bufs["psDb"] = bufs["psD"].bitcast(BF16)
    return bufs


def emit_const_init(r, bufs, consts_dram):
    r.op("sy", "dma_start", out=bufs["ones"], in_=consts_dram["ones"],
         inc="c0ld", inck=16)
    r.op("sy", "dma_start", out=bufs["cvec"], in_=consts_dram["cvec"],
         inc="c0ld", inck=16)
    for e in Rec.ENGS:
        r.wait(e, "c0ld")


SUB_BUILDERS = {}


DEBUG_TAPS = []
DEBUG_SCRATCH = []


def build_program(subnames, shapes_inputs):
    nc = bass.Bass("TRN2", target_bir_lowering=False)
    dram = {}
    for name, (shape, dt) in shapes_inputs.items():
        mdt = F32 if dt == np.float32 else BF16
        dram[name] = nc.dram_tensor(name, list(shape), mdt,
                                    kind="ExternalInput").ap()
    xo = nc.dram_tensor("xo", [NDC, 128, T], F32, kind="ExternalOutput").ap()
    def _kind(nm):
        return "ExternalOutput" if nm in DEBUG_SCRATCH else "Internal"
    scratch = {
        "hS": nc.dram_tensor("hS", [32, 128, T], BF16, kind=_kind("hS")).ap(),
        "hSf": nc.dram_tensor("hSf", [NDC, 128, T], F32, kind=_kind("hSf")).ap(),
        "sgS": nc.dram_tensor("sgS", [NDC, 128, T], F32, kind=_kind("sgS")).ap(),
        "qS": nc.dram_tensor("qS", [NDC, 128, T], BF16, kind=_kind("qS")).ap(),
        "kS": nc.dram_tensor("kS", [NDC, 128, T], BF16, kind=_kind("kS")).ap(),
        "vS_tm": nc.dram_tensor("vS_tm", [NDC, T, 128], BF16, kind=_kind("vS_tm")).ap(),
        "aoS": nc.dram_tensor("aoS", [NDC, 128, T], BF16, kind=_kind("aoS")).ap(),
        "vfS": nc.dram_tensor("vfS", [NDC, 128, T], BF16, kind=_kind("vfS")).ap(),
        "inpS": nc.dram_tensor("inpS", [NDC, 128, T], F32, kind=_kind("inpS")).ap(),
        "xaS": nc.dram_tensor("xaS", [NDC, 128, T], F32, kind=_kind("xaS")).ap(),
        "evS": nc.dram_tensor("evS", [64, 128, 12288], BF16, kind="Internal").ap(),
        "combS": nc.dram_tensor("combS", [64, T], F32, kind=_kind("combS")).ap(),
        "routS": nc.dram_tensor("routS", [NDC, 128, T], F32, kind=_kind("routS")).ap(),
    }
    for tapname in DEBUG_TAPS:
        scratch["tap_" + tapname] = nc.dram_tensor(
            "tap_" + tapname, [NDC, 128, T], F32, kind="ExternalOutput").ap()
    stack = ExitStack()
    with stack:
        bufs = alloc_bufs(nc, stack)
        r = Rec()
        emit_const_init(r, bufs, dram)
        if any(s.startswith("l1_ffn") for s in subnames):
            emit_expert_cast(r, bufs, {**dram, **scratch})
        x_cur = dram["xt"]
        for i, sub in enumerate(subnames):
            x_next = xo if i == len(subnames) - 1 else \
                nc.dram_tensor(f"xmid{i}", [NDC, 128, T], F32, kind="Internal").ap()
            io = {"xin": x_cur, "xout": x_next, **dram, **scratch}
            SUB_BUILDERS[sub](r, bufs, io, "u_")
            x_cur = x_next
        r.build(nc, stack)
    return nc


SUB_BUILDERS["l0_ffn"] = emit_l0_ffn
SUB_BUILDERS["l0_ffn2"] = emit_l0_ffn
SUB_BUILDERS["l1_attn_b"] = emit_l1_attn_b
SUB_BUILDERS["l0_attn_b"] = emit_l0_attn_b
SUB_BUILDERS["l1_attn_a"] = emit_l1_attn_a
SUB_BUILDERS["l0_attn_a"] = emit_l0_attn_a
SUB_BUILDERS["l1_ffn"] = emit_l1_ffn
SUB_BUILDERS["l1_ffn2"] = emit_l1_ffn


def to_fmajor(x2):
    n = x2.shape[0]
    return np.ascontiguousarray(
        x2.reshape(n * S, NDC, 128).transpose(1, 2, 0))


def from_fmajor(xt, n):
    return np.ascontiguousarray(xt.transpose(2, 0, 1).reshape(n, S, D))


CONST_COLS = {"eps5": 0, "eps6": 1, "eps_hgrn": 2}


def rope_full_tables():
    half = 512
    pos = np.arange(S, dtype=np.float32)[:, None]
    freqs = (1.0 / 10000.0 ** (np.arange(half, dtype=np.float32) / half))
    ang = pos * freqs[None, :]
    cos = np.cos(ang).T.astype(np.float32)
    sin = np.sin(ang).T.astype(np.float32)
    return (np.ascontiguousarray(cos.reshape(4, 128, S)),
            np.ascontiguousarray(sin.reshape(4, 128, S)))


def host_consts():
    cv = np.zeros((128, 8), np.float32)
    cv[:, 0] = 1e-5
    cv[:, 1] = 1e-6
    cv[:, 2] = 1e-5
    qp = np.arange(128, dtype=np.float32)[:, None]
    kf = np.arange(S, dtype=np.float32)[None, :]
    d0 = kf - qp
    j = np.arange(512)[None, :]
    m4 = np.concatenate(
        [np.where(j <= qt * 128 + qp, 0.0, -1e9).astype(np.float32)
         for qt in range(4)], axis=1)
    rc, rs = rope_full_tables()
    div = np.exp(np.arange(0, 64, 2, dtype=np.float32) * (-np.log(10000.0) / 64))
    ang1 = np.arange(S, dtype=np.float32)[:, None] * div[None, :]
    c1 = np.cos(ang1).T
    s1 = np.sin(ang1).T
    rc1 = np.ascontiguousarray(np.tile(c1, (4, 1)).astype(np.float32))
    rs1 = np.ascontiguousarray(np.tile(s1, (4, 1)).astype(np.float32))
    return {"ones": np.ones((128, 128), np.float32), "cvec": cv,
            "d0c": d0.astype(np.float32), "m4c": m4,
            "ropeC": rc, "ropeS": rs,
            "ropeC1": rc1, "ropeS1": rs1,
            "identb": np.eye(128).astype(BF),
            "triuC": np.triu(np.ones((128, 128), np.float32)).T.copy() * 0 +
                     (np.arange(128)[:, None] <= np.arange(128)[None, :]
                      ).astype(np.float32),
            "onesbC": np.ones((128, 128)).astype(BF),
            "identf": np.eye(128, dtype=np.float32)}


def prep_weights(inputs):
    g = {k: np.asarray(v) for k, v in inputs.items()}
    w = {}
    w["wfcT"] = np.ascontiguousarray(g["L0_ffn_W_fc"].T)
    w["wprojT"] = np.ascontiguousarray(g["L0_ffn_W_proj"].T)
    w["hgWiT"] = np.ascontiguousarray(g["L1_attn_b_Wi"].T)
    w["hgWfT"] = np.ascontiguousarray(g["L1_attn_b_Wf"].T)
    w["hgWgT"] = np.ascontiguousarray(g["L1_attn_b_Wg"].T)
    w["hgWoT"] = np.ascontiguousarray(g["L1_attn_b_Wo"].T)
    w["hg_nw_d"] = np.ascontiguousarray(g["L1_attn_b_norm_w"].reshape(8, 128).T)
    w["abWq"] = np.ascontiguousarray(g["L0_attn_b_wq"].T)
    w["abWk"] = np.ascontiguousarray(g["L0_attn_b_wk"].T)
    w["abWv"] = np.ascontiguousarray(g["L0_attn_b_wv"].T)
    w["abWo"] = np.ascontiguousarray(g["L0_attn_b_out_w"].T)
    w["ab_ob"] = np.ascontiguousarray(g["L0_attn_b_out_b"].reshape(8, 128).T)
    E = np.concatenate([h * 64 + np.arange(0, 64, 2) for h in range(16)])
    O = np.concatenate([h * 64 + np.arange(1, 64, 2) for h in range(16)])
    perm1024 = np.concatenate([E, O])
    wqkv = g["L1_attn_a_w_qkv"]
    bqkv = g["L1_attn_a_b_qkv"]
    wq1 = wqkv[0:1024][perm1024]
    wk1_ = wqkv[1024:2048][perm1024]
    wv1 = wqkv[2048:3072]
    w["laWqkv"] = np.ascontiguousarray(
        np.concatenate([wq1, wk1_, wv1], 0).T)
    bq = bqkv[0:1024][perm1024].reshape(8, 128).T
    bk = bqkv[1024:2048][perm1024].reshape(8, 128).T
    bv = bqkv[2048:3072].reshape(8, 128).T
    w["la_bias"] = np.ascontiguousarray(np.concatenate([bq, bk, bv], 1))
    wl = g["L1_attn_a_w_lepe"][:, 0, :]
    lcols = [np.ascontiguousarray(wl[:, i].reshape(8, 128).T) for i in range(3)]
    lcols.append(np.ascontiguousarray(g["L1_attn_a_b_lepe"].reshape(8, 128).T))
    w["la_lepe"] = np.ascontiguousarray(np.concatenate(lcols, 1))
    w["aawq"] = np.ascontiguousarray(g["L0_attn_a_wq"].reshape(1024, 1024))
    w["aawk"] = np.ascontiguousarray(g["L0_attn_a_wk"].reshape(1024, 1024))
    w["aawv"] = np.ascontiguousarray(g["L0_attn_a_wv"].reshape(1024, 1024))
    w["aawo"] = np.ascontiguousarray(g["L0_attn_a_wo"].reshape(1024, 1024))
    w["aaw1"] = np.ascontiguousarray(g["L0_attn_a_w1"])
    w["aaw2"] = np.ascontiguousarray(g["L0_attn_a_w2"])
    cols48 = np.zeros((128, 48), np.float32)
    cols48[:, 0:8] = g["L0_attn_a_ln1_scale"].reshape(8, 128).T
    cols48[:, 8:16] = g["L0_attn_a_ln1_bias"].reshape(8, 128).T
    cols48[:, 16:24] = g["L0_attn_a_ln2_scale"].reshape(8, 128).T
    cols48[:, 24:32] = g["L0_attn_a_ln2_bias"].reshape(8, 128).T
    cols48[:, 32:40] = g["L0_attn_a_b2"].reshape(8, 128).T
    w["aagb"] = np.ascontiguousarray(cols48)
    w["aab1"] = np.ascontiguousarray(g["L0_attn_a_b1"].reshape(32, 128).T)
    w["moeEG"] = np.ascontiguousarray(g["L1_ffn_eg_w"].transpose(0, 2, 1))
    w["moeEU"] = np.ascontiguousarray(g["L1_ffn_eu_w"].transpose(0, 2, 1))
    w["moeED"] = np.ascontiguousarray(g["L1_ffn_ed_w"].transpose(0, 2, 1))
    w["moeGT"] = np.ascontiguousarray(g["L1_ffn_gate_w"].T)
    gb = np.zeros((128, 2), np.float32)
    gb[0:64, 0] = g["L1_ffn_gate_bias"]
    w["moeGB"] = gb
    w["moeSG"] = np.ascontiguousarray(g["L1_ffn_sg_w"].T)
    w["moeSU"] = np.ascontiguousarray(g["L1_ffn_su_w"].T)
    w["moeSD"] = np.ascontiguousarray(g["L1_ffn_sd_w"].T)
    return w


def kernel_chain(x, inputs, subs):
    x = np.asarray(x)
    consts = host_consts()
    w = prep_weights(inputs)
    per_core_const = {**consts, **w}
    shapes = {"xt": ((NDC, 128, T), np.float32)}
    for k, v in per_core_const.items():
        shapes[k] = (v.shape, v.dtype)
    nc = build_program(subs, shapes)
    in_maps = []
    for c in range(NCORES):
        xt = to_fmajor(x[2 * c:2 * c + 2])
        in_maps.append({"xt": xt, **per_core_const})
    res = run_bass_kernel_spmd(nc, in_maps, core_ids=list(range(NCORES)))
    out = np.empty_like(x)
    for c in range(NCORES):
        out[2 * c:2 * c + 2] = from_fmajor(res.results[c]["xo"], 2)
    return out


ALL_SUBS = ["l0_attn_a", "l0_ffn", "l0_attn_b", "l0_ffn2",
            "l1_attn_a", "l1_ffn", "l1_attn_b", "l1_ffn2"]


def kernel(**inputs):
    x = np.asarray(inputs["L0_attn_a_inputs"])
    return kernel_chain(x, inputs, ALL_SUBS)


# version 03
# speedup vs baseline: 1.2604x; 1.0084x over previous
import numpy as np
import ml_dtypes
from contextlib import ExitStack

import concourse.bass as bass
import concourse.mybir as mybir
from concourse.bass_utils import run_bass_kernel_spmd

F32 = mybir.dt.float32
BF16 = mybir.dt.bfloat16
AF = mybir.ActivationFunctionType
ALU = mybir.AluOpType
AX = mybir.AxisListType

NCORES = 8
T = 4096
S = 2048
D = 1024
NDC = 8
TT = 512
NTT = T // TT
BF = np.dtype(ml_dtypes.bfloat16)


class Rec:
    ENGS = ("te", "v", "sc", "g", "sy")

    def __init__(self):
        self.q = {e: [] for e in self.ENGS}
        self.cnt = {}
        self.sems = {}

    def plan(self, sem, k=1):
        self.cnt[sem] = self.cnt.get(sem, 0) + k
        return self.cnt[sem]

    def n(self, sem):
        return self.cnt.get(sem, 0)

    def op(self, eng, meth, *args, inc=None, inck=1, **kw):
        tgt = self.plan(inc, inck) if inc else None

        def run(e, meth=meth, args=args, kw=kw, inc=inc, inck=inck):
            inst = getattr(e, meth)(*args, **kw)
            if inc is not None:
                inst.then_inc(self.sems[inc], inck)
        self.q[eng].append(run)
        return tgt

    def wait(self, eng, sem, val=None):
        if val is None:
            val = self.n(sem)
        if val <= 0:
            self.cnt.setdefault(sem, 0)
            return
        self.cnt.setdefault(sem, 0)

        def run(e, sem=sem, val=val):
            e.wait_ge(self.sems[sem], val)
        self.q[eng].append(run)

    def build(self, nc, stack):
        for s in self.cnt:
            self.sems[s] = stack.enter_context(nc.semaphore(s))
        with nc.Block() as b:
            b.tensor(lambda te: [f(te) for f in self.q["te"]])
            b.vector(lambda v: [f(v) for f in self.q["v"]])
            b.scalar(lambda sc: [f(sc) for f in self.q["sc"]])
            b.gpsimd(lambda g: [f(g) for f in self.q["g"]])
            b.sync(lambda sy: [f(sy) for f in self.q["sy"]])


def emit_ln_pass(r, bufs, src, eps_name, pref, xn_dst=None, gamma_col=None,
                 beta_col=None, fp32_dst=None):
    xf = bufs["lnxf"]
    sq = bufs["wk0"]
    tmp = bufs["wk1"]
    tmp2 = bufs["wk2"]
    row = bufs["row"]
    ones = bufs["ones"]
    psA = bufs["psA"]
    psB = bufs["psB"]
    psC = bufs["psC"]
    sld, smm, ssc, sv, sst = (pref + x for x in ("ld", "mm", "sc", "v", "st"))

    for tt in range(NTT):
        t0 = tt * TT
        r.wait("sy", sv)
        for dc in range(NDC):
            r.op("sy", "dma_start", out=xf[:, dc * TT:(dc + 1) * TT],
                 in_=src[dc, :, t0:t0 + TT], inc=sld, inck=16)
        r.wait("te", sld)
        for dc in range(NDC):
            r.op("te", "matmul", psA[0:1, :], ones[:, 0:1],
                 xf[:, dc * TT:(dc + 1) * TT], start=(dc == 0),
                 stop=(dc == NDC - 1), inc=(smm if dc == NDC - 1 else None))
        r.wait("sc", sld)
        s2 = pref + "s2"
        sqb = (sq, bufs["wk3"])
        for dc in range(NDC):
            r.wait("sc", s2, r.n(s2) - 1)
            r.op("sc", "activation", sqb[dc % 2][:], xf[:, dc * TT:(dc + 1) * TT],
                 AF.Square, inc=ssc)
            r.wait("te", ssc)
            r.op("te", "matmul", psC[0:1, :], ones[:, 0:1], sqb[dc % 2][:],
                 start=(dc == 0), stop=(dc == NDC - 1), inc=s2)
        mean, var, a_r, b_r = (row[:, i * TT:(i + 1) * TT] for i in range(4))
        r.wait("v", smm)
        r.wait("v", pref + "s2")
        r.op("v", "tensor_scalar_mul", mean, psA[0:1, :], 1.0 / D)
        r.op("v", "tensor_scalar_mul", var, psC[0:1, :], 1.0 / D)
        r.op("v", "tensor_tensor", sq[0:1, :], mean, mean, ALU.mult)
        r.op("v", "tensor_tensor", var, var, sq[0:1, :], ALU.subtract, inc=sv)
        r.wait("sc", sv)
        ec = CONST_COLS[eps_name]
        r.op("sc", "activation", a_r, var, AF.Ln,
             bias=bufs["cvec"][0:1, ec:ec + 1])
        r.op("sc", "activation", a_r, a_r, AF.Exp, scale=-0.5, inc=ssc)
        r.wait("v", ssc)
        r.op("v", "tensor_tensor", b_r, mean, a_r, ALU.mult, inc=sv)
        r.wait("te", sv)
        r.op("te", "matmul", psA[:], ones[0:1, :], a_r, start=True, stop=True)
        r.op("te", "matmul", psB[:], ones[0:1, :], b_r, start=True, stop=True,
             inc=smm)
        r.wait("v", smm)
        for dc in range(NDC):
            xsrc = xf[:, dc * TT:(dc + 1) * TT]
            r.op("v", "tensor_tensor", tmp[:], xsrc, psA[:], ALU.mult)
            need_f32 = fp32_dst is not None
            if xn_dst is not None and gamma_col is None and not need_f32:
                r.op("v", "tensor_tensor", xn_dst[:, dc * T + t0: dc * T + t0 + TT],
                     tmp[:], psB[:], ALU.subtract, inc=sv)
            else:
                if need_f32:
                    r.wait("v", sst)
                r.op("v", "tensor_tensor", tmp2[:], tmp[:], psB[:], ALU.subtract,
                     inc=sv)
                if xn_dst is not None:
                    dst = xn_dst[:, dc * T + t0: dc * T + t0 + TT]
                    if gamma_col is not None:
                        r.op("v", "tensor_scalar", dst, tmp2[:],
                             gamma_col[:, dc:dc + 1], beta_col[:, dc:dc + 1],
                             ALU.mult, ALU.add, inc=sv)
                    else:
                        r.op("v", "tensor_copy", dst, tmp2[:], inc=sv)
                if need_f32:
                    r.wait("sc", sv)
                    r.op("sc", "dma_start", out=fp32_dst[dc, :, t0:t0 + TT],
                         in_=tmp2[:], inc=sst, inck=16)
        if fp32_dst is not None:
            r.wait("sy", sst)
    for e in ("te", "v", "sc", "sy", "g"):
        r.wait(e, sv)
        if fp32_dst is not None:
            r.wait(e, sst)


def emit_residual(r, bufs, xin, xout, pref, delta_of, nchunk_free=None):
    raise NotImplementedError


def emit_l0_ffn(r, bufs, io, tag):
    xin, xout = io["xin"], io["xout"]
    wfcT, wprojT, hS = io["wfcT"], io["wprojT"], io["hS"]
    xn = bufs["xn"]
    wA = bufs["wA"]
    hA = bufs["hA"]
    yb, mem, spk, acc = bufs["wk1"], bufs["wk2"], bufs["wk3"], bufs["wk4"]
    xr = bufs["wk5"]
    psA, psB = bufs["psA"], bufs["psB"]
    swl, smm, sev, ssc, shs = (tag + x for x in ("wl", "mm", "ev", "sc", "hs"))
    shl, sxl, sxs = (tag + x for x in ("hl", "xl", "xs"))

    emit_ln_pass(r, bufs, xin, "eps5", tag + "ln", xn_dst=xn)
    if "tap_xn" in io:
        for dc in range(NDC):
            for tt2 in range(NTT):
                r.op("g", "dma_start",
                     out=io["tap_xn"][dc, :, tt2 * TT:(tt2 + 1) * TT],
                     in_=xn[:, dc * T + tt2 * TT: dc * T + (tt2 + 1) * TT],
                     inc=tag + "tapxn", inck=16)
        r.wait("g", tag + "tapxn")
        for e in ("te", "v", "sc", "sy"):
            r.wait(e, tag + "tapxn")

    ybufs = (bufs["wk1"], bufs["wk2"], bufs["wk3"], bufs["wk4"])

    def p2_pre(m, grp):
        ib, mc = m
        if mc == 0 and grp[0] == 0:
            r.wait("g", tag + "gmm")
            for kc in range(NDC):
                r.op("g", "dma_start", out=wA[:, kc * 512:(kc + 1) * 512],
                     in_=wfcT[kc * 128:(kc + 1) * 128, ib * 512:(ib + 1) * 512],
                     inc=swl, inck=16)
            r.wait("te", swl)

    def p2_lhs(m, kc):
        ib, mc = m
        return wA[:, kc * 512 + mc * 128: kc * 512 + (mc + 1) * 128]

    def p2_rhs(kc, tt):
        return xn[:, kc * T + tt * TT: kc * T + tt * TT + TT]

    def p2_evac(m, tt, bank, mmc):
        ib, mc = m
        yb = ybufs[tt % 4]
        slot = (tt % 8)
        hv = hA[:, slot * 512:(slot + 1) * 512]
        r.wait("v", tag + "gmm", mmc)
        r.wait("v", ssc)
        r.op("v", "tensor_scalar_max", yb[:], bank[:], 0.0, inc=tag + "gev")
        r.wait("sc", tag + "gev")
        r.wait("sc", shs)
        r.op("sc", "activation", hv, yb[:], AF.Square, inc=ssc)
        r.wait("sc", ssc)
        r.op("sc", "dma_start", out=hS[ib * 4 + mc, :, tt * TT:(tt + 1) * TT],
             in_=hv, inc=shs, inck=16)

    emit_gemm4(r, bufs, tag, [(ib, mc) for ib in range(8) for mc in range(4)],
               p2_lhs, NDC, p2_rhs, p2_evac, pre_group=p2_pre)
    r.wait("sy", shs)

    for e in ("te", "v", "sc", "g", "sy"):
        r.wait(e, smm)
        r.wait(e, shs)
    wp = bufs["xn"]
    swp = tag + "wp"
    for kc in range(32):
        r.op("g", "dma_start", out=wp[:, kc * 1024:(kc + 1) * 1024],
             in_=wprojT[kc * 128:(kc + 1) * 128, :], inc=swp, inck=16)
    for tt in range(NTT):
        t0 = tt * TT
        r.wait("sy", sev)
        r.wait("sy", smm)
        for kc in range(32):
            r.op("sy", "dma_start", out=hA[:, kc * 512:(kc + 1) * 512],
                 in_=hS[kc, :, t0:t0 + TT], inc=shl, inck=16)
        r.wait("te", swp)
        r.wait("te", shl)
        for dc in range(NDC):
            pt = psA if dc % 2 == 0 else psB
            r.wait("te", sev)
            for kc in range(32):
                r.op("te", "matmul", pt[:],
                     wp[:, kc * 1024 + dc * 128: kc * 1024 + (dc + 1) * 128],
                     hA[:, kc * 512:(kc + 1) * 512],
                     start=(kc == 0), stop=(kc == 31),
                     inc=(smm if kc == 31 else None))
            r.wait("sc", sev)
            r.op("sc", "dma_start", out=xr[:], in_=xin[dc, :, t0:t0 + TT],
                 inc=sxl, inck=16)
            r.wait("v", sxs)
            r.wait("v", smm)
            r.op("v", "tensor_copy", yb[:], pt[:], inc=sev)
            if "tap_y" in io:
                r.wait("sy", sev)
                r.op("sy", "dma_start", out=io["tap_y"][dc, :, t0:t0 + TT],
                     in_=yb[:], inc=tag + "tapy", inck=16)
                r.wait("v", tag + "tapy")
            r.op("v", "tensor_scalar", spk[:], yb[:], 1.0, None, ALU.is_gt)
            r.op("v", "tensor_copy", acc[:], spk[:])
            r.op("v", "tensor_copy", mem[:], yb[:])
            for _ in range(4):
                r.op("v", "scalar_tensor_tensor", mem[:], mem[:], 0.9, yb[:],
                     ALU.mult, ALU.add)
                r.op("v", "tensor_tensor", mem[:], mem[:], spk[:], ALU.subtract)
                r.op("v", "tensor_scalar", spk[:], mem[:], 1.0, None, ALU.is_gt)
                r.op("v", "tensor_tensor", acc[:], acc[:], spk[:], ALU.add)
            r.wait("v", sxl)
            r.op("v", "scalar_tensor_tensor", acc[:], acc[:], 0.2, xr[:],
                 ALU.mult, ALU.add, inc=sev)
            r.wait("sy", sev)
            r.op("sy", "dma_start", out=xout[dc, :, t0:t0 + TT], in_=acc[:],
                 inc=sxs, inck=16)
    for e in ("te", "v", "sc", "g", "sy"):
        r.wait(e, sxs)


def emit_l0_attn_a(r, bufs, io, tag):
    xin, xout = io["xin"], io["xout"]
    qS, kS, vfS, aoS, hS = io["qS"], io["kS"], io["vfS"], io["aoS"], io["hS"]
    inpS, xaS = io["inpS"], io["xaS"]
    xn = bufs["xn"]
    wA2 = bufs["wA2"]
    atn = bufs["atn"]
    idb = bufs["idb"]
    gbc = bufs["gbc"]
    mb1 = bufs["mb1"]
    trb = bufs["trb"]
    onb = bufs["onb"]
    pst = bufs["pst"]
    row = bufs["row"]
    wk1, wk2, xr = bufs["wk1"], bufs["wk2"], bufs["wk5"]
    psA, psB, psC, psE = bufs["psA"], bufs["psB"], bufs["psC"], bufs["psE"]
    psD, psF = bufs["psDb"], bufs["psF"]
    swl, smm, ssc, sv, sst = (tag + x for x in ("wl", "mm", "sc", "v", "st"))
    sld, sxl, sxs, str_ = (tag + x for x in ("ld", "xl", "xs", "tr"))

    emit_ln_pass(r, bufs, xin, "eps5", tag + "lnA", fp32_dst=inpS)
    r.op("g", "dma_start", out=gbc[:], in_=io["aagb"], inc=swl, inck=16)
    r.op("g", "dma_start", out=mb1[:], in_=io["aab1"], inc=swl, inck=16)
    r.op("g", "dma_start", out=trb[:], in_=io["triuC"], inc=swl, inck=16)
    r.op("g", "dma_start", out=onb[:], in_=io["onesbC"], inc=swl, inck=16)
    r.op("g", "dma_start", out=idb[:], in_=io["identb"], inc=swl, inck=16)
    for e in Rec.ENGS:
        r.wait(e, swl)
    emit_ln_pass(r, bufs, inpS, "eps6", tag + "lnB", xn_dst=xn,
                 gamma_col=gbc[:, 0:8], beta_col=gbc[:, 8:16])

    for comp, (wname, dstS) in enumerate((("aawq", qS), ("aawk", kS),
                                          ("aawv", vfS))):
        wR = io[wname]
        r.wait("g", tag + "gmm")
        for kc in range(NDC):
            r.op("g", "dma_start", out=wA2[:, kc * 1024:(kc + 1) * 1024],
                 in_=wR[kc * 128:(kc + 1) * 128, :], inc=swl, inck=16)
        r.wait("te", swl)

        def pr_lhs(m, kc, comp=comp):
            return wA2[:, kc * 1024 + m * 128: kc * 1024 + (m + 1) * 128]

        def pr_rhs(kc, tt):
            return xn[:, kc * T + tt * TT: kc * T + tt * TT + TT]

        def pr_evac(m, tt, bank, mmc, comp=comp, dstS=dstS):
            slot = tt % 8
            stg = atn[:, slot * 512:(slot + 1) * 512]
            r.wait("v", tag + "gmm", mmc)
            r.wait("v", sst)
            if comp < 2:
                r.op("v", "tensor_scalar", stg, bank[:], 0.0, 0.001,
                     ALU.max, ALU.add, inc=tag + "gev")
            else:
                r.op("v", "tensor_copy", stg, bank[:], inc=tag + "gev")
            r.wait("sc", tag + "gev")
            r.op("sc", "dma_start", out=dstS[m, :, tt * TT:(tt + 1) * TT],
                 in_=stg, inc=sst, inck=16)

        emit_gemm4(r, bufs, tag, list(range(NDC)), pr_lhs, NDC, pr_rhs, pr_evac)
        r.wait("sy", sst)

    for e in Rec.ENGS:
        r.wait(e, sst)
        r.wait(e, sv)
        r.wait(e, swl)
    sao = tag + "ao"
    for sq in range(2):
        for dc in range(NDC):
            r.wait("sy", smm)
            r.wait("sy", sv)
            r.op("sy", "dma_start", out=atn[:, 0:S],
                 in_=qS[dc, :, sq * S:(sq + 1) * S], inc=sld, inck=16)
            r.op("sy", "dma_start", out=atn[:, S:2 * S],
                 in_=kS[dc, :, sq * S:(sq + 1) * S], inc=sld, inck=16)
            r.op("sy", "dma_start", out=atn[:, 2 * S:3 * S],
                 in_=vfS[dc, :, sq * S:(sq + 1) * S], inc=sld, inck=16)
            r.wait("v", sld)
            r.op("v", "memset", pst[:], 0.0)
            r.op("v", "memset", atn[:, 6848:6849], 1.0)
            r.op("v", "memset", atn[:, 6913:6914], 1.0, inc=sv)
            r.wait("te", sld)
            r.wait("te", sv)
            for j in range(16):
                jc = j * 128
                r.wait("te", sv)
                r.op("te", "transpose", psD[:, 0:128],
                     atn[:, S + jc:S + jc + 128], idb[:])
                r.op("te", "transpose", psD[:, 128:256],
                     atn[:, 2 * S + jc:2 * S + jc + 128], idb[:], inc=str_)
                r.wait("v", str_)
                r.op("v", "tensor_copy", atn[:, 6656:6784], psD[:, 0:128])
                r.op("v", "tensor_copy", atn[:, 6784:6848], psD[:, 128:192])
                r.op("v", "tensor_copy", atn[:, 6849:6913], psD[:, 192:256],
                     inc=sv)
                ktm = atn[:, 6656:6784]
                for h in range(2):
                    hb = h * 64
                    r.wait("te", sv)
                    r.op("te", "matmul", psA[:, 0:128],
                         atn[hb:hb + 64, S + jc:S + jc + 128],
                         atn[hb:hb + 64, jc:jc + 128],
                         start=True, stop=True, inc=smm)
                    r.wait("v", smm)
                    r.op("v", "tensor_tensor", atn[:, 6944:7072], psA[:, 0:128],
                         trb[:], ALU.mult, inc=sv)
                    r.wait("te", sv)
                    vx0 = 6784 + h * 65
                    r.op("te", "matmul", psB[0:64, 0:128],
                         atn[:, vx0:vx0 + 64], atn[:, 6944:7072],
                         start=True, stop=False)
                    r.op("te", "matmul", psB[0:64, 0:128],
                         pst[hb:hb + 64, 0:64],
                         atn[hb:hb + 64, jc:jc + 128],
                         start=False, stop=True)
                    r.op("te", "matmul", psC[0:1, 0:128],
                         onb[:, 0:1], atn[:, 6944:7072],
                         start=True, stop=False)
                    r.op("te", "matmul", psC[0:1, 0:128],
                         pst[hb:hb + 64, 64:65],
                         atn[hb:hb + 64, jc:jc + 128],
                         start=False, stop=True, inc=smm)
                    psX = psE if h == 0 else psF
                    r.op("te", "matmul", psX[0:64, 0:65],
                         ktm[:, hb:hb + 64], atn[:, vx0:vx0 + 65],
                         start=(j == 0), stop=True, inc=smm)
                    r.wait("v", smm)
                    r.op("v", "tensor_copy", pst[hb:hb + 64, 0:65],
                         psX[0:64, 0:65], inc=sv)
                    r.op("v", "reciprocal", row[:, 0:128], psC[0:1, 0:128],
                         inc=sv)
                    r.wait("te", sv)
                    r.op("te", "matmul", psC[0:64, 0:128],
                         bufs["ones"][0:1, 0:64], row[:, 0:128],
                         start=True, stop=True, inc=smm)
                    r.wait("v", smm)
                    r.op("v", "tensor_copy", wk1[0:64, 0:128], psB[0:64, 0:128])
                    r.wait("v", sao)
                    r.op("v", "tensor_tensor", atn[0:64, 6144:6272],
                         wk1[0:64, 0:128], psC[0:64, 0:128], ALU.mult, inc=sv)
                    r.wait("sc", sv)
                    r.op("sc", "dma_start",
                         out=aoS[dc, hb:hb + 64, sq * S + jc: sq * S + jc + 128],
                         in_=atn[0:64, 6144:6272], inc=sao, inck=16)

    for e in Rec.ENGS:
        r.wait(e, sao)
        r.wait(e, sv)
    for kc in range(NDC):
        r.op("g", "dma_start", out=wA2[:, kc * 1024:(kc + 1) * 1024],
             in_=io["aawo"][kc * 128:(kc + 1) * 128, :], inc=swl, inck=16)
    for tt in range(NTT):
        t0 = tt * TT
        r.wait("sy", smm)
        for kc in range(NDC):
            r.op("sy", "dma_start", out=bufs["hA"][:, kc * 512:(kc + 1) * 512],
                 in_=aoS[kc, :, t0:t0 + TT], inc=sld, inck=16)
        r.wait("te", swl)
        r.wait("te", sld)
        for dci in range(NDC):
            pt = psA if dci % 2 == 0 else psB
            r.wait("te", sv)
            for kc in range(NDC):
                r.op("te", "matmul", pt[:],
                     wA2[:, kc * 1024 + dci * 128: kc * 1024 + (dci + 1) * 128],
                     bufs["hA"][:, kc * 512:(kc + 1) * 512],
                     start=(kc == 0), stop=(kc == NDC - 1),
                     inc=(smm if kc == NDC - 1 else None))
            r.wait("sc", sv)
            r.op("sc", "dma_start", out=xr[:], in_=inpS[dci, :, t0:t0 + TT],
                 inc=sxl, inck=16)
            r.wait("v", sst)
            r.wait("v", smm)
            r.wait("v", sxl)
            r.op("v", "tensor_tensor", wk1[:], pt[:], xr[:], ALU.add, inc=sv)
            r.wait("sc", sv)
            r.op("sc", "dma_start", out=xaS[dci, :, t0:t0 + TT], in_=wk1[:],
                 inc=sst, inck=16)
            r.wait("v", sst)
    for e in Rec.ENGS:
        r.wait(e, sst)
        r.wait(e, sv)

    emit_ln_pass(r, bufs, xaS, "eps6", tag + "lnC", xn_dst=xn,
                 gamma_col=gbc[:, 16:24], beta_col=gbc[:, 24:32])
    wfcT = io["aaw1"]
    hA = bufs["hA"]
    ybufs = (bufs["wk1"], bufs["wk2"], bufs["wk3"], bufs["wk4"])
    lnx = bufs["lnxf"]

    def g_pre(m, grp):
        ib, mc = m
        if mc == 0 and grp[0] == 0:
            r.wait("g", tag + "gmm")
            for kc in range(NDC):
                r.op("g", "dma_start", out=bufs["wA"][:, kc * 512:(kc + 1) * 512],
                     in_=wfcT[kc * 128:(kc + 1) * 128, ib * 512:(ib + 1) * 512],
                     inc=swl, inck=16)
            r.wait("te", swl)

    def g_lhs(m, kc):
        ib, mc = m
        return bufs["wA"][:, kc * 512 + mc * 128: kc * 512 + (mc + 1) * 128]

    def g_rhs(kc, tt):
        return xn[:, kc * T + tt * TT: kc * T + tt * TT + TT]

    def g_evac(m, tt, bank, mmc):
        ib, mc = m
        fi = ib * 4 + mc
        b1c = mb1[:, fi: fi + 1]
        z = ybufs[tt % 4]
        sq2 = lnx[:, (tt % 4) * 512:(tt % 4) * 512 + 512]
        slot = tt % 8
        hv = hA[:, slot * 512:(slot + 1) * 512]
        r.wait("v", tag + "gmm", mmc)
        r.wait("v", ssc)
        r.op("v", "tensor_scalar_add", z[:], bank[:], b1c, inc=sv)
        r.wait("sc", sv)
        r.op("sc", "activation", sq2, bank[:], AF.Square, bias=b1c,
             inc=tag + "gev")
        r.wait("v", tag + "gev")
        r.op("v", "tensor_scalar", sq2, sq2, 0.0356774081, 0.7978845608,
             ALU.mult, ALU.add)
        r.op("v", "tensor_tensor", sq2, sq2, z[:], ALU.mult, inc=sv)
        r.wait("sc", sv)
        r.op("sc", "activation", sq2, sq2, AF.Tanh, inc=ssc)
        r.wait("v", ssc)
        r.wait("v", tag + "hs")
        r.op("v", "tensor_scalar", sq2, sq2, 0.5, 0.5, ALU.mult, ALU.add)
        r.op("v", "tensor_tensor", hv, sq2, z[:], ALU.mult, inc=sv)
        r.wait("sc", sv)
        r.op("sc", "dma_start", out=hS[fi, :, tt * TT:(tt + 1) * TT],
             in_=hv, inc=tag + "hs", inck=16)

    emit_gemm4(r, bufs, tag, [(ib, mc) for ib in range(8) for mc in range(4)],
               g_lhs, NDC, g_rhs, g_evac, pre_group=g_pre)
    r.wait("sy", tag + "hs")

    for e in Rec.ENGS:
        r.wait(e, smm)
        r.wait(e, tag + "hs")
        r.wait(e, sv)
    wp = bufs["xn"]
    swp = tag + "wp"
    for kc in range(32):
        r.op("g", "dma_start", out=wp[:, kc * 1024:(kc + 1) * 1024],
             in_=io["aaw2"][kc * 128:(kc + 1) * 128, :], inc=swp, inck=16)
    for tt in range(NTT):
        t0 = tt * TT
        r.wait("sy", sv)
        r.wait("sy", smm)
        for kc in range(32):
            r.op("sy", "dma_start", out=hA[:, kc * 512:(kc + 1) * 512],
                 in_=hS[kc, :, t0:t0 + TT], inc=sld, inck=16)
        r.wait("te", swp)
        r.wait("te", sld)
        for dci in range(NDC):
            pt = psA if dci % 2 == 0 else psB
            r.wait("te", sv)
            for kc in range(32):
                r.op("te", "matmul", pt[:],
                     wp[:, kc * 1024 + dci * 128: kc * 1024 + (dci + 1) * 128],
                     hA[:, kc * 512:(kc + 1) * 512],
                     start=(kc == 0), stop=(kc == 31),
                     inc=(smm if kc == 31 else None))
            r.wait("sc", sxs)
            r.op("sc", "dma_start", out=xr[:], in_=xin[dci, :, t0:t0 + TT],
                 inc=sxl, inck=16)
            r.op("sc", "dma_start", out=wk2[:], in_=xaS[dci, :, t0:t0 + TT],
                 inc=sxl, inck=16)
            r.wait("v", sxs)
            r.wait("v", smm)
            r.wait("v", sxl)
            r.op("v", "tensor_scalar_add", wk1[:], pt[:],
                 gbc[:, 32 + dci:32 + dci + 1])
            r.op("v", "tensor_tensor", wk1[:], wk1[:], wk2[:], ALU.add)
            r.op("v", "tensor_tensor", wk1[:], wk1[:], xr[:], ALU.add, inc=sv)
            r.wait("sy", sv)
            r.op("sy", "dma_start", out=xout[dci, :, t0:t0 + TT], in_=wk1[:],
                 inc=sxs, inck=16)
    for e in Rec.ENGS:
        r.wait(e, sxs)


ALIBI = [2.0 ** (-8.0 * (i + 1) / 16) for i in range(16)]


def emit_l0_attn_b(r, bufs, io, tag):
    xin, xout = io["xin"], io["xout"]
    wqR, wkR, wvR, woR = io["abWq"], io["abWk"], io["abWv"], io["abWo"]
    qS, kS, vS, aoS = io["qS"], io["kS"], io["vS_tm"], io["aoS"]
    xn = bufs["xn"]
    wA2 = bufs["wA2"]
    atn = bufs["atn"]
    s_sb = bufs["lnxf"][:, 0:S]
    D0 = bufs["lnxf"][:, S:2 * S]
    M4 = bufs["fC"]
    cosT = bufs["fB"][:, 0:TT]
    sinT = bufs["fB"][:, TT:2 * TT]
    idb = bufs["idb"]
    cols = bufs["cols"]
    obc = bufs["obc"]
    wk1, wk2, xr = bufs["wk1"], bufs["wk2"], bufs["wk5"]
    psA, psB, psC = bufs["psA"], bufs["psB"], bufs["psC"]
    psD = bufs["psDb"]
    swl, smm, ssc, sv, sst = (tag + x for x in ("wl", "mm", "sc", "v", "st"))
    sld, sxl, sxs, str_ = (tag + x for x in ("ld", "xl", "xs", "tr"))

    emit_ln_pass(r, bufs, xin, "eps5", tag + "ln", xn_dst=xn)
    r.op("g", "dma_start", out=idb[:], in_=io["identb"], inc=swl, inck=16)
    r.op("g", "dma_start", out=obc[:], in_=io["ab_ob"], inc=swl, inck=16)

    for comp, (wR, dstS) in enumerate(((wqR, qS), (wkR, kS), (wvR, vS))):
        r.wait("g", smm)
        for kc in range(NDC):
            r.op("g", "dma_start", out=wA2[:, kc * 1024:(kc + 1) * 1024],
                 in_=wR[kc * 128:(kc + 1) * 128, :], inc=swl, inck=16)
        r.wait("te", swl)
        for tt in range(NTT):
            t0 = tt * TT
            pos0 = t0 % S
            if comp < 2:
                for pc in range(4):
                    r.wait("sy", sv)
                    r.op("sy", "dma_start", out=cosT,
                         in_=io["ropeC"][pc, :, pos0:pos0 + TT], inc=sld, inck=16)
                    r.op("sy", "dma_start", out=sinT,
                         in_=io["ropeS"][pc, :, pos0:pos0 + TT], inc=sld, inck=16)
                    r.wait("te", sv)
                    for kc in range(NDC):
                        r.op("te", "matmul", psA[:],
                             wA2[:, kc * 1024 + pc * 128: kc * 1024 + (pc + 1) * 128],
                             xn[:, kc * T + t0: kc * T + t0 + TT],
                             start=(kc == 0), stop=(kc == NDC - 1))
                    for kc in range(NDC):
                        r.op("te", "matmul", psB[:],
                             wA2[:, kc * 1024 + (pc + 4) * 128: kc * 1024 + (pc + 5) * 128],
                             xn[:, kc * T + t0: kc * T + t0 + TT],
                             start=(kc == 0), stop=(kc == NDC - 1),
                             inc=(smm if kc == NDC - 1 else None))
                    r.wait("v", smm)
                    r.wait("v", sld)
                    r.wait("v", sst)
                    r.op("v", "tensor_tensor", wk1[:], psA[:], cosT, ALU.mult)
                    r.op("v", "tensor_tensor", wk2[:], psB[:], sinT, ALU.mult)
                    r.op("v", "tensor_tensor", atn[:, 0:TT], wk1[:], wk2[:],
                         ALU.subtract)
                    r.op("v", "tensor_tensor", wk1[:], psA[:], sinT, ALU.mult)
                    r.op("v", "tensor_tensor", wk2[:], psB[:], cosT, ALU.mult)
                    r.op("v", "tensor_tensor", atn[:, TT:2 * TT], wk1[:], wk2[:],
                         ALU.add, inc=sv)
                    r.wait("sc", sv)
                    r.op("sc", "dma_start", out=dstS[pc, :, t0:t0 + TT],
                         in_=atn[:, 0:TT], inc=sst, inck=16)
                    r.op("sc", "dma_start", out=dstS[pc + 4, :, t0:t0 + TT],
                         in_=atn[:, TT:2 * TT], inc=sst, inck=16)
            else:
                for dc in range(NDC):
                    r.wait("te", sv)
                    for kc in range(NDC):
                        r.op("te", "matmul", psA[:],
                             wA2[:, kc * 1024 + dc * 128: kc * 1024 + (dc + 1) * 128],
                             xn[:, kc * T + t0: kc * T + t0 + TT],
                             start=(kc == 0), stop=(kc == NDC - 1),
                             inc=(smm if kc == NDC - 1 else None))
                    r.wait("v", smm)
                    r.wait("v", str_)
                    r.wait("v", sst)
                    r.op("v", "tensor_copy", atn[:, 0:TT], psA[:], inc=sv)
                    r.wait("te", sv)
                    for i in range(4):
                        r.op("te", "transpose",
                             psD[:, i * 128:(i + 1) * 128],
                             atn[:, i * 128:(i + 1) * 128], idb[:], inc=str_)
                    r.wait("v", str_)
                    for i in range(4):
                        r.op("v", "tensor_copy",
                             atn[:, TT + i * 128: TT + (i + 1) * 128],
                             psD[:, i * 128:(i + 1) * 128], inc=sv)
                    r.wait("sc", sv)
                    for i in range(4):
                        r.op("sc", "dma_start",
                             out=vS[dc, t0 + i * 128: t0 + (i + 1) * 128, :],
                             in_=atn[:, TT + i * 128: TT + (i + 1) * 128],
                             inc=sst, inck=16)

    for e in ("te", "v", "sc", "g", "sy"):
        r.wait(e, sst)
        r.wait(e, sv)
        r.wait(e, swl)
    r.op("g", "dma_start", out=D0, in_=io["d0c"], inc=swl, inck=16)
    r.op("g", "dma_start", out=M4[:, 0:S], in_=io["m4c"], inc=swl, inck=16)
    for e in ("te", "v", "sc", "sy"):
        r.wait(e, swl)
    sao = tag + "ao"
    for sq in range(2):
        for dc in range(NDC):
            r.wait("sy", smm)
            r.wait("sy", sv)
            r.op("sy", "dma_start", out=atn[:, 0:S],
                 in_=qS[dc, :, sq * S:(sq + 1) * S], inc=sld, inck=16)
            r.op("sy", "dma_start", out=atn[:, S:2 * S],
                 in_=kS[dc, :, sq * S:(sq + 1) * S], inc=sld, inck=16)
            for kt in range(16):
                r.op("sy", "dma_start",
                     out=atn[:, 2 * S + kt * 128: 2 * S + (kt + 1) * 128],
                     in_=vS[dc, sq * S + kt * 128: sq * S + (kt + 1) * 128, :],
                     inc=sld, inck=16)
            r.wait("te", sld)
            for h in range(2):
                hb = h * 64
                slope8 = 8.0 * ALIBI[2 * dc + h]
                for qb in range(4):
                    for qt in range(4):
                        q0 = qb * 512 + qt * 128
                        for kb in range(qb + 1):
                            r.wait("te", sv)
                            r.op("te", "matmul", psA[:],
                                 atn[hb:hb + 64, q0:q0 + 128],
                                 atn[hb:hb + 64, S + kb * 512: S + (kb + 1) * 512],
                                 start=True, stop=True, inc=smm)
                            r.wait("v", smm)
                            r.op("v", "scalar_tensor_tensor",
                                 s_sb[:, kb * 512:(kb + 1) * 512],
                                 D0[:, kb * 512:(kb + 1) * 512], slope8, psA[:],
                                 ALU.mult, ALU.add, inc=sv)
                        r.op("v", "tensor_tensor",
                             s_sb[:, qb * 512:(qb + 1) * 512],
                             s_sb[:, qb * 512:(qb + 1) * 512],
                             M4[:, qt * 512:(qt + 1) * 512], ALU.add)
                        nk = (qb + 1) * 512
                        r.op("v", "reduce_max", cols[:, 0:1], s_sb[:, 0:nk],
                             AX.X, inc=sv)
                        r.wait("v", sv)
                        r.op("v", "tensor_scalar_mul", cols[:, 1:2],
                             cols[:, 0:1], -0.125, inc=sv)
                        Prow = bufs["hA"][:, qt * S: qt * S + nk]
                        r.wait("sc", sv)
                        r.wait("sc", str_)
                        r.op("sc", "activation", Prow, s_sb[:, 0:nk], AF.Exp,
                             bias=cols[:, 1:2], scale=0.125,
                             accum_out=cols[:, 2:3], inc=ssc)
                        r.wait("v", ssc)
                        r.op("v", "reciprocal", cols[:, 3:4], cols[:, 2:3],
                             inc=sv)
                        r.wait("v", sv)
                        r.op("v", "tensor_scalar_mul", Prow, Prow,
                             cols[:, 3:4], inc=tag + "pn")
                    nkt = (qb + 1) * 4
                    r.wait("te", tag + "pn")
                    for grp0 in range(0, nkt * 4, 8):
                        r.wait("te", sv)
                        for gi in range(grp0, min(grp0 + 8, nkt * 4)):
                            kt, qt = divmod(gi, 4)
                            r.op("te", "transpose",
                                 psD[:, (gi % 8) * 128:(gi % 8 + 1) * 128],
                                 bufs["hA"][:, qt * S + kt * 128: qt * S + (kt + 1) * 128],
                                 idb[:], inc=str_)
                        r.wait("v", str_)
                        for gi in range(grp0, min(grp0 + 8, nkt * 4)):
                            kt, qt = divmod(gi, 4)
                            r.op("v", "tensor_copy",
                                 bufs["hA"][:, 8192 + kt * 512 + qt * 128:
                                            8192 + kt * 512 + (qt + 1) * 128],
                                 psD[:, (gi % 8) * 128:(gi % 8 + 1) * 128],
                                 inc=sv)
                    r.wait("te", sv)
                    for kt in range(nkt):
                        r.op("te", "matmul", psB[0:64, :],
                             atn[:, 2 * S + kt * 128 + hb: 2 * S + kt * 128 + hb + 64],
                             bufs["hA"][:, 8192 + kt * 512: 8192 + (kt + 1) * 512],
                             start=(kt == 0), stop=(kt == nkt - 1),
                             inc=(smm if kt == nkt - 1 else None))
                    r.wait("v", smm)
                    r.wait("v", sao)
                    r.op("v", "tensor_copy", atn[0:64, 6144:6656], psB[0:64, :],
                         inc=sv)
                    r.wait("sc", sv)
                    r.op("sc", "dma_start",
                         out=aoS[dc, hb:hb + 64, sq * S + qb * 512: sq * S + (qb + 1) * 512],
                         in_=atn[0:64, 6144:6656], inc=sao, inck=16)

    for e in ("te", "v", "sc", "g", "sy"):
        r.wait(e, sao)
        r.wait(e, sv)
    for kc in range(NDC):
        r.op("g", "dma_start", out=wA2[:, kc * 1024:(kc + 1) * 1024],
             in_=woR[kc * 128:(kc + 1) * 128, :], inc=swl, inck=16)
    for tt in range(NTT):
        t0 = tt * TT
        r.wait("sy", smm)
        for kc in range(NDC):
            r.op("sy", "dma_start", out=bufs["hA"][:, kc * 512:(kc + 1) * 512],
                 in_=aoS[kc, :, t0:t0 + TT], inc=sld, inck=16)
        r.wait("te", swl)
        r.wait("te", sld)
        for dci in range(NDC):
            pt = psA if dci % 2 == 0 else psB
            r.wait("te", sv)
            for kc in range(NDC):
                r.op("te", "matmul", pt[:],
                     wA2[:, kc * 1024 + dci * 128: kc * 1024 + (dci + 1) * 128],
                     bufs["hA"][:, kc * 512:(kc + 1) * 512],
                     start=(kc == 0), stop=(kc == NDC - 1),
                     inc=(smm if kc == NDC - 1 else None))
            r.wait("sc", sv)
            r.op("sc", "dma_start", out=xr[:], in_=xin[dci, :, t0:t0 + TT],
                 inc=sxl, inck=16)
            r.wait("v", sxs)
            r.wait("v", smm)
            r.wait("v", sxl)
            r.op("v", "tensor_tensor", wk1[:], pt[:], xr[:], ALU.add)
            r.op("v", "tensor_scalar_add", wk1[:], wk1[:], obc[:, dci:dci + 1],
                 inc=sv)
            r.wait("sy", sv)
            r.op("sy", "dma_start", out=xout[dci, :, t0:t0 + TT], in_=wk1[:],
                 inc=sxs, inck=16)
    for e in ("te", "v", "sc", "g", "sy"):
        r.wait(e, sxs)


def emit_l1_attn_a(r, bufs, io, tag):
    xin, xout = io["xin"], io["xout"]
    wqkvT = io["laWqkv"]
    qS, kS, vS, vfS, aoS = io["qS"], io["kS"], io["vS_tm"], io["vfS"], io["aoS"]
    xn = bufs["xn"]
    wA2 = bufs["wA2"]
    atn = bufs["atn"]
    s_sb = bufs["lnxf"][:, 0:S]
    cosT = bufs["lnxf"][:, S:S + TT]
    sinT = bufs["lnxf"][:, S + TT:S + 2 * TT]
    idb = bufs["idb"]
    cols = bufs["cols"]
    wk1, wk2, wk3, xr = bufs["wk1"], bufs["wk2"], bufs["wk3"], bufs["wk5"]
    psA, psB = bufs["psA"], bufs["psB"]
    psD = bufs["psDb"]
    lep = bufs["lep"]
    vle = bufs["fC"]
    swl, smm, ssc, sv, sst = (tag + x for x in ("wl", "mm", "sc", "v", "st"))
    sld, sxl, sxs, str_ = (tag + x for x in ("ld", "xl", "xs", "tr"))

    emit_ln_pass(r, bufs, xin, "eps5", tag + "ln", xn_dst=xn)
    r.op("g", "dma_start", out=idb[:], in_=io["identb"], inc=swl, inck=16)
    r.op("g", "dma_start", out=lep[:], in_=io["la_lepe"], inc=swl, inck=16)
    r.op("g", "dma_start", out=bufs["bq3"][:], in_=io["la_bias"], inc=swl, inck=16)
    bq3 = bufs["bq3"]

    for comp in range(3):
        r.wait("g", smm)
        for kc in range(NDC):
            r.op("g", "dma_start", out=wA2[:, kc * 1024:(kc + 1) * 1024],
                 in_=wqkvT[kc * 128:(kc + 1) * 128,
                           comp * 1024:(comp + 1) * 1024], inc=swl, inck=16)
        r.wait("te", swl)
        for tt in range(NTT):
            t0 = tt * TT
            pos0 = t0 % S
            if comp < 2:
                r.wait("sy", sv)
                r.op("sy", "dma_start", out=cosT,
                     in_=io["ropeC1"][:, pos0:pos0 + TT], inc=sld, inck=16)
                r.op("sy", "dma_start", out=sinT,
                     in_=io["ropeS1"][:, pos0:pos0 + TT], inc=sld, inck=16)
            dstS = (qS, kS, None)[comp]
            if comp < 2:
                for pc in range(4):
                    bE = bq3[:, comp * 8 + pc: comp * 8 + pc + 1]
                    bO = bq3[:, comp * 8 + pc + 4: comp * 8 + pc + 5]
                    r.wait("te", sv)
                    r.wait("te", ssc)
                    for kc in range(NDC):
                        r.op("te", "matmul", psA[:],
                             wA2[:, kc * 1024 + pc * 128: kc * 1024 + (pc + 1) * 128],
                             xn[:, kc * T + t0: kc * T + t0 + TT],
                             start=(kc == 0), stop=(kc == NDC - 1))
                    for kc in range(NDC):
                        r.op("te", "matmul", psB[:],
                             wA2[:, kc * 1024 + (pc + 4) * 128: kc * 1024 + (pc + 5) * 128],
                             xn[:, kc * T + t0: kc * T + t0 + TT],
                             start=(kc == 0), stop=(kc == NDC - 1),
                             inc=(smm if kc == NDC - 1 else None))
                    r.wait("v", smm)
                    r.wait("v", ssc)
                    r.op("v", "tensor_scalar", wk1[:], psA[:], bE, 0.0,
                         ALU.add, ALU.min)
                    r.op("v", "tensor_scalar", wk3[:], psA[:], bE, 0.0,
                         ALU.add, ALU.max)
                    r.op("v", "tensor_scalar", wk2[:], psB[:], bO, 0.0,
                         ALU.add, ALU.min)
                    r.op("v", "tensor_scalar", bufs["wk4"][:], psB[:], bO, 0.0,
                         ALU.add, ALU.max, inc=sv)
                    r.wait("sc", sv)
                    r.op("sc", "activation", wk1[:], wk1[:], AF.Exp)
                    r.op("sc", "activation", wk2[:], wk2[:], AF.Exp, inc=ssc)
                    r.wait("v", ssc)
                    r.wait("v", sld)
                    r.wait("v", sst)
                    r.op("v", "tensor_tensor", wk3[:], wk1[:], wk3[:], ALU.add)
                    r.op("v", "tensor_tensor", bufs["wk4"][:], wk2[:],
                         bufs["wk4"][:], ALU.add)
                    r.op("v", "tensor_tensor", wk1[:], wk3[:], cosT, ALU.mult)
                    r.op("v", "tensor_tensor", wk2[:], bufs["wk4"][:], sinT,
                         ALU.mult)
                    r.op("v", "tensor_tensor", atn[:, 0:TT], wk1[:], wk2[:],
                         ALU.subtract)
                    r.op("v", "tensor_tensor", wk1[:], wk3[:], sinT, ALU.mult)
                    r.op("v", "tensor_tensor", wk2[:], bufs["wk4"][:], cosT,
                         ALU.mult)
                    r.op("v", "tensor_tensor", atn[:, TT:2 * TT], wk1[:],
                         wk2[:], ALU.add, inc=sv)
                    r.wait("sc", sv)
                    for g4 in range(4):
                        h = pc * 4 + g4
                        r.op("sc", "dma_start",
                             out=dstS[h // 2, (h % 2) * 64:(h % 2) * 64 + 32,
                                      t0:t0 + TT],
                             in_=atn[g4 * 32:(g4 + 1) * 32, 0:TT],
                             inc=sst, inck=16)
                        r.op("sc", "dma_start",
                             out=dstS[h // 2, (h % 2) * 64 + 32:(h % 2) * 64 + 64,
                                      t0:t0 + TT],
                             in_=atn[g4 * 32:(g4 + 1) * 32, TT:2 * TT],
                             inc=sst, inck=16)
                continue
            for dc in range(NDC):
                bcol = bq3[:, comp * 8 + dc: comp * 8 + dc + 1]
                r.wait("te", sv)
                r.wait("te", ssc)
                for kc in range(NDC):
                    r.op("te", "matmul", psA[:],
                         wA2[:, kc * 1024 + dc * 128: kc * 1024 + (dc + 1) * 128],
                         xn[:, kc * T + t0: kc * T + t0 + TT],
                         start=(kc == 0), stop=(kc == NDC - 1),
                         inc=(smm if kc == NDC - 1 else None))
                if True:
                    r.wait("v", smm)
                    r.wait("v", str_)
                    r.wait("v", sst)
                    r.op("v", "tensor_scalar_add", atn[:, 0:TT], psA[:], bcol,
                         inc=sv)
                    r.wait("sc", sv)
                    r.op("sc", "dma_start", out=vfS[dc, :, t0:t0 + TT],
                         in_=atn[:, 0:TT], inc=sst, inck=16)
                    r.wait("te", sv)
                    for i in range(4):
                        r.op("te", "transpose", psD[:, i * 128:(i + 1) * 128],
                             atn[:, i * 128:(i + 1) * 128], idb[:], inc=str_)
                    r.wait("v", str_)
                    for i in range(4):
                        r.op("v", "tensor_copy",
                             atn[:, TT + i * 128: TT + (i + 1) * 128],
                             psD[:, i * 128:(i + 1) * 128], inc=sv)
                    r.wait("sc", sv)
                    for i in range(4):
                        r.op("sc", "dma_start",
                             out=vS[dc, t0 + i * 128: t0 + (i + 1) * 128, :],
                             in_=atn[:, TT + i * 128: TT + (i + 1) * 128],
                             inc=sst, inck=16)

    for e in ("te", "v", "sc", "g", "sy"):
        r.wait(e, sst)
        r.wait(e, sv)
        r.wait(e, swl)
    sao = tag + "ao"
    for sq in range(2):
        for dc in range(NDC):
            r.wait("sy", smm)
            r.wait("sy", sv)
            r.op("sy", "dma_start", out=atn[:, 0:S],
                 in_=qS[dc, :, sq * S:(sq + 1) * S], inc=sld, inck=16)
            r.op("sy", "dma_start", out=atn[:, S:2 * S],
                 in_=kS[dc, :, sq * S:(sq + 1) * S], inc=sld, inck=16)
            for kt in range(16):
                r.op("sy", "dma_start",
                     out=atn[:, 2 * S + kt * 128: 2 * S + (kt + 1) * 128],
                     in_=vS[dc, sq * S + kt * 128: sq * S + (kt + 1) * 128, :],
                     inc=sld, inck=16)
            r.wait("te", sld)
            for h in range(2):
                hb = h * 64
                for qb in range(4):
                    for qt in range(4):
                        q0 = qb * 512 + qt * 128
                        for kb in range(4):
                            r.wait("te", sv)
                            r.op("te", "matmul", psA[:],
                                 atn[hb:hb + 64, q0:q0 + 128],
                                 atn[hb:hb + 64, S + kb * 512: S + (kb + 1) * 512],
                                 start=True, stop=True, inc=smm)
                            r.wait("v", smm)
                            r.op("v", "tensor_copy",
                                 s_sb[:, kb * 512:(kb + 1) * 512], psA[:],
                                 inc=sv)
                        r.op("v", "reduce_max", cols[:, 0:1], s_sb[:, 0:S],
                             AX.X, inc=sv)
                        r.wait("v", sv)
                        r.op("v", "tensor_scalar_mul", cols[:, 1:2],
                             cols[:, 0:1], -0.125, inc=sv)
                        Prow = bufs["hA"][:, qt * S: (qt + 1) * S]
                        r.wait("sc", sv)
                        r.wait("sc", str_)
                        r.op("sc", "activation", Prow, s_sb[:, 0:S], AF.Exp,
                             bias=cols[:, 1:2], scale=0.125,
                             accum_out=cols[:, 2:3], inc=ssc)
                        r.wait("v", ssc)
                        r.op("v", "reciprocal", cols[:, 3:4], cols[:, 2:3],
                             inc=sv)
                        r.wait("v", sv)
                        r.op("v", "tensor_scalar_mul", Prow, Prow,
                             cols[:, 3:4], inc=tag + "pn")
                    r.wait("te", tag + "pn")
                    for grp0 in range(0, 64, 8):
                        r.wait("te", sv)
                        for gi in range(grp0, grp0 + 8):
                            kt, qt = divmod(gi, 4)
                            r.op("te", "transpose",
                                 psD[:, (gi % 8) * 128:(gi % 8 + 1) * 128],
                                 bufs["hA"][:, qt * S + kt * 128: qt * S + (kt + 1) * 128],
                                 idb[:], inc=str_)
                        r.wait("v", str_)
                        for gi in range(grp0, grp0 + 8):
                            kt, qt = divmod(gi, 4)
                            r.op("v", "tensor_copy",
                                 bufs["hA"][:, 8192 + kt * 512 + qt * 128:
                                            8192 + kt * 512 + (qt + 1) * 128],
                                 psD[:, (gi % 8) * 128:(gi % 8 + 1) * 128],
                                 inc=sv)
                    r.wait("te", sv)
                    for kt in range(16):
                        r.op("te", "matmul", psB[0:64, :],
                             atn[:, 2 * S + kt * 128 + hb: 2 * S + kt * 128 + hb + 64],
                             bufs["hA"][:, 8192 + kt * 512: 8192 + (kt + 1) * 512],
                             start=(kt == 0), stop=(kt == 15),
                             inc=(smm if kt == 15 else None))
                    r.wait("v", smm)
                    r.wait("v", sao)
                    r.op("v", "tensor_copy", atn[0:64, 6144:6656], psB[0:64, :],
                         inc=sv)
                    r.wait("sc", sv)
                    r.op("sc", "dma_start",
                         out=aoS[dc, hb:hb + 64, sq * S + qb * 512: sq * S + (qb + 1) * 512],
                         in_=atn[0:64, 6144:6656], inc=sao, inck=16)

    for e in ("te", "v", "sc", "g", "sy"):
        r.wait(e, sao)
        r.wait(e, sv)
    for tt in range(NTT):
        t0 = tt * TT
        sq0 = (t0 % S == 0)
        sq1 = ((t0 + TT) % S == 0)
        for dc in range(NDC):
            r.wait("sy", sv)
            r.wait("g", sv)
            lo = t0 if sq0 else t0 - 1
            hi = t0 + TT if sq1 else t0 + TT + 1
            off = 1 if sq0 else 0
            r.op("g", "dma_start", out=vle[:, off:off + (hi - lo)],
                 in_=vfS[dc, :, lo:hi], inc=sld, inck=16)
            r.op("sy", "dma_start", out=wk2[:], in_=xin[dc, :, t0:t0 + TT],
                 inc=sld, inck=16)
            r.op("sy", "dma_start", out=atn[:, 0:TT],
                 in_=aoS[dc, :, t0:t0 + TT], inc=sld, inck=16)
            r.wait("v", sld)
            if sq0 or sq1:
                if sq0:
                    r.op("v", "memset", vle[:, 0:1], 0.0)
                if sq1:
                    r.op("v", "memset", vle[:, TT + 1:TT + 2], 0.0, inc=sv)
                else:
                    r.op("v", "memset", vle[:, 0:1], 0.0, inc=sv)
                r.wait("v", sv)
            r.op("v", "tensor_scalar", wk1[:], vle[:, 1:TT + 1],
                 lep[:, 8 + dc:8 + dc + 1], lep[:, 24 + dc:24 + dc + 1],
                 ALU.mult, ALU.add)
            r.op("v", "scalar_tensor_tensor", wk1[:], vle[:, 0:TT],
                 lep[:, dc:dc + 1], wk1[:], ALU.mult, ALU.add)
            r.op("v", "scalar_tensor_tensor", wk1[:], vle[:, 2:TT + 2],
                 lep[:, 16 + dc:16 + dc + 1], wk1[:], ALU.mult, ALU.add)
            r.op("v", "tensor_tensor", wk1[:], wk1[:], atn[:, 0:TT], ALU.add)
            r.op("v", "tensor_tensor", wk1[:], wk1[:], wk2[:], ALU.add, inc=sv)
            r.wait("sy", sv)
            r.op("sy", "dma_start", out=xout[dc, :, t0:t0 + TT], in_=wk1[:],
                 inc=sxs, inck=16)
            r.wait("v", sxs)
    for e in ("te", "v", "sc", "g", "sy"):
        r.wait(e, sxs)


def emit_l1_attn_b(r, bufs, io, tag):
    xin, xout = io["xin"], io["xout"]
    WiT, WfT, WgT, WoT = io["hgWiT"], io["hgWfT"], io["hgWgT"], io["hgWoT"]
    hSf, sgS = io["hSf"], io["sgS"]
    nwcol = bufs["hg_nw"]
    xn = bufs["xn"]
    wB = bufs["wB"]
    wA = bufs["wA2"]
    fA, fB, fC = bufs["fA"], bufs["fB"], bufs["fC"]
    sil = bufs["wk0"]
    tmp = bufs["wk1"]
    sq = bufs["wk2"]
    sq2 = bufs["wk3"]
    row = bufs["row"]
    ones = bufs["ones"]
    xr = bufs["wk5"]
    psA, psB, psC = bufs["psA"], bufs["psB"], bufs["psC"]
    swl, smm, ssc, sv, sst = (tag + x for x in ("wl", "mm", "sc", "v", "st"))
    shl, sxl, sxs, s2 = (tag + x for x in ("hl", "xl", "xs", "s2"))

    emit_ln_pass(r, bufs, xin, "eps5", tag + "ln", xn_dst=xn)
    r.op("g", "dma_start", out=nwcol[:], in_=io["hg_nw_d"], inc=swl, inck=16)
    for kc in range(NDC):
        r.op("g", "dma_start", out=wA[:, kc * 1024:(kc + 1) * 1024],
             in_=WoT[kc * 128:(kc + 1) * 128, :], inc=swl, inck=16)

    for dc in range(NDC):
        r.wait("g", smm)
        for wi, WT in enumerate((WiT, WfT, WgT)):
            for kc in range(NDC):
                r.op("g", "dma_start",
                     out=wB[:, wi * 1024 + kc * 128: wi * 1024 + (kc + 1) * 128],
                     in_=WT[kc * 128:(kc + 1) * 128, dc * 128:(dc + 1) * 128],
                     inc=swl, inck=16)
        r.wait("te", swl)
        for sqi in range(2):
            for ttl in range(4):
                t0 = sqi * S + ttl * TT
                ts_loc = ttl * TT
                r.wait("te", sv)
                r.wait("te", ssc)
                for wi, pt in ((0, psA), (1, psB), (2, psC)):
                    for kc in range(NDC):
                        r.op("te", "matmul", pt[:],
                             wB[:, wi * 1024 + kc * 128: wi * 1024 + (kc + 1) * 128],
                             xn[:, kc * T + t0: kc * T + t0 + TT],
                             start=(kc == 0), stop=(kc == NDC - 1),
                             inc=(smm if kc == NDC - 1 else None))
                r.wait("sc", smm)
                r.wait("sc", sv)
                r.op("sc", "activation", fA[:, ts_loc:ts_loc + TT], psB[:],
                     AF.Sigmoid, inc=ssc)
                r.op("sc", "activation", sil[:], psA[:], AF.Silu, inc=ssc)
                r.wait("sc", tag + "sg")
                r.op("sc", "activation", tmp[:], psC[:], AF.Silu, inc=ssc)
                r.wait("sc", ssc)
                r.op("sc", "dma_start", out=sgS[dc, :, t0:t0 + TT], in_=tmp[:],
                     inc=tag + "sg", inck=16)
                r.wait("v", ssc)
                r.op("v", "tensor_scalar", sq[:], fA[:, ts_loc:ts_loc + TT],
                     -1.0, 1.0, ALU.mult, ALU.add)
                r.op("v", "tensor_tensor", fB[:, ts_loc:ts_loc + TT], sil[:],
                     sq[:], ALU.mult, inc=sv)
            r.wait("v", smm)
            r.op("v", "tensor_tensor_scan", fC[:], fA[:], fB[:], 0.0,
                 ALU.mult, ALU.add, inc=sv)
            r.wait("sy", sv)
            r.op("sy", "dma_start", out=hSf[dc, :, sqi * S:(sqi + 1) * S],
                 in_=fC[:], inc=sst, inck=16)
            r.wait("v", sst)

    for e in ("te", "v", "sc", "g", "sy"):
        r.wait(e, sst)
        r.wait(e, tag + "sg")
        r.wait(e, swl)
        r.wait(e, sv)
    for tt in range(NTT):
        t0 = tt * TT
        r.wait("sy", sv)
        for dc in range(NDC):
            r.op("sy", "dma_start", out=bufs["lnxf"][:, dc * TT:(dc + 1) * TT],
                 in_=hSf[dc, :, t0:t0 + TT], inc=shl, inck=16)
        r.wait("sc", shl)
        sqb = (sq, sq2)
        for dc in range(NDC):
            r.wait("sc", s2, r.n(s2) - 1)
            r.op("sc", "activation", sqb[dc % 2][:],
                 bufs["lnxf"][:, dc * TT:(dc + 1) * TT], AF.Square, inc=ssc)
            r.wait("te", ssc)
            r.op("te", "matmul", psC[0:1, :], ones[:, 0:1], sqb[dc % 2][:],
                 start=(dc == 0), stop=(dc == NDC - 1), inc=s2)
        mrow = row[:, 0:TT]
        r.wait("v", s2)
        r.op("v", "tensor_scalar_mul", mrow, psC[0:1, :], 1.0 / D, inc=sv)
        r.wait("sc", sv)
        r.op("sc", "activation", mrow, mrow, AF.Ln,
             bias=bufs["cvec"][0:1, CONST_COLS["eps_hgrn"]:CONST_COLS["eps_hgrn"] + 1])
        r.op("sc", "activation", mrow, mrow, AF.Exp, scale=-0.5, inc=ssc)
        r.wait("te", ssc)
        r.op("te", "matmul", psA[:], ones[0:1, :], mrow, start=True, stop=True,
             inc=smm)
        r.wait("v", smm)
        for dc in range(NDC):
            r.wait("sc", sv)
            r.op("sc", "dma_start", out=sil[:], in_=sgS[dc, :, t0:t0 + TT],
                 inc=sxl, inck=16)
            r.op("v", "tensor_tensor", tmp[:],
                 bufs["lnxf"][:, dc * TT:(dc + 1) * TT], psA[:], ALU.mult)
            r.op("v", "tensor_scalar_mul", tmp[:], tmp[:], nwcol[:, dc:dc + 1])
            r.wait("v", sxl)
            r.op("v", "tensor_tensor", xn[:, dc * T + t0: dc * T + t0 + TT],
                 tmp[:], sil[:], ALU.mult, inc=sv)

    for e in ("te", "v", "sc", "g", "sy"):
        r.wait(e, sv)
    for tt in range(NTT):
        t0 = tt * TT
        for dc in range(NDC):
            pt = psA if dc % 2 == 0 else psB
            r.wait("te", sv)
            for kc in range(NDC):
                r.op("te", "matmul", pt[:],
                     wA[:, kc * 1024 + dc * 128: kc * 1024 + (dc + 1) * 128],
                     xn[:, kc * T + t0: kc * T + t0 + TT],
                     start=(kc == 0), stop=(kc == NDC - 1),
                     inc=(smm if kc == NDC - 1 else None))
            r.wait("sc", sv)
            r.op("sc", "dma_start", out=xr[:], in_=xin[dc, :, t0:t0 + TT],
                 inc=sxl, inck=16)
            r.wait("v", sxs)
            r.wait("v", smm)
            r.wait("v", sxl)
            r.op("v", "tensor_tensor", tmp[:], pt[:], xr[:], ALU.add, inc=sv)
            r.wait("sy", sv)
            r.op("sy", "dma_start", out=xout[dc, :, t0:t0 + TT], in_=tmp[:],
                 inc=sxs, inck=16)
    for e in ("te", "v", "sc", "g", "sy"):
        r.wait(e, sxs)



def emit_gemm4(r, bufs, pref, m_list, lhsT_of, nK, rhs_of, evac, tts=None,
               pre_group=None):
    sev = pref + "gev"
    smm = pref + "gmm"
    sets = ([bufs["psA"], bufs["psB"], bufs["psC"], bufs["psD"]],
            [bufs["psE"], bufs["psF"], bufs["psG"], bufs["psH"]])
    tts = list(range(NTT)) if tts is None else tts
    hist = []
    gi = 0
    for m in m_list:
        for g0 in range(0, len(tts), 4):
            grp = tts[g0:g0 + 4]
            bs = sets[gi % 2]
            if pre_group is not None:
                pre_group(m, grp)
            if gi >= 2:
                r.wait("te", sev, hist[gi - 2])
            for kc in range(nK):
                L = lhsT_of(m, kc)
                for bi, tt in enumerate(grp):
                    r.op("te", "matmul", bs[bi][:], L, rhs_of(kc, tt),
                         start=(kc == 0), stop=(kc == nK - 1),
                         inc=(smm if (kc == nK - 1 and bi == len(grp) - 1)
                              else None))
            mmc = r.n(smm)
            for bi, tt in enumerate(grp):
                evac(m, tt, bs[bi], mmc)
            hist.append(r.n(sev))
            gi += 1
    for e in Rec.ENGS:
        r.wait(e, sev)
        r.wait(e, smm)


def emit_expert_cast(r, bufs, io):
    evS = io["evS"]
    hA = bufs["hA"]
    sld, sst = "ecld", "ecst"
    for e in range(64):
        groups = (
            [(io["moeEG"][e, kc * 128:(kc + 1) * 128, :],
              evS[e, :, kc * 512:(kc + 1) * 512], kc * 512, 512)
             for kc in range(NDC)] +
            [(io["moeEU"][e, kc * 128:(kc + 1) * 128, :],
              evS[e, :, 4096 + kc * 512: 4096 + (kc + 1) * 512],
              4096 + kc * 512, 512) for kc in range(NDC)] +
            [(io["moeED"][e, kc * 128:(kc + 1) * 128, :],
              evS[e, :, 8192 + kc * 1024: 8192 + (kc + 1) * 1024],
              8192 + kc * 1024, 1024) for kc in range(4)])
        r.wait("g", sst)
        for (srcap, dstap, off, wdt) in groups:
            r.op("g", "dma_start", out=hA[:, off:off + wdt], in_=srcap,
                 inc=sld, inck=16)
        r.wait("sc", sld)
        for (srcap, dstap, off, wdt) in groups:
            r.op("sc", "dma_start", out=dstap, in_=hA[:, off:off + wdt],
                 inc=sst, inck=16)
    for e in ("g", "sc", "sy", "v", "te"):
        r.wait(e, sst)


def emit_l1_ffn(r, bufs, io, tag):
    xin, xout = io["xin"], io["xout"]
    evS, combS, routS = io["evS"], io["combS"], io["routS"]
    shS, hS = io["xaS"], io["hS"]
    xn = bufs["xn"]
    wA2, wA, wB = bufs["wA2"], bufs["wA"], bufs["wB"]
    hA = bufs["hA"]
    atn = bufs["atn"]
    idf = bufs["idf"]
    cols = bufs["cols"]
    wk0, wk1, wk2, wk3, wk4, xr = (bufs[f"wk{i}"] for i in range(6))
    row = bufs["row"]
    zb16 = bufs["zb16"]
    gmx = bufs["gmx"]
    psA, psB, psC = bufs["psA"], bufs["psB"], bufs["psC"]
    psD = bufs["psD"]
    dnb = (bufs["psE"], bufs["psF"], bufs["psG"], bufs["psH"])
    swl, smm, ssc, sv, sst = (tag + x for x in ("wl", "mm", "sc", "v", "st"))
    sld, sxl, sxs, str_ = (tag + x for x in ("ld", "xl", "xs", "tr"))
    scb, sfl = tag + "cb", tag + "fl"

    emit_ln_pass(r, bufs, xin, "eps5", tag + "ln", xn_dst=xn)
    r.op("v", "memset", zb16[:], 0.0, inc=sv)
    r.op("g", "dma_start", out=idf[:], in_=io["identf"], inc=swl, inck=16)
    r.op("g", "dma_start", out=bufs["gbb"][:], in_=io["moeGB"], inc=swl, inck=16)
    for kc in range(NDC):
        r.op("g", "dma_start", out=wB[:, kc * 64:(kc + 1) * 64],
             in_=io["moeGT"][kc * 128:(kc + 1) * 128, :], inc=swl, inck=16)
    for e in Rec.ENGS:
        r.wait(e, swl)
        r.wait(e, sv)

    r.op("v", "memset", wk1[64:128, :], 0.0)
    r.op("v", "memset", wk4[:, 64:128], 0.0, inc=sv)
    for tt in range(NTT):
        t0 = tt * TT
        r.wait("te", sv)
        r.wait("te", ssc)
        for kc in range(NDC):
            r.op("te", "matmul", psA[0:64, :], wB[:, kc * 64:(kc + 1) * 64],
                 xn[:, kc * T + t0: kc * T + t0 + TT],
                 start=(kc == 0), stop=(kc == NDC - 1),
                 inc=(smm if kc == NDC - 1 else None))
        r.wait("sc", smm)
        r.wait("sc", sv)
        r.wait("sc", str_)
        r.op("sc", "activation", wk1[0:64, :], psA[0:64, :], AF.Sigmoid,
             inc=ssc)
        r.wait("v", ssc)
        r.op("v", "tensor_scalar_add", wk1[0:64, :], wk1[0:64, :],
             bufs["gbb"][0:64, 0:1], inc=sv)
        for i4 in range(4):
            tloc = t0 + i4 * 128
            r.wait("te", sv)
            r.op("te", "transpose", psB[:, 0:128],
                 wk1[:, i4 * 128:(i4 + 1) * 128], idf[:], inc=str_)
            r.wait("v", str_)
            r.wait("v", scb)
            stok, ms, comb = wk2[:, 0:64], wk3[:, 0:64], wk4[:, 0:64]
            eq64 = wk0[:, 0:64]
            gmax, gw, gsel, eq8 = (gmx[:, i * 8:(i + 1) * 8] for i in range(4))

            def vs(meth, *a, **kw):
                r.op("v", meth, *a, inc=sv, **kw)
                r.wait("v", sv)

            vs("tensor_copy", stok, psB[:, 0:64])
            s3 = bass.AP(stok.tensor, stok.offset,
                         [stok.ap[0], [8, 8], [1, 8]])
            vs("reduce_max", gmax, s3, AX.X)
            vs("tensor_copy", gw, gmax)
            vs("memset", gsel, 0.0)
            for _ in range(4):
                vs("reduce_max", cols[:, 0:1], gw, AX.X)
                vs("tensor_scalar", eq8, gw, cols[:, 0:1], None, ALU.is_ge)
                vs("tensor_tensor", gsel, gsel, eq8, ALU.add)
                vs("scalar_tensor_tensor", gw, eq8, -1e30, gw,
                   ALU.mult, ALU.add)
            for g8 in range(8):
                vs("tensor_scalar_mul", ms[:, g8 * 8:(g8 + 1) * 8],
                   stok[:, g8 * 8:(g8 + 1) * 8], gsel[:, g8:g8 + 1])
            vs("memset", comb, 0.0)
            vs("memset", cols[:, 4:5], 0.0)
            for _ in range(6):
                vs("reduce_max", cols[:, 0:1], ms, AX.X)
                vs("tensor_scalar_max", cols[:, 1:2], cols[:, 0:1], 0.0)
                vs("tensor_tensor", cols[:, 4:5], cols[:, 4:5],
                   cols[:, 1:2], ALU.add)
                vs("tensor_scalar", eq64, ms, cols[:, 0:1], None, ALU.is_ge)
                vs("scalar_tensor_tensor", comb, eq64, cols[:, 1:2],
                   comb, ALU.mult, ALU.add)
                vs("scalar_tensor_tensor", ms, eq64, -1e30, ms,
                   ALU.mult, ALU.add)
            vs("tensor_scalar_add", cols[:, 5:6], cols[:, 4:5], 1e-8)
            vs("reciprocal", cols[:, 6:7], cols[:, 5:6])
            vs("tensor_scalar_mul", comb, comb, cols[:, 6:7])
            r.wait("te", sv)
            r.op("te", "transpose", psB[:, 128:256], wk4[:, 0:128], idf[:],
                 inc=str_)
            r.wait("v", str_)
            r.op("v", "tensor_copy", wk0[0:64, 0:128], psB[0:64, 128:256],
                 inc=sv)
            r.wait("sc", sv)
            r.op("sc", "dma_start", out=combS[0:64, tloc:tloc + 128],
                 in_=wk0[0:64, 0:128], inc=scb, inck=16)

    for e in Rec.ENGS:
        r.wait(e, sv)
        r.wait(e, scb)
    for kc in range(NDC):
        r.op("g", "dma_start", out=wA2[:, kc * 1024:(kc + 1) * 1024],
             in_=io["moeSG"][kc * 128:(kc + 1) * 128, :], inc=swl, inck=16)
        r.op("g", "dma_start", out=hA[:, 8192 + kc * 1024: 8192 + (kc + 1) * 1024],
             in_=io["moeSU"][kc * 128:(kc + 1) * 128, :], inc=swl, inck=16)
    r.wait("te", swl)
    for tt in range(NTT):
        t0 = tt * TT
        for ic in range(NDC):
            r.wait("te", sv)
            r.wait("te", ssc)
            for kc in range(NDC):
                r.op("te", "matmul", psA[:],
                     wA2[:, kc * 1024 + ic * 128: kc * 1024 + (ic + 1) * 128],
                     xn[:, kc * T + t0: kc * T + t0 + TT],
                     start=(kc == 0), stop=(kc == NDC - 1))
            for kc in range(NDC):
                r.op("te", "matmul", psB[:],
                     hA[:, 8192 + kc * 1024 + ic * 128: 8192 + kc * 1024 + (ic + 1) * 128],
                     xn[:, kc * T + t0: kc * T + t0 + TT],
                     start=(kc == 0), stop=(kc == NDC - 1),
                     inc=(smm if kc == NDC - 1 else None))
            r.wait("sc", smm)
            r.wait("sc", sv)
            r.op("sc", "activation", wk1[:], psA[:], AF.Silu, inc=ssc)
            r.wait("v", ssc)
            r.wait("v", sst)
            r.op("v", "tensor_tensor", atn[:, 0:TT], wk1[:], psB[:], ALU.mult,
                 inc=sv)
            r.wait("sc", sv)
            r.op("sc", "dma_start", out=hS[ic, :, t0:t0 + TT],
                 in_=atn[:, 0:TT], inc=sst, inck=16)
    for e in Rec.ENGS:
        r.wait(e, sst)
        r.wait(e, sv)
    for kc in range(NDC):
        r.op("g", "dma_start", out=wA2[:, kc * 1024:(kc + 1) * 1024],
             in_=io["moeSD"][kc * 128:(kc + 1) * 128, :], inc=swl, inck=16)
    for tt in range(NTT):
        t0 = tt * TT
        r.wait("sy", smm)
        r.wait("sy", sv)
        for kc in range(NDC):
            r.op("sy", "dma_start", out=hA[:, kc * 512:(kc + 1) * 512],
                 in_=hS[kc, :, t0:t0 + TT], inc=sld, inck=16)
        r.wait("te", swl)
        r.wait("te", sld)
        for dci in range(NDC):
            pt = psA if dci % 2 == 0 else psB
            r.wait("te", sv)
            for kc in range(NDC):
                r.op("te", "matmul", pt[:],
                     wA2[:, kc * 1024 + dci * 128: kc * 1024 + (dci + 1) * 128],
                     hA[:, kc * 512:(kc + 1) * 512],
                     start=(kc == 0), stop=(kc == NDC - 1),
                     inc=(smm if kc == NDC - 1 else None))
            r.wait("v", smm)
            r.wait("v", sst)
            r.op("v", "tensor_copy", wk1[:], pt[:], inc=sv)
            r.wait("sc", sv)
            r.op("sc", "dma_start", out=shS[dci, :, t0:t0 + TT], in_=wk1[:],
                 inc=sst, inck=16)
            r.wait("v", sst)

    for e in Rec.ENGS:
        r.wait(e, sst)
        r.wait(e, sv)
        r.wait(e, smm)
    dn = (bufs["psA"], bufs["psB"], bufs["psC"], bufs["psD"],
          bufs["psE"], bufs["psF"], bufs["psG"], bufs["psH"])
    psG, psH = bufs["psG"], bufs["psH"]
    cbuf = bufs["cbuf"]
    fstg = bufs["lnxf"]
    wa2_free = r.n(smm)
    cb_prev = r.n(smm)
    for tp in range(4):
        tA = 2 * tp * TT
        tB = tA + TT
        for g4 in range(16):
            for el in range(4):
                e = g4 * 4 + el
                r.wait("g", smm, wa2_free)
                r.op("g", "dma_start", out=wA2[:], in_=evS[e, :, 0:8192],
                     inc=swl, inck=16)
                r.wait("sy", smm, cb_prev)
                r.op("sy", "dma_start", out=row[:, 0:TT],
                     in_=combS[e, tA:tA + TT], inc=sld, inck=16)
                r.op("sy", "dma_start", out=row[:, TT:2 * TT],
                     in_=combS[e, tB:tB + TT], inc=sld, inck=16)
                r.wait("te", sld)
                r.wait("te", sv)
                r.op("te", "matmul", psG[:], bufs["ones"][0:1, :],
                     row[:, 0:TT], start=True, stop=True)
                r.op("te", "matmul", psH[:], bufs["ones"][0:1, :],
                     row[:, TT:2 * TT], start=True, stop=True, inc=smm)
                cb_mms = cb_prev = r.n(smm)
                r.wait("v", smm, cb_mms)
                r.op("v", "tensor_copy", cbuf[:, 0:TT], psG[:])
                r.op("v", "tensor_copy", cbuf[:, TT:2 * TT], psH[:], inc=sv)
                cbuf_ready = r.n(sv)
                r.wait("te", swl)
                silu_done = r.n(ssc)
                u_cons = r.n(sv)
                for ic in range(4):
                    r.wait("te", ssc, silu_done)
                    r.wait("te", sv, u_cons)
                    for kc in range(NDC):
                        L = wA2[:, kc * 512 + ic * 128: kc * 512 + (ic + 1) * 128]
                        r.op("te", "matmul", dn[0][:], L,
                             xn[:, kc * T + tA: kc * T + tA + TT],
                             start=(kc == 0), stop=(kc == NDC - 1))
                        r.op("te", "matmul", dn[1][:], L,
                             xn[:, kc * T + tB: kc * T + tB + TT],
                             start=(kc == 0), stop=(kc == NDC - 1),
                             inc=(smm if kc == NDC - 1 else None))
                    g_done = r.n(smm)
                    for kc in range(NDC):
                        L = wA2[:, 4096 + kc * 512 + ic * 128: 4096 + kc * 512 + (ic + 1) * 128]
                        r.op("te", "matmul", dn[2][:], L,
                             xn[:, kc * T + tA: kc * T + tA + TT],
                             start=(kc == 0), stop=(kc == NDC - 1))
                        r.op("te", "matmul", dn[4][:], L,
                             xn[:, kc * T + tB: kc * T + tB + TT],
                             start=(kc == 0), stop=(kc == NDC - 1),
                             inc=(smm if kc == NDC - 1 else None))
                    u_done = r.n(smm)
                    r.wait("sc", smm, g_done)
                    r.wait("sc", sv)
                    r.op("sc", "activation", wk1[:], dn[0][:], AF.Silu)
                    r.op("sc", "activation", wk2[:], dn[1][:], AF.Silu,
                         inc=ssc)
                    silu_done = r.n(ssc)
                    r.wait("v", ssc, silu_done)
                    r.wait("v", smm, u_done)
                    r.wait("v", sv, cbuf_ready)
                    hb0 = el * 4096 + ic * 512
                    r.op("v", "tensor_tensor", wk3[:], wk1[:], dn[2][:],
                         ALU.mult)
                    r.op("v", "tensor_tensor", hA[:, hb0:hb0 + TT], wk3[:],
                         cbuf[:, 0:TT], ALU.mult)
                    r.op("v", "tensor_tensor", wk4[:], wk2[:], dn[4][:],
                         ALU.mult)
                    r.op("v", "tensor_tensor", hA[:, hb0 + 2048:hb0 + 2048 + TT],
                         wk4[:], cbuf[:, TT:2 * TT], ALU.mult, inc=sv)
                    u_cons = r.n(sv)
                wa2_free = r.n(smm)
            wa_free = r.n(smm)
            for tx, tbase in ((0, tA), (2048, tB)):
                r.wait("te", sv)
                for hc in range(NDC):
                    r.op("te", "matmul", dn[hc][:], zb16[:], xn[:, 0:512],
                         start=True, stop=True,
                         inc=(smm if hc == NDC - 1 else None))
                for el in range(4):
                    e = g4 * 4 + el
                    r.wait("g", smm, wa_free)
                    r.op("g", "dma_start", out=wA[:], in_=evS[e, :, 8192:12288],
                         inc=swl, inck=16)
                    r.wait("te", swl)
                    for hc in range(NDC):
                        for ic in range(4):
                            r.op("te", "matmul", dn[hc][:],
                                 wA[:, ic * 1024 + hc * 128: ic * 1024 + (hc + 1) * 128],
                                 hA[:, el * 4096 + tx + ic * 512:
                                    el * 4096 + tx + (ic + 1) * 512],
                                 start=False, stop=(ic == 3),
                                 inc=(smm if (hc == NDC - 1 and ic == 3) else None))
                    wa_free = r.n(smm)
                r.wait("v", smm)
                r.wait("v", sfl)
                for hc in range(NDC):
                    r.op("v", "tensor_copy", fstg[:, hc * 512:(hc + 1) * 512],
                         dn[hc][:], inc=sv)
                r.wait("g", sv)
                for hc in range(NDC):
                    kw = {} if g4 == 0 else {"accum_op": ALU.add}
                    r.op("g", "dma_start",
                         out=routS[hc, :, tbase:tbase + TT],
                         in_=fstg[:, hc * 512:(hc + 1) * 512],
                         inc=sfl, inck=16, **kw)

    for e in Rec.ENGS:
        r.wait(e, sfl)
        r.wait(e, sv)
    for tt in range(NTT):
        t0 = tt * TT
        for dci in range(NDC):
            r.wait("sy", sv)
            r.op("sy", "dma_start", out=wk1[:], in_=xin[dci, :, t0:t0 + TT],
                 inc=sld, inck=16)
            r.op("sy", "dma_start", out=wk2[:], in_=shS[dci, :, t0:t0 + TT],
                 inc=sld, inck=16)
            r.op("sy", "dma_start", out=wk3[:], in_=routS[dci, :, t0:t0 + TT],
                 inc=sld, inck=16)
            r.wait("v", sld)
            r.wait("v", sxs)
            r.op("v", "tensor_tensor", wk4[:], wk1[:], wk2[:], ALU.add)
            r.op("v", "tensor_tensor", wk4[:], wk4[:], wk3[:], ALU.add, inc=sv)
            r.wait("sy", sv)
            r.op("sy", "dma_start", out=xout[dci, :, t0:t0 + TT], in_=wk4[:],
                 inc=sxs, inck=16)
    for e in Rec.ENGS:
        r.wait(e, sxs)


def alloc_bufs(nc, stack):
    bufs = {}

    def sb(name, shape, dt):
        bufs[name] = stack.enter_context(nc.sbuf_tensor("sb_" + name, shape, dt))[:]

    sb("xn", [128, 8 * T], BF16)
    sb("wA", [128, 8 * 512], BF16)
    sb("wA2", [128, 8 * 1024], BF16)
    sb("wB", [128, 3 * 1024], BF16)
    sb("hA", [128, 32 * 512], BF16)
    sb("fC", [128, S], F32)
    sb("hg_nw", [128, 8], F32)
    sb("atn", [128, 8192], BF16)
    sb("idb", [128, 128], BF16)
    sb("cols", [128, 8], F32)
    sb("obc", [128, 8], F32)
    sb("bq3", [128, 24], F32)
    sb("lep", [128, 32], F32)
    sb("gbc", [128, 48], F32)
    sb("mb1", [128, 32], F32)
    sb("trb", [128, 128], F32)
    sb("onb", [128, 128], BF16)
    sb("pst", [128, 128], BF16)
    sb("zb16", [128, 128], BF16)
    sb("idf", [128, 128], F32)
    sb("gmx", [128, 32], F32)
    sb("gbb", [128, 2], F32)
    sb("cbuf", [128, 1024], F32)
    sb("lnxf", [128, 8 * TT], F32)
    sb("row", [1, 4 * TT], F32)
    sb("ones", [128, 128], F32)
    sb("cvec", [128, 8], F32)
    for i in range(6):
        sb(f"wk{i}", [128, TT], F32)
    bufs["fA"] = bufs["lnxf"][:, 0:S]
    bufs["fB"] = bufs["lnxf"][:, S:2 * S]

    def ps(name, shape, dt):
        bufs[name] = stack.enter_context(nc.psum_tensor("ps_" + name, shape, dt))[:]

    ps("psA", [128, 512], F32)
    ps("psB", [128, 512], F32)
    ps("psC", [128, 512], F32)
    ps("psD", [128, 512], F32)
    ps("psE", [128, 512], F32)
    ps("psF", [128, 512], F32)
    ps("psG", [128, 512], F32)
    ps("psH", [128, 512], F32)
    bufs["psDb"] = bufs["psD"].bitcast(BF16)
    return bufs


def emit_const_init(r, bufs, consts_dram):
    r.op("sy", "dma_start", out=bufs["ones"], in_=consts_dram["ones"],
         inc="c0ld", inck=16)
    r.op("sy", "dma_start", out=bufs["cvec"], in_=consts_dram["cvec"],
         inc="c0ld", inck=16)
    for e in Rec.ENGS:
        r.wait(e, "c0ld")


SUB_BUILDERS = {}


DEBUG_TAPS = []
DEBUG_SCRATCH = []


def build_program(subnames, shapes_inputs):
    nc = bass.Bass("TRN2", target_bir_lowering=False)
    dram = {}
    for name, (shape, dt) in shapes_inputs.items():
        mdt = F32 if dt == np.float32 else BF16
        dram[name] = nc.dram_tensor(name, list(shape), mdt,
                                    kind="ExternalInput").ap()
    xo = nc.dram_tensor("xo", [NDC, 128, T], F32, kind="ExternalOutput").ap()
    def _kind(nm):
        return "ExternalOutput" if nm in DEBUG_SCRATCH else "Internal"
    scratch = {
        "hS": nc.dram_tensor("hS", [32, 128, T], BF16, kind=_kind("hS")).ap(),
        "hSf": nc.dram_tensor("hSf", [NDC, 128, T], F32, kind=_kind("hSf")).ap(),
        "sgS": nc.dram_tensor("sgS", [NDC, 128, T], F32, kind=_kind("sgS")).ap(),
        "qS": nc.dram_tensor("qS", [NDC, 128, T], BF16, kind=_kind("qS")).ap(),
        "kS": nc.dram_tensor("kS", [NDC, 128, T], BF16, kind=_kind("kS")).ap(),
        "vS_tm": nc.dram_tensor("vS_tm", [NDC, T, 128], BF16, kind=_kind("vS_tm")).ap(),
        "aoS": nc.dram_tensor("aoS", [NDC, 128, T], BF16, kind=_kind("aoS")).ap(),
        "vfS": nc.dram_tensor("vfS", [NDC, 128, T], BF16, kind=_kind("vfS")).ap(),
        "inpS": nc.dram_tensor("inpS", [NDC, 128, T], F32, kind=_kind("inpS")).ap(),
        "xaS": nc.dram_tensor("xaS", [NDC, 128, T], F32, kind=_kind("xaS")).ap(),
        "evS": nc.dram_tensor("evS", [64, 128, 12288], BF16, kind="Internal").ap(),
        "combS": nc.dram_tensor("combS", [64, T], F32, kind=_kind("combS")).ap(),
        "routS": nc.dram_tensor("routS", [NDC, 128, T], F32, kind=_kind("routS")).ap(),
    }
    for tapname in DEBUG_TAPS:
        scratch["tap_" + tapname] = nc.dram_tensor(
            "tap_" + tapname, [NDC, 128, T], F32, kind="ExternalOutput").ap()
    stack = ExitStack()
    with stack:
        bufs = alloc_bufs(nc, stack)
        r = Rec()
        emit_const_init(r, bufs, dram)
        if any(s.startswith("l1_ffn") for s in subnames):
            emit_expert_cast(r, bufs, {**dram, **scratch})
        x_cur = dram["xt"]
        for i, sub in enumerate(subnames):
            x_next = xo if i == len(subnames) - 1 else \
                nc.dram_tensor(f"xmid{i}", [NDC, 128, T], F32, kind="Internal").ap()
            io = {"xin": x_cur, "xout": x_next, **dram, **scratch}
            SUB_BUILDERS[sub](r, bufs, io, "u_")
            x_cur = x_next
        r.build(nc, stack)
    return nc


SUB_BUILDERS["l0_ffn"] = emit_l0_ffn
SUB_BUILDERS["l0_ffn2"] = emit_l0_ffn
SUB_BUILDERS["l1_attn_b"] = emit_l1_attn_b
SUB_BUILDERS["l0_attn_b"] = emit_l0_attn_b
SUB_BUILDERS["l1_attn_a"] = emit_l1_attn_a
SUB_BUILDERS["l0_attn_a"] = emit_l0_attn_a
SUB_BUILDERS["l1_ffn"] = emit_l1_ffn
SUB_BUILDERS["l1_ffn2"] = emit_l1_ffn


def to_fmajor(x2):
    n = x2.shape[0]
    return np.ascontiguousarray(
        x2.reshape(n * S, NDC, 128).transpose(1, 2, 0))


def from_fmajor(xt, n):
    return np.ascontiguousarray(xt.transpose(2, 0, 1).reshape(n, S, D))


CONST_COLS = {"eps5": 0, "eps6": 1, "eps_hgrn": 2}


def rope_full_tables():
    half = 512
    pos = np.arange(S, dtype=np.float32)[:, None]
    freqs = (1.0 / 10000.0 ** (np.arange(half, dtype=np.float32) / half))
    ang = pos * freqs[None, :]
    cos = np.cos(ang).T.astype(np.float32)
    sin = np.sin(ang).T.astype(np.float32)
    return (np.ascontiguousarray(cos.reshape(4, 128, S)),
            np.ascontiguousarray(sin.reshape(4, 128, S)))


def host_consts():
    cv = np.zeros((128, 8), np.float32)
    cv[:, 0] = 1e-5
    cv[:, 1] = 1e-6
    cv[:, 2] = 1e-5
    qp = np.arange(128, dtype=np.float32)[:, None]
    kf = np.arange(S, dtype=np.float32)[None, :]
    d0 = kf - qp
    j = np.arange(512)[None, :]
    m4 = np.concatenate(
        [np.where(j <= qt * 128 + qp, 0.0, -1e9).astype(np.float32)
         for qt in range(4)], axis=1)
    rc, rs = rope_full_tables()
    div = np.exp(np.arange(0, 64, 2, dtype=np.float32) * (-np.log(10000.0) / 64))
    ang1 = np.arange(S, dtype=np.float32)[:, None] * div[None, :]
    c1 = np.cos(ang1).T
    s1 = np.sin(ang1).T
    rc1 = np.ascontiguousarray(np.tile(c1, (4, 1)).astype(np.float32))
    rs1 = np.ascontiguousarray(np.tile(s1, (4, 1)).astype(np.float32))
    return {"ones": np.ones((128, 128), np.float32), "cvec": cv,
            "d0c": d0.astype(np.float32), "m4c": m4,
            "ropeC": rc, "ropeS": rs,
            "ropeC1": rc1, "ropeS1": rs1,
            "identb": np.eye(128).astype(BF),
            "triuC": np.triu(np.ones((128, 128), np.float32)).T.copy() * 0 +
                     (np.arange(128)[:, None] <= np.arange(128)[None, :]
                      ).astype(np.float32),
            "onesbC": np.ones((128, 128)).astype(BF),
            "identf": np.eye(128, dtype=np.float32)}


def prep_weights(inputs):
    g = {k: np.asarray(v) for k, v in inputs.items()}
    w = {}
    w["wfcT"] = np.ascontiguousarray(g["L0_ffn_W_fc"].T)
    w["wprojT"] = np.ascontiguousarray(g["L0_ffn_W_proj"].T)
    w["hgWiT"] = np.ascontiguousarray(g["L1_attn_b_Wi"].T)
    w["hgWfT"] = np.ascontiguousarray(g["L1_attn_b_Wf"].T)
    w["hgWgT"] = np.ascontiguousarray(g["L1_attn_b_Wg"].T)
    w["hgWoT"] = np.ascontiguousarray(g["L1_attn_b_Wo"].T)
    w["hg_nw_d"] = np.ascontiguousarray(g["L1_attn_b_norm_w"].reshape(8, 128).T)
    w["abWq"] = np.ascontiguousarray(g["L0_attn_b_wq"].T)
    w["abWk"] = np.ascontiguousarray(g["L0_attn_b_wk"].T)
    w["abWv"] = np.ascontiguousarray(g["L0_attn_b_wv"].T)
    w["abWo"] = np.ascontiguousarray(g["L0_attn_b_out_w"].T)
    w["ab_ob"] = np.ascontiguousarray(g["L0_attn_b_out_b"].reshape(8, 128).T)
    E = np.concatenate([h * 64 + np.arange(0, 64, 2) for h in range(16)])
    O = np.concatenate([h * 64 + np.arange(1, 64, 2) for h in range(16)])
    perm1024 = np.concatenate([E, O])
    wqkv = g["L1_attn_a_w_qkv"]
    bqkv = g["L1_attn_a_b_qkv"]
    wq1 = wqkv[0:1024][perm1024]
    wk1_ = wqkv[1024:2048][perm1024]
    wv1 = wqkv[2048:3072]
    w["laWqkv"] = np.ascontiguousarray(
        np.concatenate([wq1, wk1_, wv1], 0).T)
    bq = bqkv[0:1024][perm1024].reshape(8, 128).T
    bk = bqkv[1024:2048][perm1024].reshape(8, 128).T
    bv = bqkv[2048:3072].reshape(8, 128).T
    w["la_bias"] = np.ascontiguousarray(np.concatenate([bq, bk, bv], 1))
    wl = g["L1_attn_a_w_lepe"][:, 0, :]
    lcols = [np.ascontiguousarray(wl[:, i].reshape(8, 128).T) for i in range(3)]
    lcols.append(np.ascontiguousarray(g["L1_attn_a_b_lepe"].reshape(8, 128).T))
    w["la_lepe"] = np.ascontiguousarray(np.concatenate(lcols, 1))
    w["aawq"] = np.ascontiguousarray(g["L0_attn_a_wq"].reshape(1024, 1024))
    w["aawk"] = np.ascontiguousarray(g["L0_attn_a_wk"].reshape(1024, 1024))
    w["aawv"] = np.ascontiguousarray(g["L0_attn_a_wv"].reshape(1024, 1024))
    w["aawo"] = np.ascontiguousarray(g["L0_attn_a_wo"].reshape(1024, 1024))
    w["aaw1"] = np.ascontiguousarray(g["L0_attn_a_w1"])
    w["aaw2"] = np.ascontiguousarray(g["L0_attn_a_w2"])
    cols48 = np.zeros((128, 48), np.float32)
    cols48[:, 0:8] = g["L0_attn_a_ln1_scale"].reshape(8, 128).T
    cols48[:, 8:16] = g["L0_attn_a_ln1_bias"].reshape(8, 128).T
    cols48[:, 16:24] = g["L0_attn_a_ln2_scale"].reshape(8, 128).T
    cols48[:, 24:32] = g["L0_attn_a_ln2_bias"].reshape(8, 128).T
    cols48[:, 32:40] = g["L0_attn_a_b2"].reshape(8, 128).T
    w["aagb"] = np.ascontiguousarray(cols48)
    w["aab1"] = np.ascontiguousarray(g["L0_attn_a_b1"].reshape(32, 128).T)
    w["moeEG"] = np.ascontiguousarray(g["L1_ffn_eg_w"].transpose(0, 2, 1))
    w["moeEU"] = np.ascontiguousarray(g["L1_ffn_eu_w"].transpose(0, 2, 1))
    w["moeED"] = np.ascontiguousarray(g["L1_ffn_ed_w"].transpose(0, 2, 1))
    w["moeGT"] = np.ascontiguousarray(g["L1_ffn_gate_w"].T)
    gb = np.zeros((128, 2), np.float32)
    gb[0:64, 0] = g["L1_ffn_gate_bias"]
    w["moeGB"] = gb
    w["moeSG"] = np.ascontiguousarray(g["L1_ffn_sg_w"].T)
    w["moeSU"] = np.ascontiguousarray(g["L1_ffn_su_w"].T)
    w["moeSD"] = np.ascontiguousarray(g["L1_ffn_sd_w"].T)
    return w


def kernel_chain(x, inputs, subs):
    x = np.asarray(x)
    consts = host_consts()
    w = prep_weights(inputs)
    per_core_const = {**consts, **w}
    shapes = {"xt": ((NDC, 128, T), np.float32)}
    for k, v in per_core_const.items():
        shapes[k] = (v.shape, v.dtype)
    nc = build_program(subs, shapes)
    in_maps = []
    for c in range(NCORES):
        xt = to_fmajor(x[2 * c:2 * c + 2])
        in_maps.append({"xt": xt, **per_core_const})
    res = run_bass_kernel_spmd(nc, in_maps, core_ids=list(range(NCORES)))
    out = np.empty_like(x)
    for c in range(NCORES):
        out[2 * c:2 * c + 2] = from_fmajor(res.results[c]["xo"], 2)
    return out


ALL_SUBS = ["l0_attn_a", "l0_ffn", "l0_attn_b", "l0_ffn2",
            "l1_attn_a", "l1_ffn", "l1_attn_b", "l1_ffn2"]


def kernel(**inputs):
    x = np.asarray(inputs["L0_attn_a_inputs"])
    return kernel_chain(x, inputs, ALL_SUBS)
